# version 30
# baseline (speedup 1.0000x reference)
import numpy as np
import ml_dtypes
import concourse.bass as bass
import concourse.mybir as mybir
from concourse.bass_utils import run_bass_kernel_spmd

F32 = mybir.dt.float32
BF16 = mybir.dt.bfloat16
AF = mybir.ActivationFunctionType
ALU = mybir.AluOpType

NCORES = 8
F = 1024
KC = 8
UT = 32
NT = 512 // UT
TA = 16 * UT
RF = 256
EPS = 1e-5

_prog_cache = {}
CAPTURE_OPS = False
_captured_ops = None


def build_nc(debug=False):
    nc = bass.Bass("TRN2", target_bir_lowering=False)

    d_xattn = nc.dram_tensor("x_attn", [8192, F], F32, kind="ExternalInput").ap()
    d_xffn = nc.dram_tensor("x_ffn", [2, 2048, F], F32, kind="ExternalInput").ap()
    d_wqkv = nc.dram_tensor("wqkv", [F, 2048], F32, kind="ExternalInput").ap()
    d_wout = nc.dram_tensor("wout", [F, F], F32, kind="ExternalInput").ap()
    d_wvt = nc.dram_tensor("wvt", [F, F], F32, kind="ExternalInput").ap()
    d_wgt = nc.dram_tensor("wgt", [F, 8], F32, kind="ExternalInput").ap()
    d_wpt = nc.dram_tensor("wpt", [F, 64], F32, kind="ExternalInput").ap()
    d_w1t = nc.dram_tensor("w1t", [F, 256], F32, kind="ExternalInput").ap()
    d_w2t = nc.dram_tensor("w2t", [256, F], F32, kind="ExternalInput").ap()
    d_b2 = nc.dram_tensor("b2", [8, F], F32, kind="ExternalInput").ap()
    d_smalls = nc.dram_tensor("smalls", [128, 16], F32, kind="ExternalInput").ap()
    d_grbc = nc.dram_tensor("grbc", [128, 40], F32, kind="ExternalInput").ap()
    d_alpha = nc.dram_tensor("alpha", [128, 1], F32, kind="ExternalInput").ap()
    d_eyeb = nc.dram_tensor("eyeb", [128, 128], BF16, kind="ExternalInput").ap()
    d_eyef = nc.dram_tensor("eyef", [128, 128], F32, kind="ExternalInput").ap()
    d_blk8 = nc.dram_tensor("blk8", [128, 128], BF16, kind="ExternalInput").ap()
    d_ones8 = nc.dram_tensor("ones8", [128, 8], BF16, kind="ExternalInput").ap()
    d_onesc = nc.dram_tensor("onesc", [8, 128], BF16, kind="ExternalInput").ap()
    d_sel32 = nc.dram_tensor("sel32", [8, 256], BF16, kind="ExternalInput").ap()
    d_selp = nc.dram_tensor("selp", [64, 8], BF16, kind="ExternalInput").ap()
    d_reps = nc.dram_tensor("reps", [128, 1024], BF16, kind="ExternalInput").ap()
    d_out = nc.dram_tensor("out", [2, 2048, F], F32, kind="ExternalOutput").ap()

    dbg = {}
    if debug:
        for name, shape, dt in [
            ("dbg_xnfm", [128, KC * TA], BF16), ("dbg_q", [128, TA], BF16),
            ("dbg_k", [128, TA], BF16), ("dbg_av", [128, TA], BF16),
            ("dbg_w", [128, TA], BF16), ("dbg_p", [128, TA], BF16),
            ("dbg_zw", [128, TA], BF16), ("dbg_fv", [128, KC * RF], BF16),
            ("dbg_we", [128, 16], F32), ("dbg_hs", [128, 2 * RF], BF16),
            ("dbg_xnffn", [128, KC * RF], BF16), ("dbg_rtm", [128, 16], F32),
            ("dbg_ctm", [128, 16], F32), ("dbg_sgt", [128, 16], F32),
            ("dbg_vtm", [128, 512], BF16), ("dbg_wzr", [128, TA], BF16),
        ]:
            dbg[name] = nc.dram_tensor(name, shape, dt, kind="ExternalOutput").ap()

    from contextlib import ExitStack
    with ExitStack() as stk:
        e = stk.enter_context
        wqkv = e(nc.sbuf_tensor("wqkv_sb", [128, KC * 2048], BF16))
        wout = e(nc.sbuf_tensor("wout_sb", [128, KC * F], BF16))
        wvt = e(nc.sbuf_tensor("wvt_sb", [128, KC * F], BF16))
        wgt = e(nc.sbuf_tensor("wgt_sb", [128, KC * 8], BF16))
        wpt = e(nc.sbuf_tensor("wpt_sb", [128, KC * 64], BF16))
        w1t = e(nc.sbuf_tensor("w1t_sb", [128, KC * 256], BF16))
        w2t = e(nc.sbuf_tensor("w2t_sb", [128, 2 * F], BF16))
        b2sb = e(nc.sbuf_tensor("b2_sb", [8, F], BF16))
        smalls = e(nc.sbuf_tensor("smalls_sb", [128, 16], F32))
        hbias = e(nc.sbuf_tensor("hbias_sb", [128, 2], F32))
        grbc = e(nc.sbuf_tensor("grbc_sb", [128, 40], F32))
        alpsb = e(nc.sbuf_tensor("alpha_sb", [128, 1], F32))
        sigpp = e(nc.sbuf_tensor("sigpp_sb", [128, 1], F32))
        eyeb = e(nc.sbuf_tensor("eyeb_sb", [128, 128], BF16))
        eyef = e(nc.sbuf_tensor("eyef_sb", [128, 128], F32))
        blk8 = e(nc.sbuf_tensor("blk8_sb", [128, 128], BF16))
        ones8 = e(nc.sbuf_tensor("ones8_sb", [128, 8], BF16))
        onesc = e(nc.sbuf_tensor("onesc_sb", [8, 128], BF16))
        sel32 = e(nc.sbuf_tensor("sel32_sb", [8, 256], BF16))
        selp = e(nc.sbuf_tensor("selp_sb", [64, 8], BF16))
        reps = e(nc.sbuf_tensor("reps_sb", [128, 1024], BF16))

        NXA = 5
        xat = e(nc.sbuf_tensor("xat", [128, NXA * F], F32))
        xft = e(nc.sbuf_tensor("xft", [128, 4 * F], F32))
        xnat = e(nc.sbuf_tensor("xnat", [128, 2 * F], BF16))
        deadf = e(nc.sbuf_tensor("deadf", [128, F // 2], F32))
        deadb2 = deadf.ap().bitcast(BF16)[:, 0:F]
        bns = e(nc.sbuf_tensor("bns", [128, 8], F32))
        mv = e(nc.sbuf_tensor("mv", [128, 8], F32))
        lnv = e(nc.sbuf_tensor("lnv", [128, 6], F32))
        rstd = e(nc.sbuf_tensor("rstd", [128, 12], F32))
        nmr = e(nc.sbuf_tensor("nmr", [128, 12], F32))
        xnfm = e(nc.sbuf_tensor("xnfm", [128, KC * TA], BF16))
        xnffm = e(nc.sbuf_tensor("xnffm", [128, KC * RF], BF16))
        qcmp = [[[e(nc.sbuf_tensor(f"qcmp{p}{g}{h}", [128, TA], BF16))
                  for h in (0, 1)] for g in (0, 1)] for p in (0, 1)]
        kcmp = [[[e(nc.sbuf_tensor(f"kcmp{p}{g}{h}", [128, TA], BF16))
                  for h in (0, 1)] for g in (0, 1)] for p in (0, 1)]
        avcmp = [[[e(nc.sbuf_tensor(f"avcmp{p}{g}{h}", [128, TA], BF16))
                   for h in (0, 1)] for g in (0, 1)] for p in (0, 1)]
        wcmp = [[[e(nc.sbuf_tensor(f"wcmp{p}{g}{h}", [128, TA], BF16))
                  for h in (0, 1)] for g in (0, 1)] for p in (0, 1)]
        fvc = [e(nc.sbuf_tensor(f"fvc{p}", [128, KC * RF], BF16)) for p in (0, 1)]
        rsb = [e(nc.sbuf_tensor(f"rsb{p}", [8, RF], F32)) for p in (0, 1)]
        pv2 = [e(nc.sbuf_tensor(f"pv2{p}", [64, RF], BF16)) for p in (0, 1)]
        c2sb = [e(nc.sbuf_tensor(f"c2sb{p}", [8, RF], F32)) for p in (0, 1)]
        pcmp = [e(nc.sbuf_tensor(f"pcmp{h}", [128, TA], BF16)) for h in (0, 1)]
        bdk = [e(nc.sbuf_tensor(f"bdk{h}", [128, 1024], BF16)) for h in (0, 1)]
        bdv = [e(nc.sbuf_tensor(f"bdv{h}", [128, 1024], BF16)) for h in (0, 1)]
        vtm = [e(nc.sbuf_tensor(f"vtm{h}", [128, 512], BF16)) for h in (0, 1)]
        zrs = [e(nc.sbuf_tensor(f"zrs{h}", [8, 512], BF16)) for h in (0, 1)]
        wzr = [e(nc.sbuf_tensor(f"wzr{h}", [128, TA], BF16)) for h in (0, 1)]
        zw = [[e(nc.sbuf_tensor(f"zw{g}{h}", [128, TA], BF16)) for h in (0, 1)] for g in (0, 1)]
        rtm = e(nc.sbuf_tensor("rtm", [128, 16], F32))
        ctm = e(nc.sbuf_tensor("ctm", [128, 16], F32))
        sgt = e(nc.sbuf_tensor("sgt", [128, 16], F32))
        t1 = e(nc.sbuf_tensor("t1", [128, 16], F32))
        t2 = e(nc.sbuf_tensor("t2", [128, 16], F32))
        t3 = e(nc.sbuf_tensor("t3", [128, 16], F32))
        st2c = e(nc.sbuf_tensor("st2c", [128, 2], F32))
        st2b = e(nc.sbuf_tensor("st2b", [128, 2], F32))
        st2d = e(nc.sbuf_tensor("st2d", [128, 2], F32))
        st2cc = e(nc.sbuf_tensor("st2cc", [128, 2], F32))
        st2bc = e(nc.sbuf_tensor("st2bc", [128, 2], F32))
        wetm = e(nc.sbuf_tensor("wetm", [128, 16], F32))
        wef = e(nc.sbuf_tensor("wef", [8, RF], BF16))
        werep = e(nc.sbuf_tensor("werep", [128, 2 * RF], BF16))
        thsb = e(nc.sbuf_tensor("thsb", [128, 2 * RF], F32))
        hssb = e(nc.sbuf_tensor("hssb", [128, 2 * RF], BF16))
        outsb0 = e(nc.sbuf_tensor("outsb", [128, F], F32))
        outsb = [outsb0, outsb0]

        pbig = [e(nc.psum_tensor(f"pbig{i}", [128, 512], F32)) for i in (0, 1)]
        ptr = e(nc.psum_tensor("ptr", [128, 1024], BF16))
        psS = [e(nc.psum_tensor(f"psS{i}", [128, 512], F32)) for i in (0, 1)]
        psZ = [e(nc.psum_tensor(f"psZ{i}", [128, 512], F32)) for i in (0, 1)]
        pfin = e(nc.psum_tensor("pfin", [128, 512], F32))

        sems = {n: e(nc.semaphore(n)) for n in
                ["sT", "sV", "sS", "sY", "sG", "sDw", "sOB0"]
                + [f"sXA{i}" for i in range(NXA)] + [f"sXF{i}" for i in range(4)]
                + (["sDo"] if debug else [])}

        ops = []
        counts = {}
        tinyq = []

        def op(eng, fn, waits=(), inc=None, inc_amt=1):
            incs = []
            if inc is not None:
                counts[inc] = counts.get(inc, 0) + inc_amt
                incs.append((inc, inc_amt))
            if CAPTURE_OPS:
                import sys as _sys
                f = _sys._getframe(1)
                lines = []
                while f is not None and len(lines) < 3:
                    if f.f_code.co_filename.endswith("kernel.py"):
                        lines.append(f.f_lineno)
                    f = f.f_back
                fn.__op_lines__ = lines
            ops.append((eng, fn, [w for w in waits if w is not None and w[1] > 0], incs))

        def at(sem):
            return counts.get(sem, 0)

        def vtiny(fn, waits=(), mark=False):
            incs = []
            if mark:
                counts["sY"] = counts.get("sY", 0) + 1
                incs.append(("sY", 1))
            tinyq.append((fn, [w for w in waits if w is not None and w[1] > 0], incs))

        def vop(fn, waits=(), inc=None, inc_amt=1):
            op("vector", fn, waits, inc, inc_amt)
            if tinyq:
                fn2, w2, i2 = tinyq.pop(0)
                ops.append(("vector", fn2, w2, i2))

        def flush_tiny():
            while tinyq:
                fn2, w2, i2 = tinyq.pop(0)
                ops.append(("vector", (lambda e_: e_.drain()), [], []))
                ops.append(("vector", fn2, w2, i2))
            ops.append(("vector", (lambda e_: e_.drain()), [], []))

        def wdma(dst, src):
            op("gpsimd", lambda g_, d=dst, s=src: g_.dma_start(d, s), inc="sDw", inc_amt=16)

        for k in range(KC):
            wdma(wqkv[:, 2048 * k:2048 * (k + 1)], d_wqkv[128 * k:128 * (k + 1), :])
            wdma(wout[:, F * k:F * (k + 1)], d_wout[128 * k:128 * (k + 1), :])
            wdma(wvt[:, F * k:F * (k + 1)], d_wvt[128 * k:128 * (k + 1), :])
            wdma(wgt[:, 8 * k:8 * (k + 1)], d_wgt[128 * k:128 * (k + 1), :])
            wdma(wpt[:, 64 * k:64 * (k + 1)], d_wpt[128 * k:128 * (k + 1), :])
            wdma(w1t[:, 256 * k:256 * (k + 1)], d_w1t[128 * k:128 * (k + 1), :])
        for k in range(2):
            wdma(w2t[:, F * k:F * (k + 1)], d_w2t[128 * k:128 * (k + 1), :])
        for dst, src in [(b2sb, d_b2), (smalls, d_smalls), (grbc, d_grbc),
                         (alpsb, d_alpha), (eyeb, d_eyeb), (eyef, d_eyef),
                         (blk8, d_blk8), (ones8, d_ones8), (onesc, d_onesc),
                         (sel32, d_sel32), (selp, d_selp), (reps, d_reps)]:
            wdma(dst[:, :], src)
        w_init = at("sDw")

        op("scalar", lambda s_: s_.activation(sigpp[:, :], alpsb[:, :], AF.Tanh, scale=0.5),
           waits=[("sDw", w_init)], inc="sS")
        op("vector", lambda v_: v_.tensor_scalar(sigpp[:, :], sigpp[:, :], 0.5, 0.5,
                                                 op0=ALU.mult, op1=ALU.add),
           waits=[("sS", at("sS"))], inc="sV")
        op("vector", lambda v_: v_.tensor_scalar(hbias[:, :], smalls[:, 8:10], 0.5, None,
                                                 op0=ALU.mult),
           waits=[("sDw", w_init)], inc="sV")
        v_init = at("sV")

        xav = d_xattn.rearrange("(a u) f -> a u f", u=512)
        xfv = d_xffn.rearrange("g (a s) f -> g s a f", s=128)
        outv = d_out.rearrange("g (a s) f -> g s a f", s=128)
        xa_ready = [None] * NXA
        xf_ready = [None] * 4
        xa_free = [[] for _ in range(NXA)]
        xf_free = [[] for _ in range(4)]

        def dma_xa(tau, j):
            slot = (4 * tau + j) % NXA
            src = xav[:, UT * tau + 8 * j:UT * tau + 8 * j + 8, :]
            src = src.rearrange("a u f -> u a f")
            op("sync", lambda s_, d=xat[:, F * slot:F * (slot + 1)], s2=src: s_.dma_start(d, s2),
               waits=list(xa_free[slot]), inc=f"sXA{slot}", inc_amt=16)
            xa_ready[slot] = (f"sXA{slot}", at(f"sXA{slot}"))

        def dma_xf(tau, g):
            slot = (2 * tau + g) % 4
            src = xfv[g, 8 * tau:8 * tau + 8, :, :]
            op("sync", lambda s_, d=xft[:, F * slot:F * (slot + 1)], s2=src: s_.dma_start(d, s2),
               waits=list(xf_free[slot]), inc=f"sXF{slot}", inc_amt=16)
            xf_ready[slot] = (f"sXF{slot}", at(f"sXF{slot}"))

        router_state = {}
        bg_big = [[("sDw", w_init)], [("sDw", w_init)]]
        bg_S = [[], []]
        bd_guard = [[[], [], [], []], [[], [], [], []]]

        def bdslot(hh, s):
            buf = bdk[hh] if s < 2 else bdv[hh]
            off = 512 * (s % 2)
            return buf, off
        bg_Z = [[], []]
        pfin_guard = [("sDw", w_init)]
        bg_ptr = []
        sT_zrep_last = [0, 0]
        outsb_guard = [[]]
        prev_fin = [[]]
        par_free = [[], []]
        front_state = {}

        attn_stats_ready = [None]

        def attn_stats(tau):
            spar = tau % 2
            a_slots = [(4 * tau + j) % NXA for j in range(4)]
            for j, slot in enumerate(a_slots):
                w_ = [xa_ready[slot], ("sV", v_init)] + prev_fin[0] + list(outsb_guard[0])
                op("scalar", lambda s_, s2=xat[:, F * slot:F * slot + F], c=j: s_.activation(
                    outsb[0][:, :], s2, AF.Square, accum_out=mv[:, 2 + c:3 + c]),
                    waits=w_, inc="sS")
                op("vector", lambda v_, s2=xat[:, F * slot:F * slot + F], c=j: v_.tensor_scalar(
                    deadb2, s2, 1.0, 0.0, op0=ALU.mult, op1=ALU.add,
                    accum_out=bns[:, 2 + c:3 + c]),
                    waits=w_, inc="sV")
            op("scalar", lambda s_: s_.drain(), inc="sS")
            sS_st = at("sS")
            op("vector", lambda v_: v_.drain(), waits=[("sS", sS_st)])
            vtiny(lambda v_: v_.tensor_scalar(bns[:, 2:6], bns[:, 2:6], 1.0 / F, None,
                                              op0=ALU.mult))
            vtiny(lambda v_: v_.tensor_scalar(mv[:, 2:6], mv[:, 2:6], 1.0 / F, None,
                                              op0=ALU.mult))
            vtiny(lambda v_: v_.tensor_tensor(lnv[:, 2:6], bns[:, 2:6], bns[:, 2:6],
                                              op=ALU.mult))
            vtiny(lambda v_: v_.tensor_tensor(lnv[:, 2:6], mv[:, 2:6], lnv[:, 2:6],
                                              op=ALU.subtract))
            vtiny(lambda v_: v_.tensor_scalar(lnv[:, 2:6], lnv[:, 2:6], EPS, None,
                                              op0=ALU.add), mark=True)
            flush_tiny()
            op("scalar", lambda s_: s_.activation(lnv[:, 2:6], lnv[:, 2:6], AF.Ln),
               waits=[("sY", at("sY"))])
            op("scalar", lambda s_, o=rstd[:, 6 * spar:6 * spar + 4]: s_.activation(
                o, lnv[:, 2:6], AF.Exp, scale=-0.5), inc="sS")
            sS_r2 = at("sS")
            vtiny(lambda v_, o=nmr[:, 6 * spar:6 * spar + 4],
                  rs=rstd[:, 6 * spar:6 * spar + 4]: v_.scalar_tensor_tensor(
                o, bns[:, 2:6], -1.0, rs, op0=ALU.mult, op1=ALU.mult),
                waits=[("sS", sS_r2)], mark=True)
            flush_tiny()
            attn_stats_ready[0] = ("sY", at("sY"))
            prev_fin[0] = [("sY", at("sY"))]

        def front(tau):
            par = tau % 2
            a_slots = [(4 * tau + j) % NXA for j in range(4)]
            f_slots = [(2 * tau + g) % 4 for g in (0, 1)]
            for g in (0, 1):
                dma_xf(tau, g)
            ftiles = [("f", s, 4 + i) for i, s in enumerate(f_slots)]
            atiles = [("a", s, i) for i, s in enumerate(a_slots)]
            sT_tr_prev = [at("sT"), at("sT")]

            def apply_tiles(tiles, stats_w):
                for kind, slot, col in tiles:
                    buf = xat if kind == "a" else xft
                    src = buf[:, F * slot:F * (slot + 1)]
                    rp = col % 2
                    dstx = xnat[:, F * rp:F * (rp + 1)]
                    scol = 6 * par + col
                    w_ = list(stats_w) + [("sT", sT_tr_prev[rp])]
                    op("scalar", lambda s_, s2=src, d=dstx, c=scol: s_.activation(
                        d, s2, AF.Identity, bias=nmr[:, c:c + 1], scale=rstd[:, c:c + 1]),
                        waits=w_, inc="sS")
                    sS_ap = at("sS")
                    for k in range(KC):
                        w_ = ([("sS", sS_ap)] + list(bg_ptr)) if k == 0 else []
                        op("tensor", lambda t_, i_=dstx[:, 128 * k:128 * (k + 1)],
                           o_=ptr[:, 128 * k:128 * (k + 1)]: t_.transpose(o_, i_, eyeb[:, :]),
                           waits=w_, inc="sT" if k == KC - 1 else None)
                    sT_tr = at("sT")
                    sT_tr_prev[rp] = sT_tr
                    if kind == "a":
                        dfm = xnfm.ap().rearrange("p (k t) -> p k t", t=TA)[:, :, 128 * col:128 * (col + 1)]
                    else:
                        dfm = xnffm.ap().rearrange("p (k t) -> p k t", t=RF)[:, :, 128 * (col - 4):128 * (col - 3)]
                    sfm = ptr.ap().rearrange("p (k c) -> p k c", c=128)
                    if col % 2 == 0:
                        vop(lambda v_, s2=sfm, d=dfm: v_.tensor_copy(d, s2),
                            waits=[("sT", sT_tr)], inc="sV")
                        bg_ptr[:] = [("sV", at("sV"))]
                    else:
                        op("scalar", lambda s_, s2=sfm, d=dfm: s_.copy(d, s2),
                           waits=[("sT", sT_tr)], inc="sS")
                        bg_ptr[:] = [("sS", at("sS"))]
                return [("sV", at("sV")), ("sS", at("sS"))]

            afm_ready = apply_tiles(atiles, [attn_stats_ready[0]])
            for s in a_slots:
                xa_free[s] = list(afm_ready)
            if tau + 1 < NT:
                for j in range(4):
                    dma_xa(tau + 1, j)
            for kind, slot, col in ftiles:
                w_ = [xf_ready[slot], ("sV", v_init)] + prev_fin[0] + list(outsb_guard[0])
                op("scalar", lambda s_, s2=xft[:, F * slot:F * slot + F], c=col: s_.activation(
                    outsb[0][:, :], s2, AF.Square, accum_out=mv[:, c - 4:c - 3]),
                    waits=w_, inc="sS")
                op("vector", lambda v_, s2=xft[:, F * slot:F * slot + F], c=col: v_.tensor_scalar(
                    deadb2, s2, 1.0, 0.0, op0=ALU.mult, op1=ALU.add,
                    accum_out=bns[:, c - 4:c - 3]),
                    waits=w_, inc="sV")
            op("scalar", lambda s_: s_.drain(), inc="sS")
            sS_stats = at("sS")
            op("vector", lambda v_: v_.drain(), waits=[("sS", sS_stats)])
            vtiny(lambda v_: v_.tensor_scalar(bns[:, 0:2], bns[:, 0:2], 1.0 / F, None,
                                              op0=ALU.mult))
            vtiny(lambda v_: v_.tensor_scalar(mv[:, 0:2], mv[:, 0:2], 1.0 / F, None,
                                              op0=ALU.mult))
            vtiny(lambda v_: v_.tensor_tensor(lnv[:, 0:2], bns[:, 0:2], bns[:, 0:2],
                                              op=ALU.mult))
            vtiny(lambda v_: v_.tensor_tensor(lnv[:, 0:2], mv[:, 0:2], lnv[:, 0:2],
                                              op=ALU.subtract))
            vtiny(lambda v_: v_.tensor_scalar(lnv[:, 0:2], lnv[:, 0:2], EPS, None,
                                              op0=ALU.add), mark=True)
            flush_tiny()
            sY_l = at("sY")
            op("scalar", lambda s_: s_.activation(lnv[:, 0:2], lnv[:, 0:2], AF.Ln),
               waits=[("sY", sY_l)])
            op("scalar", lambda s_, o=rstd[:, 6 * par + 4:6 * par + 6]: s_.activation(
                o, lnv[:, 0:2], AF.Exp, scale=-0.5), inc="sS")
            sS_r = at("sS")
            vtiny(lambda v_, o=nmr[:, 6 * par + 4:6 * par + 6],
                  rs=rstd[:, 6 * par + 4:6 * par + 6]: v_.scalar_tensor_tensor(
                o, bns[:, 0:2], -1.0, rs, op0=ALU.mult, op1=ALU.mult),
                waits=[("sS", sS_r)], mark=True)
            flush_tiny()
            prev_fin[0] = [("sY", at("sY"))]
            if tau >= 1 and (tau - 1) not in router_state and (tau - 1) in front_state:
                router_front(tau - 1)

            cmp_ready = {}
            fi = 0
            for g in (0, 1):
                for j4 in range(4):
                    for hh in (0, 1):
                        bank = fi % 2
                        pb = pbig[bank]
                        for k in range(KC):
                            lhs = wqkv[:, 2048 * k + 1024 * g + 256 * j4 + 128 * hh:
                                       2048 * k + 1024 * g + 256 * j4 + 128 * (hh + 1)]
                            mov = xnfm[:, TA * k:TA * (k + 1)]
                            w_ = (list(afm_ready) + list(bg_big[bank]) + par_free[par]) \
                                if k == 0 else []
                            op("tensor", lambda t_, l=lhs, m=mov, k_=k, o=pb[:, :]: t_.matmul(
                                o, l, m, start=(k_ == 0), stop=(k_ == KC - 1)),
                                waits=w_, inc="sT" if k == KC - 1 else None)
                        sT_fill = at("sT")
                        arr = [qcmp, kcmp, avcmp, wcmp][j4][par][g][hh]
                        src_ap = pb[:, 0:512]
                        dst_ap = arr[:, :]
                        if fi % 2 == 0:
                            op("scalar", lambda s_, s2=src_ap, d=dst_ap: s_.copy(d, s2),
                               waits=[("sT", sT_fill)], inc="sS")
                            mark = ("sS", at("sS"))
                        else:
                            vop(lambda v_, s2=src_ap, d=dst_ap: v_.tensor_copy(d, s2),
                                waits=[("sT", sT_fill)], inc="sV")
                            mark = ("sV", at("sV"))
                        cmp_ready[(j4, g, hh)] = [mark]
                        bg_big[bank] = [mark]
                        fi += 1

            ffm_ready = apply_tiles(ftiles, [("sY", at("sY"))])
            for s in f_slots:
                xf_free[s] = list(ffm_ready)
            fm_ready = afm_ready + ffm_ready

            for dc in range(KC):
                bank = dc % 2
                pb = pbig[bank]
                for k in range(KC):
                    lhs = wvt[:, F * k + 128 * dc:F * k + 128 * (dc + 1)]
                    mov = xnffm[:, RF * k:RF * (k + 1)]
                    w_ = (list(ffm_ready) + list(bg_big[bank]) + par_free[par]) if k == 0 else []
                    op("tensor", lambda t_, l=lhs, m=mov, k_=k, o=pb[:, 0:RF]: t_.matmul(
                        o, l, m, start=(k_ == 0), stop=(k_ == KC - 1)),
                        waits=w_, inc="sT" if k == KC - 1 else None)
                sT_fill = at("sT")
                op("scalar", lambda s_, s2=pb[:, 0:RF], dc_=dc: s_.activation(
                    fvc[par][:, RF * dc_:RF * (dc_ + 1)], s2, AF.Identity,
                    bias=smalls[:, dc_:dc_ + 1]),
                    waits=[("sT", sT_fill)], inc="sS")
                bg_big[bank] = [("sS", at("sS"))]
            fv_ready = [("sS", at("sS"))]

            pb = pbig[0]
            for k in range(KC):
                w_ = (fv_ready + list(bg_big[0]) + par_free[par]) if k == 0 else []
                op("tensor", lambda t_, l=wgt[:, 8 * k:8 * (k + 1)],
                   m=fvc[par][:, RF * k:RF * (k + 1)], k_=k, o=pb[0:8, 0:RF]: t_.matmul(
                    o, l, m, start=(k_ == 0), stop=(k_ == KC - 1)),
                    waits=w_, inc="sT" if k == KC - 1 else None)
            vop(lambda v_, o=pb[0:8, 0:RF]: v_.tensor_copy(rsb[par][:, :], o),
                waits=[("sT", at("sT"))], inc="sV")
            bg_big[0] = [("sV", at("sV"))]
            pb = pbig[1]
            for k in range(KC):
                w_ = (fv_ready + list(bg_big[1]) + par_free[par]) if k == 0 else []
                op("tensor", lambda t_, l=wpt[:, 64 * k:64 * (k + 1)],
                   m=fvc[par][:, RF * k:RF * (k + 1)], k_=k, o=pb[0:64, 0:RF]: t_.matmul(
                    o, l, m, start=(k_ == 0), stop=(k_ == KC - 1)),
                    waits=w_, inc="sT" if k == KC - 1 else None)
            op("scalar", lambda s_, o=pb[0:64, 0:RF]: s_.activation(pv2[par][:, :], o, AF.Square),
               waits=[("sT", at("sT"))], inc="sS")
            sS_pv2 = at("sS")
            bg_big[1] = [("sS", sS_pv2)]
            pb = pbig[0]
            op("tensor", lambda t_, o=pb[0:8, 256:256 + RF]: t_.matmul(
                o, selp[:, :], pv2[par][:, :], start=True, stop=True),
               waits=[("sS", sS_pv2)] + list(bg_big[0]), inc="sT")
            vop(lambda v_, o=pb[0:8, 256:256 + RF]: v_.tensor_copy(c2sb[par][:, :], o),
                waits=[("sT", at("sT"))], inc="sV")
            bg_big[0] = [("sV", at("sV"))]

            front_state[tau] = dict(cmp_ready=cmp_ready, fv_ready=fv_ready,
                                    f_slots=f_slots, fm_ready=fm_ready)
            if tau + 1 < NT:
                attn_stats(tau + 1)

        def outsb_guard_xf(tau, slot):
            return []

        def router_front(rt):
            rpar = rt % 2
            trg = list(bg_S[0])
            for idx, srcb in enumerate((rsb[rpar], c2sb[rpar])):
                for w2 in range(2):
                    w_ = trg if (idx == 0 and w2 == 0) else []
                    op("tensor", lambda t_, i_=srcb[0:8, 128 * w2:128 * (w2 + 1)],
                       o_=psS[0][:, 8 * (2 * idx + w2):8 * (2 * idx + w2 + 1)]:
                       t_.transpose(o_, i_, eyef[0:8, 0:8]),
                       waits=w_, inc="sT" if (idx == 1 and w2 == 1) else None)
            sT_rt = at("sT")

            def re8f(ap_):
                return ap_.rearrange("p (w e) -> p w e", e=8)

            op("scalar", lambda s_: s_.copy(
                re8f(rtm.ap()), psS[0].ap().rearrange("p (w e) -> p w e", e=8)[:, 0:2, :]),
                waits=[("sT", sT_rt)], inc="sS")
            op("scalar", lambda s_: s_.copy(
                re8f(ctm.ap()), psS[0].ap().rearrange("p (w e) -> p w e", e=8)[:, 2:4, :]),
                inc="sS")
            bg_S[0] = [("sS", at("sS"))]
            sS_rc = at("sS")
            vtiny(lambda v_: v_.tensor_tensor(
                re8f(rtm.ap()), re8f(rtm.ap()),
                grbc[:, 32:40].unsqueeze(1).broadcast_to((128, 2, 8)), op=ALU.add),
                waits=[("sS", sS_rc)])
            vtiny(lambda v_: v_.tensor_scalar(ctm[:, :], ctm[:, :], 1e-30, None,
                                              op0=ALU.add), mark=True)
            sY_c = at("sY")
            sY_we_expected = sY_c + 3 + 1
            router_state[rt] = {"sY_c": sY_c, "sY_we_expected": sY_we_expected}

        def back(tau):
            par = tau % 2
            st = front_state.pop(tau)
            cmp_ready = st["cmp_ready"]
            fv_ready = st["fv_ready"]
            f_slots = st["f_slots"]

            rs = router_state.pop(tau)
            sY_we = rs["sY_we_expected"]

            def re8(ap_):
                return ap_.rearrange("p (w e) -> p w e", e=8)

            op("scalar", lambda s_: s_.activation(ctm[:, :], ctm[:, :], AF.Ln),
               waits=[("sY", rs["sY_c"])])
            op("scalar", lambda s_: s_.activation(ctm[:, :], ctm[:, :], AF.Exp, scale=0.5),
               inc="sS")
            sS_c = at("sS")

            def ln8_queue(srcb, mbuf, vbuf, waits=()):
                vtiny(lambda v_: v_.tensor_reduce(mbuf[:, :], re8(srcb.ap()),
                                                  axis=mybir.AxisListType.X, op=ALU.add),
                      waits=waits)
                vtiny(lambda v_: v_.tensor_scalar(mbuf[:, :], mbuf[:, :], 1.0 / 8, None,
                                                  op0=ALU.mult))
                vtiny(lambda v_: v_.tensor_tensor(t1[:, :], srcb[:, :], srcb[:, :],
                                                  op=ALU.mult))
                vtiny(lambda v_: v_.tensor_reduce(vbuf[:, :], re8(t1.ap()),
                                                  axis=mybir.AxisListType.X, op=ALU.add))
                vtiny(lambda v_: v_.tensor_scalar(vbuf[:, :], vbuf[:, :], 1.0 / 8, None,
                                                  op0=ALU.mult))
                vtiny(lambda v_: v_.tensor_tensor(st2d[:, :], mbuf[:, :], mbuf[:, :],
                                                  op=ALU.mult))
                vtiny(lambda v_: v_.tensor_tensor(vbuf[:, :], vbuf[:, :], st2d[:, :],
                                                  op=ALU.subtract))
                vtiny(lambda v_: v_.tensor_scalar(vbuf[:, :], vbuf[:, :], EPS, None,
                                                  op0=ALU.add), mark=True)

            vtiny(lambda v_: v_.tensor_scalar(ctm[:, :], ctm[:, :],
                                              float(1.0 / np.sqrt(8.0)), None, op0=ALU.mult),
                  waits=[("sS", sS_c)])
            ln8_queue(rtm, st2c, st2b)
            ln8_queue(ctm, st2cc, st2bc)
            sY_v = at("sY")

            zw_ready = [[], []]
            for g in (0, 1):
                pair_rdy = {}

                def emit_pair(kbp):
                    kb = 2 * kbp
                    slot = kb % 4
                    for hh in (0, 1):
                        in0 = kcmp[par][g][hh][:, 64 * kb:64 * (kb + 2)]
                        in0 = in0.rearrange("p (u a) -> p u a", a=16).unsqueeze(2)
                        in0 = in0.broadcast_to((128, 8, 8, 16))
                        in1 = blk8.ap().rearrange("p (h a) -> p h a", a=16).unsqueeze(1)
                        in1 = in1.broadcast_to((128, 8, 8, 16))
                        sbuf_, soff = bdslot(hh, slot)
                        dstb = sbuf_[:, 0:1024]
                        dstb = dstb.rearrange("p (u h a) -> p u h a", h=8, a=16)
                        w_ = (cmp_ready[(1, g, hh)] + bd_guard[hh][slot]
                              + bd_guard[hh][slot + 1])
                        if kbp < 2:
                            op("gpsimd", lambda g_, a_=dstb, b_=in0, c_=in1:
                               g_.tensor_tensor(a_, b_, c_, op=ALU.mult),
                               waits=w_, inc="sG")
                            pair_rdy[(kbp, hh)] = ("sG", at("sG"))
                        else:
                            vop(lambda v_, a_=dstb, b_=in0, c_=in1: v_.tensor_tensor(
                                a_, b_, c_, op=ALU.mult), waits=w_, inc="sV")
                            pair_rdy[(kbp, hh)] = ("sV", at("sV"))

                def emit_S(kb):
                    slot = kb % 4
                    for hh in (0, 1):
                        sbuf_, soff = bdslot(hh, slot)
                        for uu in range(4):
                            u = 4 * kb + uu
                            lhs = sbuf_[:, soff + 128 * uu:soff + 128 * (uu + 1)]
                            mov = qcmp[par][g][hh][:, 16 * u:16 * (u + 1)]
                            o = psS[hh][:, 16 * u:16 * (u + 1)]
                            w_ = []
                            if uu == 0:
                                w_ = cmp_ready[(0, g, hh)]
                                if kb % 2 == 0:
                                    w_ = w_ + [pair_rdy[(kb // 2, hh)]]
                                if kb == 0:
                                    w_ += bg_S[hh]
                            op("tensor", lambda t_, l=lhs, m=mov, o_=o: t_.matmul(
                                o_, l, m, start=True, stop=True), waits=w_,
                                inc="sT" if uu == 3 else None)
                        bd_guard[hh][slot] = [("sT", at("sT"))]

                emit_pair(0)
                emit_pair(1)
                emit_S(0)
                emit_S(1)
                emit_pair(2)
                emit_S(2)
                emit_S(3)
                emit_pair(3)
                for kb in range(4, 8):
                    emit_S(kb)
                sT_S = at("sT")
                for hh in (0, 1):
                    op("scalar", lambda s_, hh_=hh: s_.activation(
                        pcmp[hh_][:, :], psS[hh_][:, :], AF.Exp, scale=0.25),
                        waits=[("sT", sT_S)], inc="sS")
                sS_P = at("sS")
                for hh in (0, 1):
                    op("tensor", lambda t_, hh_=hh: t_.matmul(
                        psS[hh_][0:8, :], ones8[:, :], pcmp[hh_][:, :],
                        start=True, stop=True),
                        waits=[("sS", sS_P)] if hh == 0 else [], inc="sT")
                vtm_rdy = [0, 0]
                sT_v_m = [0, 0]
                for hh in (0, 1):
                    pzb = psZ[hh].ap().bitcast(BF16)
                    for ub in range(4):
                        w_ = (cmp_ready[(2, g, hh)] + list(bg_Z[hh])) if ub == 0 else []
                        op("tensor", lambda t_, i_=avcmp[par][g][hh][:, 128 * ub:128 * (ub + 1)],
                           o_=pzb[:, 128 * ub:128 * (ub + 1)]: t_.transpose(o_, i_, eyeb[:, :]),
                           waits=w_, inc="sT" if ub == 3 else None)
                    sT_v_m[hh] = at("sT")
                sT_Z = at("sT")
                for hh in (0, 1):
                    op("scalar", lambda s_, hh_=hh: s_.activation(
                        thsb[32 * hh_:32 * hh_ + 8, 0:512], psS[hh_][0:8, :], AF.Ln),
                        waits=[("sT", sT_Z), ("sT", sT_zrep_last[hh])])
                    op("scalar", lambda s_, hh_=hh: s_.activation(
                        zrs[hh_][:, :], thsb[32 * hh_:32 * hh_ + 8, 0:512], AF.Exp,
                        scale=-1.0), inc="sS")
                    sS_zr = at("sS")
                    op("tensor", lambda t_, hh_=hh: t_.matmul(
                        psS[hh_][:, :], onesc[:, :], zrs[hh_][:, :], start=True, stop=True),
                        waits=[("sS", sS_zr)], inc="sT")
                    sT_zrep_last[hh] = at("sT")
                    vop(lambda v_, hh_=hh, wc=wcmp[par][g][hh][:, :]: v_.scalar_tensor_tensor(
                        wzr[hh_][:, :], psS[hh_][:, :], 1.0, wc,
                        op0=ALU.mult, op1=ALU.mult),
                        waits=[("sT", sT_zrep_last[hh])] + cmp_ready[(3, g, hh)], inc="sV")
                    bg_S[hh] = [("sV", at("sV"))]
                sV_wzr = at("sV")
                for hh in (0, 1):
                    pzb = psZ[hh].ap().bitcast(BF16)
                    op("scalar", lambda s_, hh_=hh, s2=pzb[:, 0:512]: s_.copy(
                        vtm[hh_][:, :], s2), waits=[("sT", sT_v_m[hh])], inc="sS")
                    vtm_rdy[hh] = ("sS", at("sS"))
                    bg_Z[hh] = [("sS", at("sS"))]

                zg = [list(bg_Z[0]), list(bg_Z[1])]
                bank_guards = [list(pfin_guard), [("sV", sV_wzr)], [("sV", sV_wzr)]]
                bdv_mark = {}

                def emit_rep_bdv(hh, qb):
                    bidx = qb % 3
                    slot = qb % 4
                    zbuf_, boff = bdslot(hh, slot)
                    bank = [pfin, psS[0], psS[1]][bidx]
                    for uu in range(4):
                        u = 4 * qb + uu
                        ub, r8 = u // 8, u % 8
                        w_ = []
                        if uu == 0:
                            w_ = list(bank_guards[bidx])
                            if qb <= 1:
                                w_ += [vtm_rdy[hh]]
                        op("tensor", lambda t_, l=reps[:, 128 * r8:128 * (r8 + 1)],
                           m=vtm[hh][:, 128 * ub:128 * (ub + 1)],
                           o=bank[:, 128 * uu:128 * (uu + 1)]: t_.matmul(
                            o, l, m, start=True, stop=True),
                            waits=w_, inc="sT" if uu == 3 else None)
                    sT_rep = at("sT")
                    in1 = blk8.ap().unsqueeze(1).broadcast_to((128, 4, 128))
                    dstb = zbuf_[:, boff:boff + 512].rearrange("p (q c) -> p q c", c=128)
                    vop(lambda v_, d=dstb, m=in1, b=bank: v_.scalar_tensor_tensor(
                        d, b.ap().rearrange("p (q c) -> p q c", c=128), 1.0, m,
                        op0=ALU.mult, op1=ALU.mult),
                        waits=[("sT", sT_rep)] + bd_guard[hh][slot],
                        inc="sV")
                    sV_bdv = at("sV")
                    bdv_mark[(hh, qb)] = sV_bdv
                    bank_guards[bidx] = [("sV", sV_bdv)]
                    if bidx == 0:
                        pfin_guard[:] = [("sV", sV_bdv)]

                def emit_z(hh, qb):
                    slot = qb % 4
                    zbuf_, boff = bdslot(hh, slot)
                    for uu in range(4):
                        u = 4 * qb + uu
                        w_ = []
                        if uu == 0:
                            w_ = [("sV", bdv_mark[(hh, qb)])]
                            if qb == 0:
                                w_ += zg[hh]
                        op("tensor", lambda t_,
                           l=zbuf_[:, boff + 128 * uu:boff + 128 * (uu + 1)],
                           m=pcmp[hh][:, 16 * u:16 * (u + 1)],
                           o=psZ[hh][:, 128 * uu + 16 * qb:128 * uu + 16 * qb + 16]:
                           t_.matmul(o, l, m, start=True, stop=True),
                            waits=w_, inc="sT" if uu == 3 else None)
                    bd_guard[hh][slot] = [("sT", at("sT"))]

                for hh in (0, 1):
                    emit_rep_bdv(hh, 0)
                    emit_rep_bdv(hh, 1)
                    for qb in range(2, 8):
                        emit_rep_bdv(hh, qb)
                        emit_z(hh, qb - 2)
                    emit_z(hh, 6)
                    emit_z(hh, 7)
                    sT_z = at("sT")
                    for r4 in range(4):
                        srcz = psZ[hh][:, 128 * r4:128 * (r4 + 1)].rearrange(
                            "p (u4 a) -> p u4 a", a=16)
                        w1_ = wzr[hh].ap().rearrange(
                            "p (u4 r a) -> p u4 r a", r=4, a=16)[:, :, r4, :]
                        dstz = zw[g][hh][:, 128 * r4:128 * (r4 + 1)]
                        dstz = dstz.rearrange("p (u4 a) -> p u4 a", a=16)
                        w_ = [("sT", sT_z), ("sV", sV_wzr)] if r4 == 0 else []
                        vop(lambda v_, s2=srcz, d=dstz, m=w1_: v_.scalar_tensor_tensor(
                            d, s2, 1.0, m, op0=ALU.mult, op1=ALU.mult),
                            waits=w_, inc="sV" if r4 == 3 else None)
                    bg_Z[hh] = [("sV", at("sV"))]
                    if hh == 1:
                        bg_S[0] = list(bank_guards[1])
                        bg_S[1] = list(bank_guards[2])
                    zw_ready[g] = zw_ready[g] + [("sV", at("sV"))]

                if g == 0:
                    op("scalar", lambda s_: s_.activation(st2b[:, :], st2b[:, :], AF.Ln),
                       waits=[("sY", sY_v)])
                    op("scalar", lambda s_: s_.activation(st2bc[:, :], st2bc[:, :], AF.Ln))
                    op("scalar", lambda s_: s_.drain())
                    op("scalar", lambda s_: s_.activation(st2b[:, :], st2b[:, :], AF.Exp,
                                                          scale=-0.5))
                    op("scalar", lambda s_: s_.activation(st2bc[:, :], st2bc[:, :], AF.Exp,
                                                          scale=-0.5), inc="sS")
                    sS_rstd = at("sS")

                    def ln8_apply(dst_, srcb, mbuf, vbuf, gcol, bcol, waits=()):
                        vtiny(lambda v_: v_.tensor_tensor(
                            re8(dst_.ap()), re8(srcb.ap()),
                            mbuf[:, :].unsqueeze(2).broadcast_to((128, 2, 8)),
                            op=ALU.subtract), waits=waits)
                        vtiny(lambda v_: v_.tensor_tensor(
                            re8(dst_.ap()), re8(dst_.ap()),
                            vbuf[:, :].unsqueeze(2).broadcast_to((128, 2, 8)), op=ALU.mult))
                        vtiny(lambda v_: v_.tensor_tensor(
                            re8(dst_.ap()), re8(dst_.ap()),
                            grbc[:, gcol:gcol + 8].unsqueeze(1).broadcast_to((128, 2, 8)),
                            op=ALU.mult))
                        vtiny(lambda v_: v_.tensor_tensor(
                            re8(dst_.ap()), re8(dst_.ap()),
                            grbc[:, bcol:bcol + 8].unsqueeze(1).broadcast_to((128, 2, 8)),
                            op=ALU.add))

                    ln8_apply(sgt, rtm, st2c, st2b, 0, 8, waits=[("sS", sS_rstd)])
                    ln8_apply(t2, ctm, st2cc, st2bc, 16, 24)
                    vtiny(lambda v_: v_.scalar_tensor_tensor(
                        sgt[:, :], t2[:, :], sigpp[:, 0:1], sgt[:, :],
                        op0=ALU.mult, op1=ALU.add))
                    vtiny(lambda v_: v_.tensor_reduce(st2c[:, :], re8(sgt.ap()),
                                                      axis=mybir.AxisListType.X, op=ALU.max))
                    vtiny(lambda v_: v_.tensor_tensor(
                        re8(t1.ap()), re8(sgt.ap()),
                        st2c[:, :].unsqueeze(2).broadcast_to((128, 2, 8)), op=ALU.is_ge))
                    vtiny(lambda v_: v_.scalar_tensor_tensor(
                        t2[:, :], t1[:, :], -1e30, sgt[:, :], op0=ALU.mult, op1=ALU.add))
                    vtiny(lambda v_: v_.tensor_reduce(st2b[:, :], re8(t2.ap()),
                                                      axis=mybir.AxisListType.X, op=ALU.max))
                    vtiny(lambda v_: v_.tensor_tensor(
                        re8(t3.ap()), re8(t2.ap()),
                        st2b[:, :].unsqueeze(2).broadcast_to((128, 2, 8)), op=ALU.is_ge))
                    vtiny(lambda v_: v_.tensor_tensor(st2d[:, :], st2c[:, :], st2b[:, :],
                                                      op=ALU.subtract), mark=True)
                    sY_d = at("sY")
                else:
                    op("scalar", lambda s_: s_.activation(st2d[:, :], st2d[:, :], AF.Tanh,
                                                          scale=0.5),
                       waits=[("sY", sY_d)], inc="sS")
                    sS_w1 = at("sS")
                    vtiny(lambda v_: v_.tensor_scalar(st2d[:, :], st2d[:, :], 0.5, 0.5,
                                                      op0=ALU.mult, op1=ALU.add),
                          waits=[("sS", sS_w1)])
                    vtiny(lambda v_: v_.tensor_tensor(
                        re8(wetm.ap()), re8(t1.ap()),
                        st2d[:, :].unsqueeze(2).broadcast_to((128, 2, 8)), op=ALU.mult))
                    vtiny(lambda v_: v_.tensor_scalar(st2d[:, :], st2d[:, :], -1.0, 1.0,
                                                      op0=ALU.mult, op1=ALU.add))
                    vtiny(lambda v_: v_.tensor_tensor(
                        re8(t3.ap()), re8(t3.ap()),
                        st2d[:, :].unsqueeze(2).broadcast_to((128, 2, 8)), op=ALU.mult))
                    vtiny(lambda v_: v_.tensor_tensor(wetm[:, :], wetm[:, :], t3[:, :],
                                                      op=ALU.add), mark=True)
                    sY_we = at("sY")

                if debug and tau == 0 and g == 0:
                    def dump(name, src_ap, waits):
                        op("sync", lambda s_, d=dbg[name], s2=src_ap: s_.dma_start(d, s2),
                           waits=waits, inc="sDo", inc_amt=16)
                    dump("dbg_q", qcmp[0][0][0][:, :], cmp_ready[(0, 0, 0)])
                    dump("dbg_k", kcmp[0][0][0][:, :], cmp_ready[(1, 0, 0)])
                    dump("dbg_av", avcmp[0][0][0][:, :], cmp_ready[(2, 0, 0)])
                    dump("dbg_w", wcmp[0][0][0][:, :], cmp_ready[(3, 0, 0)])
                    dump("dbg_p", pcmp[0][:, :], [("sS", sS_P)])
                    dump("dbg_zw", zw[0][0][:, :], [("sV", at("sV"))])
                    dump("dbg_wzr", wzr[0][:, :], [("sV", sV_wzr)])
                    dump("dbg_vtm", vtm[0][:, :], [("sV", at("sV"))])

            flush_tiny()
            if debug and tau == 0:
                op("sync", lambda s_: s_.dma_start(dbg["dbg_we"], wetm[:, :]),
                   waits=[("sY", sY_we)], inc="sDo", inc_amt=16)
                op("sync", lambda s_: s_.dma_start(dbg["dbg_rtm"], rtm[:, :]),
                   waits=[("sY", sY_we)], inc="sDo", inc_amt=16)
                op("sync", lambda s_: s_.dma_start(dbg["dbg_ctm"], ctm[:, :]),
                   waits=[("sY", sY_we)], inc="sDo", inc_amt=16)
                op("sync", lambda s_: s_.dma_start(dbg["dbg_sgt"], sgt[:, :]),
                   waits=[("sY", sY_we)], inc="sDo", inc_amt=16)
            pr = psS[0]
            for w2 in range(2):
                w_ = ([("sY", sY_we)] + list(bg_S[0])) if w2 == 0 else []
                op("tensor", lambda t_, i_=wetm[:, 8 * w2:8 * (w2 + 1)],
                   o_=pr[0:8, 128 * w2:128 * (w2 + 1)]: t_.transpose(o_, i_, eyef[:, :]),
                   waits=w_, inc="sT" if w2 == 1 else None)
            sT_wef = at("sT")
            op("vector", lambda v_: v_.tensor_copy(wef[:, 0:128], pr[0:8, 0:128]),
               waits=[("sT", sT_wef)], inc="sV")
            op("vector", lambda v_: v_.tensor_copy(wef[:, 128:256], pr[0:8, 128:256]),
               inc="sV")
            sV_wef = at("sV")
            bg_S[0] = [("sV", sV_wef)]
            for ch in (0, 1):
                pb2 = psS[ch]
                w_ = [("sV", sV_wef)] + (list(bg_S[1]) if ch == 1 else list(bg_S[0]))
                op("tensor", lambda t_, l=sel32[:, 128 * ch:128 * (ch + 1)], b=pb2:
                   t_.matmul(b[:, 0:RF], l, wef[:, :], start=True, stop=True),
                   waits=w_, inc="sT")
                op("scalar", lambda s_, b=pb2, ch_=ch: s_.copy(
                    werep[:, RF * ch_:RF * (ch_ + 1)], b[:, 0:RF]),
                    waits=[("sT", at("sT"))], inc="sS")
                bg_S[ch] = [("sS", at("sS"))]
            sS_wrep = at("sS")
            for ch in (0, 1):
                pb2 = psS[ch]
                for k in range(KC):
                    lhs = w1t[:, 256 * k + 128 * ch:256 * k + 128 * (ch + 1)]
                    mov = fvc[par][:, RF * k:RF * (k + 1)]
                    w_ = (fv_ready + list(bg_S[ch])) if k == 0 else []
                    op("tensor", lambda t_, l=lhs, m=mov, b=pb2, k_=k: t_.matmul(
                        b[:, 0:RF], l, m, start=(k_ == 0), stop=(k_ == KC - 1)),
                        waits=w_, inc="sT" if k == KC - 1 else None)
                sT_h = at("sT")
                op("scalar", lambda s_, b=pb2, ch_=ch: s_.activation(
                    thsb[:, RF * ch_:RF * (ch_ + 1)], b[:, 0:RF], AF.Tanh, scale=0.5,
                    bias=hbias[:, ch_:ch_ + 1]), waits=[("sT", sT_h)], inc="sS")
                sS_th = at("sS")
                vtiny(lambda v_, ch_=ch: v_.tensor_scalar(
                    thsb[:, RF * ch_:RF * (ch_ + 1)], thsb[:, RF * ch_:RF * (ch_ + 1)],
                    0.5, 0.5, op0=ALU.mult, op1=ALU.add), waits=[("sS", sS_th)])
                vtiny(lambda v_, b=pb2, ch_=ch: v_.scalar_tensor_tensor(
                    thsb[:, RF * ch_:RF * (ch_ + 1)], b[:, 0:RF], smalls[:, 8 + ch_:9 + ch_],
                    thsb[:, RF * ch_:RF * (ch_ + 1)], op0=ALU.add, op1=ALU.mult))
                vtiny(lambda v_, ch_=ch: v_.tensor_tensor(
                    hssb[:, RF * ch_:RF * (ch_ + 1)], thsb[:, RF * ch_:RF * (ch_ + 1)],
                    werep[:, RF * ch_:RF * (ch_ + 1)], op=ALU.mult),
                    waits=[("sS", sS_wrep)], mark=True)
                flush_tiny()
                bg_S[ch] = [("sY", at("sY"))]
            sY_hs = at("sY")
            if debug and tau == 0:
                op("sync", lambda s_: s_.dma_start(dbg["dbg_hs"], hssb[:, :]),
                   waits=[("sY", sY_hs)], inc="sDo", inc_amt=16)
                op("sync", lambda s_: s_.dma_start(dbg["dbg_fv"], fvc[0][:, :]),
                   waits=fv_ready, inc="sDo", inc_amt=16)
                op("sync", lambda s_: s_.dma_start(dbg["dbg_xnfm"], xnfm[:, :]),
                   waits=st["fm_ready"], inc="sDo", inc_amt=16)
                op("sync", lambda s_: s_.dma_start(dbg["dbg_xnffn"], xnffm[:, :]),
                   waits=st["fm_ready"], inc="sDo", inc_amt=16)

            for g in (0, 1):
                for w2 in (0, 1):
                    first = True
                    w0_ = list(pfin_guard) + [("sY", sY_hs), ("sDw", w_init)] + zw_ready[g]
                    for r4 in range(4):
                        for hh in (0, 1):
                            lhs = zw[g][hh][:, 128 * r4:128 * (r4 + 1)]
                            mov = wout[:, F * (2 * r4 + hh) + 512 * w2:
                                       F * (2 * r4 + hh) + 512 * (w2 + 1)]
                            w_ = w0_ if first else []
                            op("tensor", lambda t_, l=lhs, m=mov, st_=first: t_.matmul(
                                pfin[:, :], l, m, start=st_, stop=False), waits=w_)
                            first = False
                    for ch in (0, 1):
                        lhs = hssb[:, RF * ch + 128 * g:RF * ch + 128 * (g + 1)]
                        mov = w2t[:, F * ch + 512 * w2:F * ch + 512 * (w2 + 1)]
                        op("tensor", lambda t_, l=lhs, m=mov: t_.matmul(
                            pfin[:, :], l, m, start=False, stop=False))
                    op("tensor", lambda t_, g_=g, w2_=w2: t_.matmul(
                        pfin[:, :], wef[:, 128 * g_:128 * (g_ + 1)],
                        b2sb[:, 512 * w2_:512 * (w2_ + 1)], start=False, stop=False))
                    for q in range(4):
                        dc = 4 * w2 + q
                        lhs = fvc[par][:, RF * dc + 128 * g:RF * dc + 128 * (g + 1)]
                        stop = (q == 3)
                        op("tensor", lambda t_, l=lhs, o=pfin[:, 128 * q:128 * (q + 1)],
                           sp=stop: t_.matmul(o, l, eyeb[:, :], start=False, stop=sp),
                           inc="sT" if stop else None)
                    sT_fin = at("sT")
                    slot = f_slots[g]
                    xsrc = xft[:, F * slot + 512 * w2:F * slot + 512 * (w2 + 1)]
                    vop(lambda v_, xs=xsrc, o=outsb[g][:, 512 * w2:512 * (w2 + 1)]:
                        v_.scalar_tensor_tensor(o, pfin[:, :], 1.0, xs,
                                                op0=ALU.mult, op1=ALU.add),
                        waits=[("sT", sT_fin)] + list(outsb_guard[0]), inc="sV")
                    outsb_guard[0] = []
                    pfin_guard[:] = [("sV", at("sV"))]
                out_mark = [("sV", at("sV"))]
                dstr = outv[g, 8 * tau:8 * tau + 8, :, :]
                op("sync", lambda s_, d=dstr, g_=g: s_.dma_start(d, outsb[g_][:, :]),
                   waits=out_mark, inc="sOB0", inc_amt=16)
                outsb_guard[0] = [("sOB0", at("sOB0"))]
                xf_free[f_slots[g]] = xf_free[f_slots[g]] + out_mark
            par_free[par] = [("sT", at("sT")), ("sV", at("sV")), ("sS", at("sS"))]

        for j in range(4):
            dma_xa(0, j)
        attn_stats(0)
        front(0)
        front(1)
        for tau in range(NT):
            if tau not in router_state:
                router_front(tau)
            back(tau)
            if tau + 2 < NT:
                front(tau + 2)

        total_out = at("sDo") if debug else 0

        global _captured_ops
        if CAPTURE_OPS:
            _captured_ops = list(ops)

        block = e(nc.Block())

        def emit_for(eng_name):
            def emitter(eng):
                for eng2, fn, waits, incs in ops:
                    if eng2 != eng_name:
                        continue
                    for sem_n, val in waits:
                        eng.wait_ge(sems[sem_n], val)
                    inst = fn(eng)
                    for sem_n, amt in incs:
                        inst.then_inc(sems[sem_n], amt)
                if eng_name == "sync":
                    if debug and total_out > 0:
                        eng.wait_ge(sems["sDo"], total_out)
                    eng.wait_ge(sems["sOB0"], counts.get("sOB0", 0))
            return emitter

        block.sync(emit_for("sync"))
        block.gpsimd(emit_for("gpsimd"))
        block.tensor(emit_for("tensor"))
        block.vector(emit_for("vector"))
        block.scalar(emit_for("scalar"))

    return nc



def _mk_consts():
    eyeb = np.eye(128, dtype=ml_dtypes.bfloat16)
    eyef = np.eye(128, dtype=np.float32)
    blk8 = np.zeros((128, 128), dtype=ml_dtypes.bfloat16)
    for p in range(128):
        blk8[p, (p // 16) * 16:(p // 16) * 16 + 16] = 1
    ones8 = np.zeros((128, 8), dtype=ml_dtypes.bfloat16)
    for p in range(128):
        ones8[p, p // 16] = 1
    onesc = np.zeros((8, 128), dtype=ml_dtypes.bfloat16)
    for hh in range(8):
        onesc[hh, 16 * hh:16 * (hh + 1)] = 1
    sel32 = np.zeros((8, 256), dtype=ml_dtypes.bfloat16)
    for ch in range(2):
        for p in range(128):
            sel32[4 * ch + p // 32, 128 * ch + p] = 1
    selp = np.zeros((64, 8), dtype=ml_dtypes.bfloat16)
    for ep in range(64):
        selp[ep, ep // 8] = 1
    reps = np.zeros((8, 128, 128), dtype=ml_dtypes.bfloat16)
    for r8 in range(8):
        for a2 in range(16):
            for hh in range(8):
                reps[r8, 16 * r8 + a2, 16 * hh + a2] = 1
    reps = reps.transpose(1, 0, 2).reshape(128, 1024)
    return eyeb, eyef, blk8, ones8, onesc, sel32, selp, reps


def _host_prep(inp):
    x = np.ascontiguousarray(np.asarray(inp["L0_attn_x"], dtype=np.float32))
    Wqkv = np.asarray(inp["L0_attn_w_qkvw"], dtype=np.float32)
    Wout = np.ascontiguousarray(np.asarray(inp["L0_attn_w_out"], dtype=np.float32))
    Wv = np.asarray(inp["L0_ffn_Wv"], dtype=np.float32)
    bv = np.asarray(inp["L0_ffn_bv"], dtype=np.float32)
    Wg = np.asarray(inp["L0_ffn_Wg"], dtype=np.float32)
    bg = np.asarray(inp["L0_ffn_bg"], dtype=np.float32)
    Wp = np.asarray(inp["L0_ffn_Wp"], dtype=np.float32)
    alpha = np.ascontiguousarray(np.broadcast_to(
        np.asarray(inp["L0_ffn_alpha"], dtype=np.float32).reshape(1, 1), (128, 1)))
    W1 = np.asarray(inp["L0_ffn_W1"], dtype=np.float32)
    b1 = np.asarray(inp["L0_ffn_b1"], dtype=np.float32)
    W2 = np.asarray(inp["L0_ffn_W2"], dtype=np.float32)
    b2 = np.ascontiguousarray(np.asarray(inp["L0_ffn_b2"], dtype=np.float32))

    Wq6 = Wqkv.reshape(1024, 4, 4, 2, 8, 16)
    wvt = np.ascontiguousarray(Wv.T)
    wgt = np.ascontiguousarray(Wg.T)
    wpt = np.ascontiguousarray(Wp.reshape(64, 1024).T)
    w1t = np.ascontiguousarray(W1.reshape(256, 1024).T)
    w2t = np.ascontiguousarray(W2.transpose(0, 2, 1).reshape(256, 1024))
    smalls = np.zeros((128, 16), dtype=np.float32)
    smalls[:, 0:8] = bv.reshape(8, 128).T
    smalls[:, 8:10] = b1.reshape(2, 128).T
    grbc = np.zeros((128, 40), dtype=np.float32)
    grbc[:, 0:8] = np.asarray(inp["L0_ffn_g_r"], dtype=np.float32)[None, :]
    grbc[:, 8:16] = np.asarray(inp["L0_ffn_b_r"], dtype=np.float32)[None, :]
    grbc[:, 16:24] = np.asarray(inp["L0_ffn_g_c"], dtype=np.float32)[None, :]
    grbc[:, 24:32] = np.asarray(inp["L0_ffn_b_c"], dtype=np.float32)[None, :]
    grbc[:, 32:40] = bg[None, :]
    eyeb, eyef, blk8, ones8, onesc, sel32, selp, reps = _mk_consts()

    in_maps = []
    for c in range(NCORES):
        b = c // 2
        i0, i1 = 2 * (c % 2), 2 * (c % 2) + 1
        x_attn = np.ascontiguousarray(x[b])
        x_ffn = np.ascontiguousarray(np.stack([x[i0, b * 2048:(b + 1) * 2048],
                                               x[i1, b * 2048:(b + 1) * 2048]]))
        wq = np.ascontiguousarray(
            np.stack([Wq6[:, i0], Wq6[:, i1]], axis=1).reshape(1024, 2048))
        in_maps.append({
            "x_attn": x_attn, "x_ffn": x_ffn, "wqkv": wq, "wout": Wout,
            "wvt": wvt, "wgt": wgt, "wpt": wpt, "w1t": w1t, "w2t": w2t,
            "b2": b2, "smalls": smalls, "grbc": grbc, "alpha": alpha,
            "eyeb": eyeb, "eyef": eyef, "blk8": blk8, "ones8": ones8,
            "onesc": onesc, "sel32": sel32, "selp": selp, "reps": reps,
        })
    return in_maps


def kernel(**inputs):
    in_maps = _host_prep(inputs)
    if "prog" not in _prog_cache:
        _prog_cache["prog"] = build_nc(debug=False)
    nc = _prog_cache["prog"]
    res = run_bass_kernel_spmd(nc, in_maps, core_ids=list(range(NCORES)))
    out = np.empty((4, 8192, 1024), dtype=np.float32)
    for c in range(NCORES):
        b = c // 2
        i0, i1 = 2 * (c % 2), 2 * (c % 2) + 1
        oc = res.results[c]["out"]
        out[i0, b * 2048:(b + 1) * 2048] = oc[0]
        out[i1, b * 2048:(b + 1) * 2048] = oc[1]
    return out


# version 31
# speedup vs baseline: 1.0232x; 1.0232x over previous
import numpy as np
import ml_dtypes
import concourse.bass as bass
import concourse.mybir as mybir
from concourse.bass_utils import run_bass_kernel_spmd

F32 = mybir.dt.float32
BF16 = mybir.dt.bfloat16
AF = mybir.ActivationFunctionType
ALU = mybir.AluOpType

NCORES = 8
F = 1024
KC = 8
UT = 32
NT = 512 // UT
TA = 16 * UT
RF = 256
EPS = 1e-5

_prog_cache = {}
CAPTURE_OPS = False
_captured_ops = None


def build_nc(debug=False):
    nc = bass.Bass("TRN2", target_bir_lowering=False)

    d_xattn = nc.dram_tensor("x_attn", [8192, F], F32, kind="ExternalInput").ap()
    d_xffn = nc.dram_tensor("x_ffn", [2, 2048, F], F32, kind="ExternalInput").ap()
    d_wqkv = nc.dram_tensor("wqkv", [F, 2048], F32, kind="ExternalInput").ap()
    d_wout = nc.dram_tensor("wout", [F, F], F32, kind="ExternalInput").ap()
    d_wvt = nc.dram_tensor("wvt", [F, F], F32, kind="ExternalInput").ap()
    d_wgt = nc.dram_tensor("wgt", [F, 8], F32, kind="ExternalInput").ap()
    d_wpt = nc.dram_tensor("wpt", [F, 64], F32, kind="ExternalInput").ap()
    d_w1t = nc.dram_tensor("w1t", [F, 256], F32, kind="ExternalInput").ap()
    d_w2t = nc.dram_tensor("w2t", [256, F], F32, kind="ExternalInput").ap()
    d_b2 = nc.dram_tensor("b2", [8, F], F32, kind="ExternalInput").ap()
    d_smalls = nc.dram_tensor("smalls", [128, 16], F32, kind="ExternalInput").ap()
    d_grbc = nc.dram_tensor("grbc", [128, 40], F32, kind="ExternalInput").ap()
    d_alpha = nc.dram_tensor("alpha", [128, 1], F32, kind="ExternalInput").ap()
    d_eyeb = nc.dram_tensor("eyeb", [128, 128], BF16, kind="ExternalInput").ap()
    d_eyef = nc.dram_tensor("eyef", [128, 128], F32, kind="ExternalInput").ap()
    d_blk8 = nc.dram_tensor("blk8", [128, 128], BF16, kind="ExternalInput").ap()
    d_ones8 = nc.dram_tensor("ones8", [128, 8], BF16, kind="ExternalInput").ap()
    d_onesc = nc.dram_tensor("onesc", [8, 128], BF16, kind="ExternalInput").ap()
    d_sel32 = nc.dram_tensor("sel32", [8, 256], BF16, kind="ExternalInput").ap()
    d_selp = nc.dram_tensor("selp", [64, 8], BF16, kind="ExternalInput").ap()
    d_reps = nc.dram_tensor("reps", [128, 1024], BF16, kind="ExternalInput").ap()
    d_out = nc.dram_tensor("out", [2, 2048, F], F32, kind="ExternalOutput").ap()

    dbg = {}
    if debug:
        for name, shape, dt in [
            ("dbg_xnfm", [128, KC * TA], BF16), ("dbg_q", [128, TA], BF16),
            ("dbg_k", [128, TA], BF16), ("dbg_av", [128, TA], BF16),
            ("dbg_w", [128, TA], BF16), ("dbg_p", [128, TA], BF16),
            ("dbg_zw", [128, TA], BF16), ("dbg_fv", [128, KC * RF], BF16),
            ("dbg_we", [128, 16], F32), ("dbg_hs", [128, 2 * RF], BF16),
            ("dbg_xnffn", [128, KC * RF], BF16), ("dbg_rtm", [128, 16], F32),
            ("dbg_ctm", [128, 16], F32), ("dbg_sgt", [128, 16], F32),
            ("dbg_vtm", [128, 512], BF16), ("dbg_wzr", [128, TA], BF16),
        ]:
            dbg[name] = nc.dram_tensor(name, shape, dt, kind="ExternalOutput").ap()

    from contextlib import ExitStack
    with ExitStack() as stk:
        e = stk.enter_context
        wqkv = e(nc.sbuf_tensor("wqkv_sb", [128, KC * 2048], BF16))
        wout = e(nc.sbuf_tensor("wout_sb", [128, KC * F], BF16))
        wvt = e(nc.sbuf_tensor("wvt_sb", [128, KC * F], BF16))
        wgt = e(nc.sbuf_tensor("wgt_sb", [128, KC * 8], BF16))
        wpt = e(nc.sbuf_tensor("wpt_sb", [128, KC * 64], BF16))
        w1t = e(nc.sbuf_tensor("w1t_sb", [128, KC * 256], BF16))
        w2t = e(nc.sbuf_tensor("w2t_sb", [128, 2 * F], BF16))
        b2sb = e(nc.sbuf_tensor("b2_sb", [8, F], BF16))
        smalls = e(nc.sbuf_tensor("smalls_sb", [128, 16], F32))
        hbias = e(nc.sbuf_tensor("hbias_sb", [128, 2], F32))
        grbc = e(nc.sbuf_tensor("grbc_sb", [128, 40], F32))
        alpsb = e(nc.sbuf_tensor("alpha_sb", [128, 1], F32))
        sigpp = e(nc.sbuf_tensor("sigpp_sb", [128, 1], F32))
        eyeb = e(nc.sbuf_tensor("eyeb_sb", [128, 128], BF16))
        eyef = e(nc.sbuf_tensor("eyef_sb", [128, 128], F32))
        blk8 = e(nc.sbuf_tensor("blk8_sb", [128, 128], BF16))
        ones8 = e(nc.sbuf_tensor("ones8_sb", [128, 8], BF16))
        onesc = e(nc.sbuf_tensor("onesc_sb", [8, 128], BF16))
        sel32 = e(nc.sbuf_tensor("sel32_sb", [8, 256], BF16))
        selp = e(nc.sbuf_tensor("selp_sb", [64, 8], BF16))
        reps = e(nc.sbuf_tensor("reps_sb", [128, 1024], BF16))

        NXA = 5
        xat = e(nc.sbuf_tensor("xat", [128, NXA * F], F32))
        xft = e(nc.sbuf_tensor("xft", [128, 4 * F], F32))
        xnat = e(nc.sbuf_tensor("xnat", [128, 2 * F], BF16))
        deadf = e(nc.sbuf_tensor("deadf", [128, F // 2], F32))
        deadb2 = deadf.ap().bitcast(BF16)[:, 0:F]
        bns = e(nc.sbuf_tensor("bns", [128, 8], F32))
        mv = e(nc.sbuf_tensor("mv", [128, 8], F32))
        lnv = e(nc.sbuf_tensor("lnv", [128, 6], F32))
        rstd = e(nc.sbuf_tensor("rstd", [128, 12], F32))
        nmr = e(nc.sbuf_tensor("nmr", [128, 12], F32))
        xnfm = e(nc.sbuf_tensor("xnfm", [128, KC * TA], BF16))
        xnffm = e(nc.sbuf_tensor("xnffm", [128, KC * RF], BF16))
        qcmp = [[[e(nc.sbuf_tensor(f"qcmp{p}{g}{h}", [128, TA], BF16))
                  for h in (0, 1)] for g in (0, 1)] for p in (0, 1)]
        kcmp = [[[e(nc.sbuf_tensor(f"kcmp{p}{g}{h}", [128, TA], BF16))
                  for h in (0, 1)] for g in (0, 1)] for p in (0, 1)]
        avcmp = [[[e(nc.sbuf_tensor(f"avcmp{p}{g}{h}", [128, TA], BF16))
                   for h in (0, 1)] for g in (0, 1)] for p in (0, 1)]
        wcmp = [[[e(nc.sbuf_tensor(f"wcmp{p}{g}{h}", [128, TA], BF16))
                  for h in (0, 1)] for g in (0, 1)] for p in (0, 1)]
        fvc = [e(nc.sbuf_tensor(f"fvc{p}", [128, KC * RF], BF16)) for p in (0, 1)]
        rsb = [e(nc.sbuf_tensor(f"rsb{p}", [8, RF], F32)) for p in (0, 1)]
        pv2 = [e(nc.sbuf_tensor(f"pv2{p}", [64, RF], BF16)) for p in (0, 1)]
        c2sb = [e(nc.sbuf_tensor(f"c2sb{p}", [8, RF], F32)) for p in (0, 1)]
        pcmp = [e(nc.sbuf_tensor(f"pcmp{h}", [128, TA], BF16)) for h in (0, 1)]
        bdk = [e(nc.sbuf_tensor(f"bdk{h}", [128, 1024], BF16)) for h in (0, 1)]
        bdv = [e(nc.sbuf_tensor(f"bdv{h}", [128, 1024], BF16)) for h in (0, 1)]
        vtm = [e(nc.sbuf_tensor(f"vtm{h}", [128, 512], BF16)) for h in (0, 1)]
        zrs = [e(nc.sbuf_tensor(f"zrs{h}", [8, 512], BF16)) for h in (0, 1)]
        wzr = [e(nc.sbuf_tensor(f"wzr{h}", [128, TA], BF16)) for h in (0, 1)]
        zw = [[e(nc.sbuf_tensor(f"zw{g}{h}", [128, TA], BF16)) for h in (0, 1)] for g in (0, 1)]
        rtm = e(nc.sbuf_tensor("rtm", [128, 16], F32))
        ctm = e(nc.sbuf_tensor("ctm", [128, 16], F32))
        sgt = e(nc.sbuf_tensor("sgt", [128, 16], F32))
        t1 = e(nc.sbuf_tensor("t1", [128, 16], F32))
        t2 = e(nc.sbuf_tensor("t2", [128, 16], F32))
        t3 = e(nc.sbuf_tensor("t3", [128, 16], F32))
        st2c = e(nc.sbuf_tensor("st2c", [128, 2], F32))
        st2b = e(nc.sbuf_tensor("st2b", [128, 2], F32))
        st2d = e(nc.sbuf_tensor("st2d", [128, 2], F32))
        st2cc = e(nc.sbuf_tensor("st2cc", [128, 2], F32))
        st2bc = e(nc.sbuf_tensor("st2bc", [128, 2], F32))
        wetm = e(nc.sbuf_tensor("wetm", [128, 16], F32))
        wef = e(nc.sbuf_tensor("wef", [8, RF], BF16))
        werep = e(nc.sbuf_tensor("werep", [128, 2 * RF], BF16))
        thsb = e(nc.sbuf_tensor("thsb", [128, 2 * RF], F32))
        hssb = e(nc.sbuf_tensor("hssb", [128, 2 * RF], BF16))
        outsb0 = e(nc.sbuf_tensor("outsb", [128, F], F32))
        outsb = [outsb0, outsb0]

        pbig = [e(nc.psum_tensor(f"pbig{i}", [128, 512], F32)) for i in (0, 1)]
        ptr = e(nc.psum_tensor("ptr", [128, 1024], BF16))
        psS = [e(nc.psum_tensor(f"psS{i}", [128, 512], F32)) for i in (0, 1)]
        psZ = [e(nc.psum_tensor(f"psZ{i}", [128, 512], F32)) for i in (0, 1)]
        pfin = e(nc.psum_tensor("pfin", [128, 512], F32))

        sems = {n: e(nc.semaphore(n)) for n in
                ["sT", "sV", "sS", "sY", "sG", "sDw", "sOB0"]
                + [f"sXA{i}" for i in range(NXA)] + [f"sXF{i}" for i in range(4)]
                + (["sDo"] if debug else [])}

        ops = []
        counts = {}
        tinyq = []

        def op(eng, fn, waits=(), inc=None, inc_amt=1):
            incs = []
            if inc is not None:
                counts[inc] = counts.get(inc, 0) + inc_amt
                incs.append((inc, inc_amt))
            if CAPTURE_OPS:
                import sys as _sys
                f = _sys._getframe(1)
                lines = []
                while f is not None and len(lines) < 3:
                    if f.f_code.co_filename.endswith("kernel.py"):
                        lines.append(f.f_lineno)
                    f = f.f_back
                fn.__op_lines__ = lines
            ops.append((eng, fn, [w for w in waits if w is not None and w[1] > 0], incs))

        def at(sem):
            return counts.get(sem, 0)

        def vtiny(fn, waits=(), mark=False):
            incs = []
            if mark:
                counts["sY"] = counts.get("sY", 0) + 1
                incs.append(("sY", 1))
            tinyq.append((fn, [w for w in waits if w is not None and w[1] > 0], incs))

        def vop(fn, waits=(), inc=None, inc_amt=1):
            op("vector", fn, waits, inc, inc_amt)
            if tinyq:
                fn2, w2, i2 = tinyq.pop(0)
                ops.append(("vector", fn2, w2, i2))

        def flush_tiny():
            while tinyq:
                fn2, w2, i2 = tinyq.pop(0)
                ops.append(("vector", (lambda e_: e_.drain()), [], []))
                ops.append(("vector", fn2, w2, i2))
            ops.append(("vector", (lambda e_: e_.drain()), [], []))

        def wdma(dst, src):
            op("gpsimd", lambda g_, d=dst, s=src: g_.dma_start(d, s), inc="sDw", inc_amt=16)

        for k in range(KC):
            wdma(wqkv[:, 2048 * k:2048 * (k + 1)], d_wqkv[128 * k:128 * (k + 1), :])
            wdma(wout[:, F * k:F * (k + 1)], d_wout[128 * k:128 * (k + 1), :])
            wdma(wvt[:, F * k:F * (k + 1)], d_wvt[128 * k:128 * (k + 1), :])
            wdma(wgt[:, 8 * k:8 * (k + 1)], d_wgt[128 * k:128 * (k + 1), :])
            wdma(wpt[:, 64 * k:64 * (k + 1)], d_wpt[128 * k:128 * (k + 1), :])
            wdma(w1t[:, 256 * k:256 * (k + 1)], d_w1t[128 * k:128 * (k + 1), :])
        for k in range(2):
            wdma(w2t[:, F * k:F * (k + 1)], d_w2t[128 * k:128 * (k + 1), :])
        for dst, src in [(b2sb, d_b2), (smalls, d_smalls), (grbc, d_grbc),
                         (alpsb, d_alpha), (eyeb, d_eyeb), (eyef, d_eyef),
                         (blk8, d_blk8), (ones8, d_ones8), (onesc, d_onesc),
                         (sel32, d_sel32), (selp, d_selp), (reps, d_reps)]:
            wdma(dst[:, :], src)
        w_init = at("sDw")

        op("scalar", lambda s_: s_.activation(sigpp[:, :], alpsb[:, :], AF.Tanh, scale=0.5),
           waits=[("sDw", w_init)], inc="sS")
        op("vector", lambda v_: v_.tensor_scalar(sigpp[:, :], sigpp[:, :], 0.5, 0.5,
                                                 op0=ALU.mult, op1=ALU.add),
           waits=[("sS", at("sS"))], inc="sV")
        op("vector", lambda v_: v_.tensor_scalar(hbias[:, :], smalls[:, 8:10], 0.5, None,
                                                 op0=ALU.mult),
           waits=[("sDw", w_init)], inc="sV")
        v_init = at("sV")

        xav = d_xattn.rearrange("(a u) f -> a u f", u=512)
        xfv = d_xffn.rearrange("g (a s) f -> g s a f", s=128)
        outv = d_out.rearrange("g (a s) f -> g s a f", s=128)
        xa_ready = [None] * NXA
        xf_ready = [None] * 4
        xa_free = [[] for _ in range(NXA)]
        xf_free = [[] for _ in range(4)]

        def dma_xa(tau, j):
            slot = (4 * tau + j) % NXA
            src = xav[:, UT * tau + 8 * j:UT * tau + 8 * j + 8, :]
            src = src.rearrange("a u f -> u a f")
            op("sync", lambda s_, d=xat[:, F * slot:F * (slot + 1)], s2=src: s_.dma_start(d, s2),
               waits=list(xa_free[slot]), inc=f"sXA{slot}", inc_amt=16)
            xa_ready[slot] = (f"sXA{slot}", at(f"sXA{slot}"))

        def dma_xf(tau, g):
            slot = (2 * tau + g) % 4
            src = xfv[g, 8 * tau:8 * tau + 8, :, :]
            op("sync", lambda s_, d=xft[:, F * slot:F * (slot + 1)], s2=src: s_.dma_start(d, s2),
               waits=list(xf_free[slot]), inc=f"sXF{slot}", inc_amt=16)
            xf_ready[slot] = (f"sXF{slot}", at(f"sXF{slot}"))

        router_state = {}
        bg_big = [[("sDw", w_init)], [("sDw", w_init)]]
        bg_S = [[], []]
        bd_guard = [[[], [], [], []], [[], [], [], []]]

        def bdslot(hh, s):
            buf = bdk[hh] if s < 2 else bdv[hh]
            off = 512 * (s % 2)
            return buf, off
        bg_Z = [[], []]
        pfin_guard = [("sDw", w_init)]
        bg_ptr = []
        sT_zrep_last = [0, 0]
        outsb_guard = [[]]
        prev_fin = [[]]
        par_free = [[], []]
        front_state = {}

        attn_stats_ready = [None]

        def attn_stats(tau):
            spar = tau % 2
            a_slots = [(4 * tau + j) % NXA for j in range(4)]
            for j, slot in enumerate(a_slots):
                w_ = [xa_ready[slot], ("sV", v_init)] + prev_fin[0] + list(outsb_guard[0])
                op("scalar", lambda s_, s2=xat[:, F * slot:F * slot + F], c=j: s_.activation(
                    outsb[0][:, :], s2, AF.Square, accum_out=mv[:, 2 + c:3 + c]),
                    waits=w_, inc="sS")
                op("vector", lambda v_, s2=xat[:, F * slot:F * slot + F], c=j: v_.tensor_scalar(
                    deadb2, s2, 1.0, 0.0, op0=ALU.mult, op1=ALU.add,
                    accum_out=bns[:, 2 + c:3 + c]),
                    waits=w_, inc="sV")
            op("scalar", lambda s_: s_.drain(), inc="sS")
            sS_st = at("sS")
            op("vector", lambda v_: v_.drain(), waits=[("sS", sS_st)])
            vtiny(lambda v_: v_.tensor_scalar(bns[:, 2:6], bns[:, 2:6], 1.0 / F, None,
                                              op0=ALU.mult))
            vtiny(lambda v_: v_.tensor_scalar(mv[:, 2:6], mv[:, 2:6], 1.0 / F, None,
                                              op0=ALU.mult))
            vtiny(lambda v_: v_.tensor_tensor(lnv[:, 2:6], bns[:, 2:6], bns[:, 2:6],
                                              op=ALU.mult))
            vtiny(lambda v_: v_.tensor_tensor(lnv[:, 2:6], mv[:, 2:6], lnv[:, 2:6],
                                              op=ALU.subtract))
            vtiny(lambda v_: v_.tensor_scalar(lnv[:, 2:6], lnv[:, 2:6], EPS, None,
                                              op0=ALU.add), mark=True)
            flush_tiny()
            op("scalar", lambda s_: s_.activation(lnv[:, 2:6], lnv[:, 2:6], AF.Ln),
               waits=[("sY", at("sY"))])
            op("scalar", lambda s_, o=rstd[:, 6 * spar:6 * spar + 4]: s_.activation(
                o, lnv[:, 2:6], AF.Exp, scale=-0.5), inc="sS")
            sS_r2 = at("sS")
            vtiny(lambda v_, o=nmr[:, 6 * spar:6 * spar + 4],
                  rs=rstd[:, 6 * spar:6 * spar + 4]: v_.scalar_tensor_tensor(
                o, bns[:, 2:6], -1.0, rs, op0=ALU.mult, op1=ALU.mult),
                waits=[("sS", sS_r2)], mark=True)
            flush_tiny()
            attn_stats_ready[0] = ("sY", at("sY"))
            prev_fin[0] = [("sY", at("sY"))]

        def front(tau):
            par = tau % 2
            a_slots = [(4 * tau + j) % NXA for j in range(4)]
            f_slots = [(2 * tau + g) % 4 for g in (0, 1)]
            for g in (0, 1):
                dma_xf(tau, g)
            ftiles = [("f", s, 4 + i) for i, s in enumerate(f_slots)]
            atiles = [("a", s, i) for i, s in enumerate(a_slots)]
            sT_tr_prev = [at("sT"), at("sT")]

            def apply_tiles(tiles, stats_w):
                for kind, slot, col in tiles:
                    buf = xat if kind == "a" else xft
                    src = buf[:, F * slot:F * (slot + 1)]
                    rp = col % 2
                    dstx = xnat[:, F * rp:F * (rp + 1)]
                    scol = 6 * par + col
                    w_ = list(stats_w) + [("sT", sT_tr_prev[rp])]
                    op("scalar", lambda s_, s2=src, d=dstx, c=scol: s_.activation(
                        d, s2, AF.Identity, bias=nmr[:, c:c + 1], scale=rstd[:, c:c + 1]),
                        waits=w_, inc="sS")
                    sS_ap = at("sS")
                    for k in range(KC):
                        w_ = ([("sS", sS_ap)] + list(bg_ptr)) if k == 0 else []
                        op("tensor", lambda t_, i_=dstx[:, 128 * k:128 * (k + 1)],
                           o_=ptr[:, 128 * k:128 * (k + 1)]: t_.transpose(o_, i_, eyeb[:, :]),
                           waits=w_, inc="sT" if k == KC - 1 else None)
                    sT_tr = at("sT")
                    sT_tr_prev[rp] = sT_tr
                    if kind == "a":
                        dfm = xnfm.ap().rearrange("p (k t) -> p k t", t=TA)[:, :, 128 * col:128 * (col + 1)]
                    else:
                        dfm = xnffm.ap().rearrange("p (k t) -> p k t", t=RF)[:, :, 128 * (col - 4):128 * (col - 3)]
                    sfm = ptr.ap().rearrange("p (k c) -> p k c", c=128)
                    if col % 2 == 0:
                        vop(lambda v_, s2=sfm, d=dfm: v_.tensor_copy(d, s2),
                            waits=[("sT", sT_tr)], inc="sV")
                        bg_ptr[:] = [("sV", at("sV"))]
                    else:
                        op("scalar", lambda s_, s2=sfm, d=dfm: s_.copy(d, s2),
                           waits=[("sT", sT_tr)], inc="sS")
                        bg_ptr[:] = [("sS", at("sS"))]
                return [("sV", at("sV")), ("sS", at("sS"))]

            afm_ready = apply_tiles(atiles, [attn_stats_ready[0]])
            for s in a_slots:
                xa_free[s] = list(afm_ready)
            if tau + 1 < NT:
                for j in range(4):
                    dma_xa(tau + 1, j)
            for kind, slot, col in ftiles:
                w_ = [xf_ready[slot], ("sV", v_init)] + prev_fin[0] + list(outsb_guard[0])
                op("scalar", lambda s_, s2=xft[:, F * slot:F * slot + F], c=col: s_.activation(
                    outsb[0][:, :], s2, AF.Square, accum_out=mv[:, c - 4:c - 3]),
                    waits=w_, inc="sS")
                op("vector", lambda v_, s2=xft[:, F * slot:F * slot + F], c=col: v_.tensor_scalar(
                    deadb2, s2, 1.0, 0.0, op0=ALU.mult, op1=ALU.add,
                    accum_out=bns[:, c - 4:c - 3]),
                    waits=w_, inc="sV")
            op("scalar", lambda s_: s_.drain(), inc="sS")
            sS_stats = at("sS")
            op("vector", lambda v_: v_.drain(), waits=[("sS", sS_stats)])
            vtiny(lambda v_: v_.tensor_scalar(bns[:, 0:2], bns[:, 0:2], 1.0 / F, None,
                                              op0=ALU.mult))
            vtiny(lambda v_: v_.tensor_scalar(mv[:, 0:2], mv[:, 0:2], 1.0 / F, None,
                                              op0=ALU.mult))
            vtiny(lambda v_: v_.tensor_tensor(lnv[:, 0:2], bns[:, 0:2], bns[:, 0:2],
                                              op=ALU.mult))
            vtiny(lambda v_: v_.tensor_tensor(lnv[:, 0:2], mv[:, 0:2], lnv[:, 0:2],
                                              op=ALU.subtract))
            vtiny(lambda v_: v_.tensor_scalar(lnv[:, 0:2], lnv[:, 0:2], EPS, None,
                                              op0=ALU.add), mark=True)
            flush_tiny()
            sY_l = at("sY")
            op("scalar", lambda s_: s_.activation(lnv[:, 0:2], lnv[:, 0:2], AF.Ln),
               waits=[("sY", sY_l)])
            op("scalar", lambda s_, o=rstd[:, 6 * par + 4:6 * par + 6]: s_.activation(
                o, lnv[:, 0:2], AF.Exp, scale=-0.5), inc="sS")
            sS_r = at("sS")
            vtiny(lambda v_, o=nmr[:, 6 * par + 4:6 * par + 6],
                  rs=rstd[:, 6 * par + 4:6 * par + 6]: v_.scalar_tensor_tensor(
                o, bns[:, 0:2], -1.0, rs, op0=ALU.mult, op1=ALU.mult),
                waits=[("sS", sS_r)], mark=True)
            flush_tiny()
            prev_fin[0] = [("sY", at("sY"))]
            if tau >= 1 and (tau - 1) not in router_state and (tau - 1) in front_state:
                router_front(tau - 1)

            cmp_ready = {}
            fi = 0
            for g in (0, 1):
                for j4 in range(4):
                    for hh in (0, 1):
                        bank = fi % 2
                        pb = pbig[bank]
                        for k in range(KC):
                            lhs = wqkv[:, 2048 * k + 1024 * g + 256 * j4 + 128 * hh:
                                       2048 * k + 1024 * g + 256 * j4 + 128 * (hh + 1)]
                            mov = xnfm[:, TA * k:TA * (k + 1)]
                            w_ = (list(afm_ready) + list(bg_big[bank]) + par_free[par]) \
                                if k == 0 else []
                            op("tensor", lambda t_, l=lhs, m=mov, k_=k, o=pb[:, :]: t_.matmul(
                                o, l, m, start=(k_ == 0), stop=(k_ == KC - 1)),
                                waits=w_, inc="sT" if k == KC - 1 else None)
                        sT_fill = at("sT")
                        arr = [qcmp, kcmp, avcmp, wcmp][j4][par][g][hh]
                        src_ap = pb[:, 0:512]
                        dst_ap = arr[:, :]
                        if fi % 2 == 0:
                            op("scalar", lambda s_, s2=src_ap, d=dst_ap: s_.copy(d, s2),
                               waits=[("sT", sT_fill)], inc="sS")
                            mark = ("sS", at("sS"))
                        else:
                            vop(lambda v_, s2=src_ap, d=dst_ap: v_.tensor_copy(d, s2),
                                waits=[("sT", sT_fill)], inc="sV")
                            mark = ("sV", at("sV"))
                        cmp_ready[(j4, g, hh)] = [mark]
                        bg_big[bank] = [mark]
                        fi += 1

            ffm_ready = apply_tiles(ftiles, [("sY", at("sY"))])
            for s in f_slots:
                xf_free[s] = list(ffm_ready)
            fm_ready = afm_ready + ffm_ready

            for dc in range(KC):
                bank = dc % 2
                pb = pbig[bank]
                for k in range(KC):
                    lhs = wvt[:, F * k + 128 * dc:F * k + 128 * (dc + 1)]
                    mov = xnffm[:, RF * k:RF * (k + 1)]
                    w_ = (list(ffm_ready) + list(bg_big[bank]) + par_free[par]) if k == 0 else []
                    op("tensor", lambda t_, l=lhs, m=mov, k_=k, o=pb[:, 0:RF]: t_.matmul(
                        o, l, m, start=(k_ == 0), stop=(k_ == KC - 1)),
                        waits=w_, inc="sT" if k == KC - 1 else None)
                sT_fill = at("sT")
                op("scalar", lambda s_, s2=pb[:, 0:RF], dc_=dc: s_.activation(
                    fvc[par][:, RF * dc_:RF * (dc_ + 1)], s2, AF.Identity,
                    bias=smalls[:, dc_:dc_ + 1]),
                    waits=[("sT", sT_fill)], inc="sS")
                bg_big[bank] = [("sS", at("sS"))]
            fv_ready = [("sS", at("sS"))]

            pb = pbig[0]
            for k in range(KC):
                w_ = (fv_ready + list(bg_big[0]) + par_free[par]) if k == 0 else []
                op("tensor", lambda t_, l=wgt[:, 8 * k:8 * (k + 1)],
                   m=fvc[par][:, RF * k:RF * (k + 1)], k_=k, o=pb[0:8, 0:RF]: t_.matmul(
                    o, l, m, start=(k_ == 0), stop=(k_ == KC - 1)),
                    waits=w_, inc="sT" if k == KC - 1 else None)
            vop(lambda v_, o=pb[0:8, 0:RF]: v_.tensor_copy(rsb[par][:, :], o),
                waits=[("sT", at("sT"))], inc="sV")
            bg_big[0] = [("sV", at("sV"))]
            pb = pbig[1]
            for k in range(KC):
                w_ = (fv_ready + list(bg_big[1]) + par_free[par]) if k == 0 else []
                op("tensor", lambda t_, l=wpt[:, 64 * k:64 * (k + 1)],
                   m=fvc[par][:, RF * k:RF * (k + 1)], k_=k, o=pb[0:64, 0:RF]: t_.matmul(
                    o, l, m, start=(k_ == 0), stop=(k_ == KC - 1)),
                    waits=w_, inc="sT" if k == KC - 1 else None)
            op("scalar", lambda s_, o=pb[0:64, 0:RF]: s_.activation(pv2[par][:, :], o, AF.Square),
               waits=[("sT", at("sT"))], inc="sS")
            sS_pv2 = at("sS")
            bg_big[1] = [("sS", sS_pv2)]
            pb = pbig[0]
            op("tensor", lambda t_, o=pb[0:8, 256:256 + RF]: t_.matmul(
                o, selp[:, :], pv2[par][:, :], start=True, stop=True),
               waits=[("sS", sS_pv2)] + list(bg_big[0]), inc="sT")
            vop(lambda v_, o=pb[0:8, 256:256 + RF]: v_.tensor_copy(c2sb[par][:, :], o),
                waits=[("sT", at("sT"))], inc="sV")
            bg_big[0] = [("sV", at("sV"))]

            front_state[tau] = dict(cmp_ready=cmp_ready, fv_ready=fv_ready,
                                    f_slots=f_slots, fm_ready=fm_ready)
            if tau + 1 < NT:
                attn_stats(tau + 1)

        def outsb_guard_xf(tau, slot):
            return []

        def router_front(rt):
            rpar = rt % 2
            trg = list(bg_S[0])
            for idx, srcb in enumerate((rsb[rpar], c2sb[rpar])):
                for w2 in range(2):
                    w_ = trg if (idx == 0 and w2 == 0) else []
                    op("tensor", lambda t_, i_=srcb[0:8, 128 * w2:128 * (w2 + 1)],
                       o_=psS[0][:, 8 * (2 * idx + w2):8 * (2 * idx + w2 + 1)]:
                       t_.transpose(o_, i_, eyef[0:8, 0:8]),
                       waits=w_, inc="sT" if (idx == 1 and w2 == 1) else None)
            sT_rt = at("sT")

            def re8f(ap_):
                return ap_.rearrange("p (w e) -> p w e", e=8)

            op("scalar", lambda s_: s_.copy(
                re8f(rtm.ap()), psS[0].ap().rearrange("p (w e) -> p w e", e=8)[:, 0:2, :]),
                waits=[("sT", sT_rt)], inc="sS")
            op("scalar", lambda s_: s_.copy(
                re8f(ctm.ap()), psS[0].ap().rearrange("p (w e) -> p w e", e=8)[:, 2:4, :]),
                inc="sS")
            bg_S[0] = [("sS", at("sS"))]
            sS_rc = at("sS")
            vtiny(lambda v_: v_.tensor_tensor(
                re8f(rtm.ap()), re8f(rtm.ap()),
                grbc[:, 32:40].unsqueeze(1).broadcast_to((128, 2, 8)), op=ALU.add),
                waits=[("sS", sS_rc)])
            vtiny(lambda v_: v_.tensor_scalar(ctm[:, :], ctm[:, :], 1e-30, None,
                                              op0=ALU.add), mark=True)
            sY_c = at("sY")
            sY_we_expected = sY_c + 3 + 1
            router_state[rt] = {"sY_c": sY_c, "sY_we_expected": sY_we_expected}

        def back(tau):
            par = tau % 2
            st = front_state.pop(tau)
            cmp_ready = st["cmp_ready"]
            fv_ready = st["fv_ready"]
            f_slots = st["f_slots"]

            rs = router_state.pop(tau)
            sY_we = rs["sY_we_expected"]

            def re8(ap_):
                return ap_.rearrange("p (w e) -> p w e", e=8)

            op("scalar", lambda s_: s_.activation(ctm[:, :], ctm[:, :], AF.Ln),
               waits=[("sY", rs["sY_c"])])
            op("scalar", lambda s_: s_.activation(ctm[:, :], ctm[:, :], AF.Exp, scale=0.5),
               inc="sS")
            sS_c = at("sS")

            def ln8_queue(srcb, mbuf, vbuf, waits=()):
                vtiny(lambda v_: v_.tensor_reduce(mbuf[:, :], re8(srcb.ap()),
                                                  axis=mybir.AxisListType.X, op=ALU.add),
                      waits=waits)
                vtiny(lambda v_: v_.tensor_scalar(mbuf[:, :], mbuf[:, :], 1.0 / 8, None,
                                                  op0=ALU.mult))
                vtiny(lambda v_: v_.tensor_tensor(t1[:, :], srcb[:, :], srcb[:, :],
                                                  op=ALU.mult))
                vtiny(lambda v_: v_.tensor_reduce(vbuf[:, :], re8(t1.ap()),
                                                  axis=mybir.AxisListType.X, op=ALU.add))
                vtiny(lambda v_: v_.tensor_scalar(vbuf[:, :], vbuf[:, :], 1.0 / 8, None,
                                                  op0=ALU.mult))
                vtiny(lambda v_: v_.tensor_tensor(st2d[:, :], mbuf[:, :], mbuf[:, :],
                                                  op=ALU.mult))
                vtiny(lambda v_: v_.tensor_tensor(vbuf[:, :], vbuf[:, :], st2d[:, :],
                                                  op=ALU.subtract))
                vtiny(lambda v_: v_.tensor_scalar(vbuf[:, :], vbuf[:, :], EPS, None,
                                                  op0=ALU.add), mark=True)

            vtiny(lambda v_: v_.tensor_scalar(ctm[:, :], ctm[:, :],
                                              float(1.0 / np.sqrt(8.0)), None, op0=ALU.mult),
                  waits=[("sS", sS_c)])
            ln8_queue(rtm, st2c, st2b)
            ln8_queue(ctm, st2cc, st2bc)
            sY_v = at("sY")

            zw_ready = [[], []]
            for g in (0, 1):
                pair_rdy = {}

                def emit_pair(kbp):
                    kb = 2 * kbp
                    slot = kb % 4
                    for hh in (0, 1):
                        in0 = kcmp[par][g][hh][:, 64 * kb:64 * (kb + 2)]
                        in0 = in0.rearrange("p (u a) -> p u a", a=16).unsqueeze(2)
                        in0 = in0.broadcast_to((128, 8, 8, 16))
                        in1 = blk8.ap().rearrange("p (h a) -> p h a", a=16).unsqueeze(1)
                        in1 = in1.broadcast_to((128, 8, 8, 16))
                        sbuf_, soff = bdslot(hh, slot)
                        dstb = sbuf_[:, 0:1024]
                        dstb = dstb.rearrange("p (u h a) -> p u h a", h=8, a=16)
                        w_ = (cmp_ready[(1, g, hh)] + bd_guard[hh][slot]
                              + bd_guard[hh][slot + 1])
                        if kbp == 0:
                            op("gpsimd", lambda g_, a_=dstb, b_=in0, c_=in1:
                               g_.tensor_tensor(a_, b_, c_, op=ALU.mult),
                               waits=w_, inc="sG")
                            pair_rdy[(kbp, hh)] = ("sG", at("sG"))
                        else:
                            vop(lambda v_, a_=dstb, b_=in0, c_=in1: v_.tensor_tensor(
                                a_, b_, c_, op=ALU.mult), waits=w_, inc="sV")
                            pair_rdy[(kbp, hh)] = ("sV", at("sV"))

                def emit_S(kb):
                    slot = kb % 4
                    for hh in (0, 1):
                        sbuf_, soff = bdslot(hh, slot)
                        for uu in range(4):
                            u = 4 * kb + uu
                            lhs = sbuf_[:, soff + 128 * uu:soff + 128 * (uu + 1)]
                            mov = qcmp[par][g][hh][:, 16 * u:16 * (u + 1)]
                            o = psS[hh][:, 16 * u:16 * (u + 1)]
                            w_ = []
                            if uu == 0:
                                w_ = cmp_ready[(0, g, hh)]
                                if kb % 2 == 0:
                                    w_ = w_ + [pair_rdy[(kb // 2, hh)]]
                                if kb == 0:
                                    w_ += bg_S[hh]
                            op("tensor", lambda t_, l=lhs, m=mov, o_=o: t_.matmul(
                                o_, l, m, start=True, stop=True), waits=w_,
                                inc="sT" if uu == 3 else None)
                        bd_guard[hh][slot] = [("sT", at("sT"))]

                emit_pair(0)
                emit_pair(1)
                emit_S(0)
                emit_S(1)
                emit_pair(2)
                emit_S(2)
                emit_S(3)
                emit_pair(3)
                for kb in range(4, 8):
                    emit_S(kb)
                sT_S = at("sT")
                for hh in (0, 1):
                    op("scalar", lambda s_, hh_=hh: s_.activation(
                        pcmp[hh_][:, :], psS[hh_][:, :], AF.Exp, scale=0.25),
                        waits=[("sT", sT_S)], inc="sS")
                sS_P = at("sS")
                for hh in (0, 1):
                    op("tensor", lambda t_, hh_=hh: t_.matmul(
                        psS[hh_][0:8, :], ones8[:, :], pcmp[hh_][:, :],
                        start=True, stop=True),
                        waits=[("sS", sS_P)] if hh == 0 else [], inc="sT")
                vtm_rdy = [0, 0]
                sT_v_m = [0, 0]
                for hh in (0, 1):
                    pzb = psZ[hh].ap().bitcast(BF16)
                    for ub in range(4):
                        w_ = (cmp_ready[(2, g, hh)] + list(bg_Z[hh])) if ub == 0 else []
                        op("tensor", lambda t_, i_=avcmp[par][g][hh][:, 128 * ub:128 * (ub + 1)],
                           o_=pzb[:, 128 * ub:128 * (ub + 1)]: t_.transpose(o_, i_, eyeb[:, :]),
                           waits=w_, inc="sT" if ub == 3 else None)
                    sT_v_m[hh] = at("sT")
                sT_Z = at("sT")
                for hh in (0, 1):
                    op("scalar", lambda s_, hh_=hh: s_.activation(
                        thsb[32 * hh_:32 * hh_ + 8, 0:512], psS[hh_][0:8, :], AF.Ln),
                        waits=[("sT", sT_Z), ("sT", sT_zrep_last[hh])])
                    op("scalar", lambda s_, hh_=hh: s_.activation(
                        zrs[hh_][:, :], thsb[32 * hh_:32 * hh_ + 8, 0:512], AF.Exp,
                        scale=-1.0), inc="sS")
                    sS_zr = at("sS")
                    op("tensor", lambda t_, hh_=hh: t_.matmul(
                        psS[hh_][:, :], onesc[:, :], zrs[hh_][:, :], start=True, stop=True),
                        waits=[("sS", sS_zr)], inc="sT")
                    sT_zrep_last[hh] = at("sT")
                    vop(lambda v_, hh_=hh, wc=wcmp[par][g][hh][:, :]: v_.scalar_tensor_tensor(
                        wzr[hh_][:, :], psS[hh_][:, :], 1.0, wc,
                        op0=ALU.mult, op1=ALU.mult),
                        waits=[("sT", sT_zrep_last[hh])] + cmp_ready[(3, g, hh)], inc="sV")
                    bg_S[hh] = [("sV", at("sV"))]
                sV_wzr = at("sV")
                for hh in (0, 1):
                    pzb = psZ[hh].ap().bitcast(BF16)
                    op("scalar", lambda s_, hh_=hh, s2=pzb[:, 0:512]: s_.copy(
                        vtm[hh_][:, :], s2), waits=[("sT", sT_v_m[hh])], inc="sS")
                    vtm_rdy[hh] = ("sS", at("sS"))
                    bg_Z[hh] = [("sS", at("sS"))]

                zg = [list(bg_Z[0]), list(bg_Z[1])]
                bank_guards = [list(pfin_guard), [("sV", sV_wzr)], [("sV", sV_wzr)]]
                bdv_mark = {}

                def emit_rep_bdv(hh, qb):
                    bidx = qb % 3
                    slot = qb % 4
                    zbuf_, boff = bdslot(hh, slot)
                    bank = [pfin, psS[0], psS[1]][bidx]
                    for uu in range(4):
                        u = 4 * qb + uu
                        ub, r8 = u // 8, u % 8
                        w_ = []
                        if uu == 0:
                            w_ = list(bank_guards[bidx])
                            if qb <= 1:
                                w_ += [vtm_rdy[hh]]
                        op("tensor", lambda t_, l=reps[:, 128 * r8:128 * (r8 + 1)],
                           m=vtm[hh][:, 128 * ub:128 * (ub + 1)],
                           o=bank[:, 128 * uu:128 * (uu + 1)]: t_.matmul(
                            o, l, m, start=True, stop=True),
                            waits=w_, inc="sT" if uu == 3 else None)
                    sT_rep = at("sT")
                    in1 = blk8.ap().unsqueeze(1).broadcast_to((128, 4, 128))
                    dstb = zbuf_[:, boff:boff + 512].rearrange("p (q c) -> p q c", c=128)
                    vop(lambda v_, d=dstb, m=in1, b=bank: v_.scalar_tensor_tensor(
                        d, b.ap().rearrange("p (q c) -> p q c", c=128), 1.0, m,
                        op0=ALU.mult, op1=ALU.mult),
                        waits=[("sT", sT_rep)] + bd_guard[hh][slot],
                        inc="sV")
                    sV_bdv = at("sV")
                    bdv_mark[(hh, qb)] = sV_bdv
                    bank_guards[bidx] = [("sV", sV_bdv)]
                    if bidx == 0:
                        pfin_guard[:] = [("sV", sV_bdv)]

                def emit_z(hh, qb):
                    slot = qb % 4
                    zbuf_, boff = bdslot(hh, slot)
                    for uu in range(4):
                        u = 4 * qb + uu
                        w_ = []
                        if uu == 0:
                            w_ = [("sV", bdv_mark[(hh, qb)])]
                            if qb == 0:
                                w_ += zg[hh]
                        op("tensor", lambda t_,
                           l=zbuf_[:, boff + 128 * uu:boff + 128 * (uu + 1)],
                           m=pcmp[hh][:, 16 * u:16 * (u + 1)],
                           o=psZ[hh][:, 128 * uu + 16 * qb:128 * uu + 16 * qb + 16]:
                           t_.matmul(o, l, m, start=True, stop=True),
                            waits=w_, inc="sT" if uu == 3 else None)
                    bd_guard[hh][slot] = [("sT", at("sT"))]

                for hh in (0, 1):
                    emit_rep_bdv(hh, 0)
                    emit_rep_bdv(hh, 1)
                    for qb in range(2, 8):
                        emit_rep_bdv(hh, qb)
                        emit_z(hh, qb - 2)
                    emit_z(hh, 6)
                    emit_z(hh, 7)
                    sT_z = at("sT")
                    for r4 in range(4):
                        srcz = psZ[hh][:, 128 * r4:128 * (r4 + 1)].rearrange(
                            "p (u4 a) -> p u4 a", a=16)
                        w1_ = wzr[hh].ap().rearrange(
                            "p (u4 r a) -> p u4 r a", r=4, a=16)[:, :, r4, :]
                        dstz = zw[g][hh][:, 128 * r4:128 * (r4 + 1)]
                        dstz = dstz.rearrange("p (u4 a) -> p u4 a", a=16)
                        w_ = [("sT", sT_z), ("sV", sV_wzr)] if r4 == 0 else []
                        vop(lambda v_, s2=srcz, d=dstz, m=w1_: v_.scalar_tensor_tensor(
                            d, s2, 1.0, m, op0=ALU.mult, op1=ALU.mult),
                            waits=w_, inc="sV" if r4 == 3 else None)
                    bg_Z[hh] = [("sV", at("sV"))]
                    if hh == 1:
                        bg_S[0] = list(bank_guards[1])
                        bg_S[1] = list(bank_guards[2])
                    zw_ready[g] = zw_ready[g] + [("sV", at("sV"))]

                if g == 0:
                    op("scalar", lambda s_: s_.activation(st2b[:, :], st2b[:, :], AF.Ln),
                       waits=[("sY", sY_v)])
                    op("scalar", lambda s_: s_.activation(st2bc[:, :], st2bc[:, :], AF.Ln))
                    op("scalar", lambda s_: s_.drain())
                    op("scalar", lambda s_: s_.activation(st2b[:, :], st2b[:, :], AF.Exp,
                                                          scale=-0.5))
                    op("scalar", lambda s_: s_.activation(st2bc[:, :], st2bc[:, :], AF.Exp,
                                                          scale=-0.5), inc="sS")
                    sS_rstd = at("sS")

                    def ln8_apply(dst_, srcb, mbuf, vbuf, gcol, bcol, waits=()):
                        vtiny(lambda v_: v_.tensor_tensor(
                            re8(dst_.ap()), re8(srcb.ap()),
                            mbuf[:, :].unsqueeze(2).broadcast_to((128, 2, 8)),
                            op=ALU.subtract), waits=waits)
                        vtiny(lambda v_: v_.tensor_tensor(
                            re8(dst_.ap()), re8(dst_.ap()),
                            vbuf[:, :].unsqueeze(2).broadcast_to((128, 2, 8)), op=ALU.mult))
                        vtiny(lambda v_: v_.tensor_tensor(
                            re8(dst_.ap()), re8(dst_.ap()),
                            grbc[:, gcol:gcol + 8].unsqueeze(1).broadcast_to((128, 2, 8)),
                            op=ALU.mult))
                        vtiny(lambda v_: v_.tensor_tensor(
                            re8(dst_.ap()), re8(dst_.ap()),
                            grbc[:, bcol:bcol + 8].unsqueeze(1).broadcast_to((128, 2, 8)),
                            op=ALU.add))

                    ln8_apply(sgt, rtm, st2c, st2b, 0, 8, waits=[("sS", sS_rstd)])
                    ln8_apply(t2, ctm, st2cc, st2bc, 16, 24)
                    vtiny(lambda v_: v_.scalar_tensor_tensor(
                        sgt[:, :], t2[:, :], sigpp[:, 0:1], sgt[:, :],
                        op0=ALU.mult, op1=ALU.add))
                    vtiny(lambda v_: v_.tensor_reduce(st2c[:, :], re8(sgt.ap()),
                                                      axis=mybir.AxisListType.X, op=ALU.max))
                    vtiny(lambda v_: v_.tensor_tensor(
                        re8(t1.ap()), re8(sgt.ap()),
                        st2c[:, :].unsqueeze(2).broadcast_to((128, 2, 8)), op=ALU.is_ge))
                    vtiny(lambda v_: v_.scalar_tensor_tensor(
                        t2[:, :], t1[:, :], -1e30, sgt[:, :], op0=ALU.mult, op1=ALU.add))
                    vtiny(lambda v_: v_.tensor_reduce(st2b[:, :], re8(t2.ap()),
                                                      axis=mybir.AxisListType.X, op=ALU.max))
                    vtiny(lambda v_: v_.tensor_tensor(
                        re8(t3.ap()), re8(t2.ap()),
                        st2b[:, :].unsqueeze(2).broadcast_to((128, 2, 8)), op=ALU.is_ge))
                    vtiny(lambda v_: v_.tensor_tensor(st2d[:, :], st2c[:, :], st2b[:, :],
                                                      op=ALU.subtract), mark=True)
                    sY_d = at("sY")
                else:
                    op("scalar", lambda s_: s_.activation(st2d[:, :], st2d[:, :], AF.Tanh,
                                                          scale=0.5),
                       waits=[("sY", sY_d)], inc="sS")
                    sS_w1 = at("sS")
                    vtiny(lambda v_: v_.tensor_scalar(st2d[:, :], st2d[:, :], 0.5, 0.5,
                                                      op0=ALU.mult, op1=ALU.add),
                          waits=[("sS", sS_w1)])
                    vtiny(lambda v_: v_.tensor_tensor(
                        re8(wetm.ap()), re8(t1.ap()),
                        st2d[:, :].unsqueeze(2).broadcast_to((128, 2, 8)), op=ALU.mult))
                    vtiny(lambda v_: v_.tensor_scalar(st2d[:, :], st2d[:, :], -1.0, 1.0,
                                                      op0=ALU.mult, op1=ALU.add))
                    vtiny(lambda v_: v_.tensor_tensor(
                        re8(t3.ap()), re8(t3.ap()),
                        st2d[:, :].unsqueeze(2).broadcast_to((128, 2, 8)), op=ALU.mult))
                    vtiny(lambda v_: v_.tensor_tensor(wetm[:, :], wetm[:, :], t3[:, :],
                                                      op=ALU.add), mark=True)
                    sY_we = at("sY")

                if debug and tau == 0 and g == 0:
                    def dump(name, src_ap, waits):
                        op("sync", lambda s_, d=dbg[name], s2=src_ap: s_.dma_start(d, s2),
                           waits=waits, inc="sDo", inc_amt=16)
                    dump("dbg_q", qcmp[0][0][0][:, :], cmp_ready[(0, 0, 0)])
                    dump("dbg_k", kcmp[0][0][0][:, :], cmp_ready[(1, 0, 0)])
                    dump("dbg_av", avcmp[0][0][0][:, :], cmp_ready[(2, 0, 0)])
                    dump("dbg_w", wcmp[0][0][0][:, :], cmp_ready[(3, 0, 0)])
                    dump("dbg_p", pcmp[0][:, :], [("sS", sS_P)])
                    dump("dbg_zw", zw[0][0][:, :], [("sV", at("sV"))])
                    dump("dbg_wzr", wzr[0][:, :], [("sV", sV_wzr)])
                    dump("dbg_vtm", vtm[0][:, :], [("sV", at("sV"))])

            flush_tiny()
            if debug and tau == 0:
                op("sync", lambda s_: s_.dma_start(dbg["dbg_we"], wetm[:, :]),
                   waits=[("sY", sY_we)], inc="sDo", inc_amt=16)
                op("sync", lambda s_: s_.dma_start(dbg["dbg_rtm"], rtm[:, :]),
                   waits=[("sY", sY_we)], inc="sDo", inc_amt=16)
                op("sync", lambda s_: s_.dma_start(dbg["dbg_ctm"], ctm[:, :]),
                   waits=[("sY", sY_we)], inc="sDo", inc_amt=16)
                op("sync", lambda s_: s_.dma_start(dbg["dbg_sgt"], sgt[:, :]),
                   waits=[("sY", sY_we)], inc="sDo", inc_amt=16)
            pr = psS[0]
            for w2 in range(2):
                w_ = ([("sY", sY_we)] + list(bg_S[0])) if w2 == 0 else []
                op("tensor", lambda t_, i_=wetm[:, 8 * w2:8 * (w2 + 1)],
                   o_=pr[0:8, 128 * w2:128 * (w2 + 1)]: t_.transpose(o_, i_, eyef[:, :]),
                   waits=w_, inc="sT" if w2 == 1 else None)
            sT_wef = at("sT")
            op("vector", lambda v_: v_.tensor_copy(wef[:, 0:128], pr[0:8, 0:128]),
               waits=[("sT", sT_wef)], inc="sV")
            op("vector", lambda v_: v_.tensor_copy(wef[:, 128:256], pr[0:8, 128:256]),
               inc="sV")
            sV_wef = at("sV")
            bg_S[0] = [("sV", sV_wef)]
            for ch in (0, 1):
                pb2 = psS[ch]
                w_ = [("sV", sV_wef)] + (list(bg_S[1]) if ch == 1 else list(bg_S[0]))
                op("tensor", lambda t_, l=sel32[:, 128 * ch:128 * (ch + 1)], b=pb2:
                   t_.matmul(b[:, 0:RF], l, wef[:, :], start=True, stop=True),
                   waits=w_, inc="sT")
                op("scalar", lambda s_, b=pb2, ch_=ch: s_.copy(
                    werep[:, RF * ch_:RF * (ch_ + 1)], b[:, 0:RF]),
                    waits=[("sT", at("sT"))], inc="sS")
                bg_S[ch] = [("sS", at("sS"))]
            sS_wrep = at("sS")
            for ch in (0, 1):
                pb2 = psS[ch]
                for k in range(KC):
                    lhs = w1t[:, 256 * k + 128 * ch:256 * k + 128 * (ch + 1)]
                    mov = fvc[par][:, RF * k:RF * (k + 1)]
                    w_ = (fv_ready + list(bg_S[ch])) if k == 0 else []
                    op("tensor", lambda t_, l=lhs, m=mov, b=pb2, k_=k: t_.matmul(
                        b[:, 0:RF], l, m, start=(k_ == 0), stop=(k_ == KC - 1)),
                        waits=w_, inc="sT" if k == KC - 1 else None)
                sT_h = at("sT")
                op("scalar", lambda s_, b=pb2, ch_=ch: s_.activation(
                    thsb[:, RF * ch_:RF * (ch_ + 1)], b[:, 0:RF], AF.Tanh, scale=0.5,
                    bias=hbias[:, ch_:ch_ + 1]), waits=[("sT", sT_h)], inc="sS")
                sS_th = at("sS")
                vtiny(lambda v_, ch_=ch: v_.tensor_scalar(
                    thsb[:, RF * ch_:RF * (ch_ + 1)], thsb[:, RF * ch_:RF * (ch_ + 1)],
                    0.5, 0.5, op0=ALU.mult, op1=ALU.add), waits=[("sS", sS_th)])
                vtiny(lambda v_, b=pb2, ch_=ch: v_.scalar_tensor_tensor(
                    thsb[:, RF * ch_:RF * (ch_ + 1)], b[:, 0:RF], smalls[:, 8 + ch_:9 + ch_],
                    thsb[:, RF * ch_:RF * (ch_ + 1)], op0=ALU.add, op1=ALU.mult))
                vtiny(lambda v_, ch_=ch: v_.tensor_tensor(
                    hssb[:, RF * ch_:RF * (ch_ + 1)], thsb[:, RF * ch_:RF * (ch_ + 1)],
                    werep[:, RF * ch_:RF * (ch_ + 1)], op=ALU.mult),
                    waits=[("sS", sS_wrep)], mark=True)
                flush_tiny()
                bg_S[ch] = [("sY", at("sY"))]
            sY_hs = at("sY")
            if debug and tau == 0:
                op("sync", lambda s_: s_.dma_start(dbg["dbg_hs"], hssb[:, :]),
                   waits=[("sY", sY_hs)], inc="sDo", inc_amt=16)
                op("sync", lambda s_: s_.dma_start(dbg["dbg_fv"], fvc[0][:, :]),
                   waits=fv_ready, inc="sDo", inc_amt=16)
                op("sync", lambda s_: s_.dma_start(dbg["dbg_xnfm"], xnfm[:, :]),
                   waits=st["fm_ready"], inc="sDo", inc_amt=16)
                op("sync", lambda s_: s_.dma_start(dbg["dbg_xnffn"], xnffm[:, :]),
                   waits=st["fm_ready"], inc="sDo", inc_amt=16)

            for g in (0, 1):
                for w2 in (0, 1):
                    first = True
                    w0_ = list(pfin_guard) + [("sY", sY_hs), ("sDw", w_init)] + zw_ready[g]
                    for r4 in range(4):
                        for hh in (0, 1):
                            lhs = zw[g][hh][:, 128 * r4:128 * (r4 + 1)]
                            mov = wout[:, F * (2 * r4 + hh) + 512 * w2:
                                       F * (2 * r4 + hh) + 512 * (w2 + 1)]
                            w_ = w0_ if first else []
                            op("tensor", lambda t_, l=lhs, m=mov, st_=first: t_.matmul(
                                pfin[:, :], l, m, start=st_, stop=False), waits=w_)
                            first = False
                    for ch in (0, 1):
                        lhs = hssb[:, RF * ch + 128 * g:RF * ch + 128 * (g + 1)]
                        mov = w2t[:, F * ch + 512 * w2:F * ch + 512 * (w2 + 1)]
                        op("tensor", lambda t_, l=lhs, m=mov: t_.matmul(
                            pfin[:, :], l, m, start=False, stop=False))
                    op("tensor", lambda t_, g_=g, w2_=w2: t_.matmul(
                        pfin[:, :], wef[:, 128 * g_:128 * (g_ + 1)],
                        b2sb[:, 512 * w2_:512 * (w2_ + 1)], start=False, stop=False))
                    for q in range(4):
                        dc = 4 * w2 + q
                        lhs = fvc[par][:, RF * dc + 128 * g:RF * dc + 128 * (g + 1)]
                        stop = (q == 3)
                        op("tensor", lambda t_, l=lhs, o=pfin[:, 128 * q:128 * (q + 1)],
                           sp=stop: t_.matmul(o, l, eyeb[:, :], start=False, stop=sp),
                           inc="sT" if stop else None)
                    sT_fin = at("sT")
                    slot = f_slots[g]
                    xsrc = xft[:, F * slot + 512 * w2:F * slot + 512 * (w2 + 1)]
                    vop(lambda v_, xs=xsrc, o=outsb[g][:, 512 * w2:512 * (w2 + 1)]:
                        v_.scalar_tensor_tensor(o, pfin[:, :], 1.0, xs,
                                                op0=ALU.mult, op1=ALU.add),
                        waits=[("sT", sT_fin)] + list(outsb_guard[0]), inc="sV")
                    outsb_guard[0] = []
                    pfin_guard[:] = [("sV", at("sV"))]
                out_mark = [("sV", at("sV"))]
                dstr = outv[g, 8 * tau:8 * tau + 8, :, :]
                op("sync", lambda s_, d=dstr, g_=g: s_.dma_start(d, outsb[g_][:, :]),
                   waits=out_mark, inc="sOB0", inc_amt=16)
                outsb_guard[0] = [("sOB0", at("sOB0"))]
                xf_free[f_slots[g]] = xf_free[f_slots[g]] + out_mark
            par_free[par] = [("sT", at("sT")), ("sV", at("sV")), ("sS", at("sS"))]

        for j in range(4):
            dma_xa(0, j)
        attn_stats(0)
        front(0)
        front(1)
        for tau in range(NT):
            if tau not in router_state:
                router_front(tau)
            back(tau)
            if tau + 2 < NT:
                front(tau + 2)

        total_out = at("sDo") if debug else 0

        global _captured_ops
        if CAPTURE_OPS:
            _captured_ops = list(ops)

        block = e(nc.Block())

        def emit_for(eng_name):
            def emitter(eng):
                for eng2, fn, waits, incs in ops:
                    if eng2 != eng_name:
                        continue
                    for sem_n, val in waits:
                        eng.wait_ge(sems[sem_n], val)
                    inst = fn(eng)
                    for sem_n, amt in incs:
                        inst.then_inc(sems[sem_n], amt)
                if eng_name == "sync":
                    if debug and total_out > 0:
                        eng.wait_ge(sems["sDo"], total_out)
                    eng.wait_ge(sems["sOB0"], counts.get("sOB0", 0))
            return emitter

        block.sync(emit_for("sync"))
        block.gpsimd(emit_for("gpsimd"))
        block.tensor(emit_for("tensor"))
        block.vector(emit_for("vector"))
        block.scalar(emit_for("scalar"))

    return nc



def _mk_consts():
    eyeb = np.eye(128, dtype=ml_dtypes.bfloat16)
    eyef = np.eye(128, dtype=np.float32)
    blk8 = np.zeros((128, 128), dtype=ml_dtypes.bfloat16)
    for p in range(128):
        blk8[p, (p // 16) * 16:(p // 16) * 16 + 16] = 1
    ones8 = np.zeros((128, 8), dtype=ml_dtypes.bfloat16)
    for p in range(128):
        ones8[p, p // 16] = 1
    onesc = np.zeros((8, 128), dtype=ml_dtypes.bfloat16)
    for hh in range(8):
        onesc[hh, 16 * hh:16 * (hh + 1)] = 1
    sel32 = np.zeros((8, 256), dtype=ml_dtypes.bfloat16)
    for ch in range(2):
        for p in range(128):
            sel32[4 * ch + p // 32, 128 * ch + p] = 1
    selp = np.zeros((64, 8), dtype=ml_dtypes.bfloat16)
    for ep in range(64):
        selp[ep, ep // 8] = 1
    reps = np.zeros((8, 128, 128), dtype=ml_dtypes.bfloat16)
    for r8 in range(8):
        for a2 in range(16):
            for hh in range(8):
                reps[r8, 16 * r8 + a2, 16 * hh + a2] = 1
    reps = reps.transpose(1, 0, 2).reshape(128, 1024)
    return eyeb, eyef, blk8, ones8, onesc, sel32, selp, reps


def _host_prep(inp):
    x = np.ascontiguousarray(np.asarray(inp["L0_attn_x"], dtype=np.float32))
    Wqkv = np.asarray(inp["L0_attn_w_qkvw"], dtype=np.float32)
    Wout = np.ascontiguousarray(np.asarray(inp["L0_attn_w_out"], dtype=np.float32))
    Wv = np.asarray(inp["L0_ffn_Wv"], dtype=np.float32)
    bv = np.asarray(inp["L0_ffn_bv"], dtype=np.float32)
    Wg = np.asarray(inp["L0_ffn_Wg"], dtype=np.float32)
    bg = np.asarray(inp["L0_ffn_bg"], dtype=np.float32)
    Wp = np.asarray(inp["L0_ffn_Wp"], dtype=np.float32)
    alpha = np.ascontiguousarray(np.broadcast_to(
        np.asarray(inp["L0_ffn_alpha"], dtype=np.float32).reshape(1, 1), (128, 1)))
    W1 = np.asarray(inp["L0_ffn_W1"], dtype=np.float32)
    b1 = np.asarray(inp["L0_ffn_b1"], dtype=np.float32)
    W2 = np.asarray(inp["L0_ffn_W2"], dtype=np.float32)
    b2 = np.ascontiguousarray(np.asarray(inp["L0_ffn_b2"], dtype=np.float32))

    Wq6 = Wqkv.reshape(1024, 4, 4, 2, 8, 16)
    wvt = np.ascontiguousarray(Wv.T)
    wgt = np.ascontiguousarray(Wg.T)
    wpt = np.ascontiguousarray(Wp.reshape(64, 1024).T)
    w1t = np.ascontiguousarray(W1.reshape(256, 1024).T)
    w2t = np.ascontiguousarray(W2.transpose(0, 2, 1).reshape(256, 1024))
    smalls = np.zeros((128, 16), dtype=np.float32)
    smalls[:, 0:8] = bv.reshape(8, 128).T
    smalls[:, 8:10] = b1.reshape(2, 128).T
    grbc = np.zeros((128, 40), dtype=np.float32)
    grbc[:, 0:8] = np.asarray(inp["L0_ffn_g_r"], dtype=np.float32)[None, :]
    grbc[:, 8:16] = np.asarray(inp["L0_ffn_b_r"], dtype=np.float32)[None, :]
    grbc[:, 16:24] = np.asarray(inp["L0_ffn_g_c"], dtype=np.float32)[None, :]
    grbc[:, 24:32] = np.asarray(inp["L0_ffn_b_c"], dtype=np.float32)[None, :]
    grbc[:, 32:40] = bg[None, :]
    eyeb, eyef, blk8, ones8, onesc, sel32, selp, reps = _mk_consts()

    in_maps = []
    for c in range(NCORES):
        b = c // 2
        i0, i1 = 2 * (c % 2), 2 * (c % 2) + 1
        x_attn = np.ascontiguousarray(x[b])
        x_ffn = np.ascontiguousarray(np.stack([x[i0, b * 2048:(b + 1) * 2048],
                                               x[i1, b * 2048:(b + 1) * 2048]]))
        wq = np.ascontiguousarray(
            np.stack([Wq6[:, i0], Wq6[:, i1]], axis=1).reshape(1024, 2048))
        in_maps.append({
            "x_attn": x_attn, "x_ffn": x_ffn, "wqkv": wq, "wout": Wout,
            "wvt": wvt, "wgt": wgt, "wpt": wpt, "w1t": w1t, "w2t": w2t,
            "b2": b2, "smalls": smalls, "grbc": grbc, "alpha": alpha,
            "eyeb": eyeb, "eyef": eyef, "blk8": blk8, "ones8": ones8,
            "onesc": onesc, "sel32": sel32, "selp": selp, "reps": reps,
        })
    return in_maps


def kernel(**inputs):
    in_maps = _host_prep(inputs)
    if "prog" not in _prog_cache:
        _prog_cache["prog"] = build_nc(debug=False)
    nc = _prog_cache["prog"]
    res = run_bass_kernel_spmd(nc, in_maps, core_ids=list(range(NCORES)))
    out = np.empty((4, 8192, 1024), dtype=np.float32)
    for c in range(NCORES):
        b = c // 2
        i0, i1 = 2 * (c % 2), 2 * (c % 2) + 1
        oc = res.results[c]["out"]
        out[i0, b * 2048:(b + 1) * 2048] = oc[0]
        out[i1, b * 2048:(b + 1) * 2048] = oc[1]
    return out


# version 32
# speedup vs baseline: 1.0262x; 1.0029x over previous
import numpy as np
import ml_dtypes
import concourse.bass as bass
import concourse.mybir as mybir
from concourse.bass_utils import run_bass_kernel_spmd

F32 = mybir.dt.float32
BF16 = mybir.dt.bfloat16
AF = mybir.ActivationFunctionType
ALU = mybir.AluOpType

NCORES = 8
F = 1024
KC = 8
UT = 32
NT = 512 // UT
TA = 16 * UT
RF = 256
EPS = 1e-5

_prog_cache = {}
CAPTURE_OPS = False
_captured_ops = None


def build_nc(debug=False):
    nc = bass.Bass("TRN2", target_bir_lowering=False)

    d_xattn = nc.dram_tensor("x_attn", [8192, F], F32, kind="ExternalInput").ap()
    d_xffn = nc.dram_tensor("x_ffn", [2, 2048, F], F32, kind="ExternalInput").ap()
    d_wqkv = nc.dram_tensor("wqkv", [F, 2048], F32, kind="ExternalInput").ap()
    d_wout = nc.dram_tensor("wout", [F, F], F32, kind="ExternalInput").ap()
    d_wvt = nc.dram_tensor("wvt", [F, F], F32, kind="ExternalInput").ap()
    d_wgt = nc.dram_tensor("wgt", [F, 8], F32, kind="ExternalInput").ap()
    d_wpt = nc.dram_tensor("wpt", [F, 64], F32, kind="ExternalInput").ap()
    d_w1t = nc.dram_tensor("w1t", [F, 256], F32, kind="ExternalInput").ap()
    d_w2t = nc.dram_tensor("w2t", [256, F], F32, kind="ExternalInput").ap()
    d_b2 = nc.dram_tensor("b2", [8, F], F32, kind="ExternalInput").ap()
    d_smalls = nc.dram_tensor("smalls", [128, 16], F32, kind="ExternalInput").ap()
    d_grbc = nc.dram_tensor("grbc", [128, 40], F32, kind="ExternalInput").ap()
    d_alpha = nc.dram_tensor("alpha", [128, 1], F32, kind="ExternalInput").ap()
    d_eyeb = nc.dram_tensor("eyeb", [128, 128], BF16, kind="ExternalInput").ap()
    d_eyef = nc.dram_tensor("eyef", [128, 128], F32, kind="ExternalInput").ap()
    d_blk8 = nc.dram_tensor("blk8", [128, 128], BF16, kind="ExternalInput").ap()
    d_ones8 = nc.dram_tensor("ones8", [128, 8], BF16, kind="ExternalInput").ap()
    d_onesc = nc.dram_tensor("onesc", [8, 128], BF16, kind="ExternalInput").ap()
    d_sel32 = nc.dram_tensor("sel32", [8, 256], BF16, kind="ExternalInput").ap()
    d_selp = nc.dram_tensor("selp", [64, 8], BF16, kind="ExternalInput").ap()
    d_reps = nc.dram_tensor("reps", [128, 1024], BF16, kind="ExternalInput").ap()
    d_out = nc.dram_tensor("out", [2, 2048, F], F32, kind="ExternalOutput").ap()

    dbg = {}
    if debug:
        for name, shape, dt in [
            ("dbg_xnfm", [128, KC * TA], BF16), ("dbg_q", [128, TA], BF16),
            ("dbg_k", [128, TA], BF16), ("dbg_av", [128, TA], BF16),
            ("dbg_w", [128, TA], BF16), ("dbg_p", [128, TA], BF16),
            ("dbg_zw", [128, TA], BF16), ("dbg_fv", [128, KC * RF], BF16),
            ("dbg_we", [128, 16], F32), ("dbg_hs", [128, 2 * RF], BF16),
            ("dbg_xnffn", [128, KC * RF], BF16), ("dbg_rtm", [128, 16], F32),
            ("dbg_ctm", [128, 16], F32), ("dbg_sgt", [128, 16], F32),
            ("dbg_vtm", [128, 512], BF16), ("dbg_wzr", [128, TA], BF16),
        ]:
            dbg[name] = nc.dram_tensor(name, shape, dt, kind="ExternalOutput").ap()

    from contextlib import ExitStack
    with ExitStack() as stk:
        e = stk.enter_context
        wqkv = e(nc.sbuf_tensor("wqkv_sb", [128, KC * 2048], BF16))
        wout = e(nc.sbuf_tensor("wout_sb", [128, KC * F], BF16))
        wvt = e(nc.sbuf_tensor("wvt_sb", [128, KC * F], BF16))
        wgt = e(nc.sbuf_tensor("wgt_sb", [128, KC * 8], BF16))
        wpt = e(nc.sbuf_tensor("wpt_sb", [128, KC * 64], BF16))
        w1t = e(nc.sbuf_tensor("w1t_sb", [128, KC * 256], BF16))
        w2t = e(nc.sbuf_tensor("w2t_sb", [128, 2 * F], BF16))
        b2sb = e(nc.sbuf_tensor("b2_sb", [8, F], BF16))
        smalls = e(nc.sbuf_tensor("smalls_sb", [128, 16], F32))
        hbias = e(nc.sbuf_tensor("hbias_sb", [128, 2], F32))
        grbc = e(nc.sbuf_tensor("grbc_sb", [128, 40], F32))
        alpsb = e(nc.sbuf_tensor("alpha_sb", [128, 1], F32))
        sigpp = e(nc.sbuf_tensor("sigpp_sb", [128, 1], F32))
        eyeb = e(nc.sbuf_tensor("eyeb_sb", [128, 128], BF16))
        eyef = e(nc.sbuf_tensor("eyef_sb", [128, 128], F32))
        blk8 = e(nc.sbuf_tensor("blk8_sb", [128, 128], BF16))
        ones8 = e(nc.sbuf_tensor("ones8_sb", [128, 8], BF16))
        onesc = e(nc.sbuf_tensor("onesc_sb", [8, 128], BF16))
        sel32 = e(nc.sbuf_tensor("sel32_sb", [8, 256], BF16))
        selp = e(nc.sbuf_tensor("selp_sb", [64, 8], BF16))
        reps = e(nc.sbuf_tensor("reps_sb", [128, 1024], BF16))

        NXA = 5
        xat = e(nc.sbuf_tensor("xat", [128, NXA * F], F32))
        xft = e(nc.sbuf_tensor("xft", [128, 4 * F], F32))
        xnat = e(nc.sbuf_tensor("xnat", [128, 2 * F], BF16))
        deadf = e(nc.sbuf_tensor("deadf", [128, F // 2], F32))
        deadb2 = deadf.ap().bitcast(BF16)[:, 0:F]
        bns = e(nc.sbuf_tensor("bns", [128, 8], F32))
        mv = e(nc.sbuf_tensor("mv", [128, 8], F32))
        lnv = e(nc.sbuf_tensor("lnv", [128, 6], F32))
        rstd = e(nc.sbuf_tensor("rstd", [128, 12], F32))
        nmr = e(nc.sbuf_tensor("nmr", [128, 12], F32))
        xnfm = e(nc.sbuf_tensor("xnfm", [128, KC * TA], BF16))
        xnffm = e(nc.sbuf_tensor("xnffm", [128, KC * RF], BF16))
        qcmp = [[[e(nc.sbuf_tensor(f"qcmp{p}{g}{h}", [128, TA], BF16))
                  for h in (0, 1)] for g in (0, 1)] for p in (0, 1)]
        kcmp = [[[e(nc.sbuf_tensor(f"kcmp{p}{g}{h}", [128, TA], BF16))
                  for h in (0, 1)] for g in (0, 1)] for p in (0, 1)]
        avcmp = [[[e(nc.sbuf_tensor(f"avcmp{p}{g}{h}", [128, TA], BF16))
                   for h in (0, 1)] for g in (0, 1)] for p in (0, 1)]
        wcmp = [[[e(nc.sbuf_tensor(f"wcmp{p}{g}{h}", [128, TA], BF16))
                  for h in (0, 1)] for g in (0, 1)] for p in (0, 1)]
        fvc = [e(nc.sbuf_tensor(f"fvc{p}", [128, KC * RF], BF16)) for p in (0, 1)]
        rsb = [e(nc.sbuf_tensor(f"rsb{p}", [8, RF], F32)) for p in (0, 1)]
        pv2 = [e(nc.sbuf_tensor(f"pv2{p}", [64, RF], BF16)) for p in (0, 1)]
        c2sb = [e(nc.sbuf_tensor(f"c2sb{p}", [8, RF], F32)) for p in (0, 1)]
        pcmp = [e(nc.sbuf_tensor(f"pcmp{h}", [128, TA], BF16)) for h in (0, 1)]
        bdk = [e(nc.sbuf_tensor(f"bdk{h}", [128, 1024], BF16)) for h in (0, 1)]
        bdv = [e(nc.sbuf_tensor(f"bdv{h}", [128, 1024], BF16)) for h in (0, 1)]
        vtm = [e(nc.sbuf_tensor(f"vtm{h}", [128, 512], BF16)) for h in (0, 1)]
        zrs = [e(nc.sbuf_tensor(f"zrs{h}", [8, 512], BF16)) for h in (0, 1)]
        wzr = [e(nc.sbuf_tensor(f"wzr{h}", [128, TA], BF16)) for h in (0, 1)]
        zw = [[e(nc.sbuf_tensor(f"zw{g}{h}", [128, TA], BF16)) for h in (0, 1)] for g in (0, 1)]
        rtm = e(nc.sbuf_tensor("rtm", [128, 16], F32))
        ctm = e(nc.sbuf_tensor("ctm", [128, 16], F32))
        sgt = e(nc.sbuf_tensor("sgt", [128, 16], F32))
        t1 = e(nc.sbuf_tensor("t1", [128, 16], F32))
        t2 = e(nc.sbuf_tensor("t2", [128, 16], F32))
        t3 = e(nc.sbuf_tensor("t3", [128, 16], F32))
        st2c = e(nc.sbuf_tensor("st2c", [128, 2], F32))
        st2b = e(nc.sbuf_tensor("st2b", [128, 2], F32))
        st2d = e(nc.sbuf_tensor("st2d", [128, 2], F32))
        st2cc = e(nc.sbuf_tensor("st2cc", [128, 2], F32))
        st2bc = e(nc.sbuf_tensor("st2bc", [128, 2], F32))
        wetm = e(nc.sbuf_tensor("wetm", [128, 16], F32))
        wef = e(nc.sbuf_tensor("wef", [8, RF], BF16))
        werep = e(nc.sbuf_tensor("werep", [128, 2 * RF], BF16))
        thsb = e(nc.sbuf_tensor("thsb", [128, 2 * RF], F32))
        hssb = e(nc.sbuf_tensor("hssb", [128, 2 * RF], BF16))
        outsb0 = e(nc.sbuf_tensor("outsb", [128, F], F32))
        outsb = [outsb0, outsb0]

        pbig = [e(nc.psum_tensor(f"pbig{i}", [128, 512], F32)) for i in (0, 1)]
        ptr = e(nc.psum_tensor("ptr", [128, 1024], BF16))
        psS = [e(nc.psum_tensor(f"psS{i}", [128, 512], F32)) for i in (0, 1)]
        psZ = [e(nc.psum_tensor(f"psZ{i}", [128, 512], F32)) for i in (0, 1)]
        pfin = e(nc.psum_tensor("pfin", [128, 512], F32))

        sems = {n: e(nc.semaphore(n)) for n in
                ["sT", "sV", "sS", "sY", "sG", "sDw", "sOB0"]
                + [f"sXA{i}" for i in range(NXA)] + [f"sXF{i}" for i in range(4)]
                + (["sDo"] if debug else [])}

        ops = []
        counts = {}
        tinyq = []

        def op(eng, fn, waits=(), inc=None, inc_amt=1):
            incs = []
            if inc is not None:
                counts[inc] = counts.get(inc, 0) + inc_amt
                incs.append((inc, inc_amt))
            if CAPTURE_OPS:
                import sys as _sys
                f = _sys._getframe(1)
                lines = []
                while f is not None and len(lines) < 3:
                    if f.f_code.co_filename.endswith("kernel.py"):
                        lines.append(f.f_lineno)
                    f = f.f_back
                fn.__op_lines__ = lines
            ops.append((eng, fn, [w for w in waits if w is not None and w[1] > 0], incs))

        def at(sem):
            return counts.get(sem, 0)

        def vtiny(fn, waits=(), mark=False):
            incs = []
            if mark:
                counts["sY"] = counts.get("sY", 0) + 1
                incs.append(("sY", 1))
            tinyq.append((fn, [w for w in waits if w is not None and w[1] > 0], incs))

        def vop(fn, waits=(), inc=None, inc_amt=1):
            op("vector", fn, waits, inc, inc_amt)
            if tinyq:
                fn2, w2, i2 = tinyq.pop(0)
                ops.append(("vector", fn2, w2, i2))

        def flush_tiny():
            while tinyq:
                fn2, w2, i2 = tinyq.pop(0)
                ops.append(("vector", (lambda e_: e_.drain()), [], []))
                ops.append(("vector", fn2, w2, i2))
            ops.append(("vector", (lambda e_: e_.drain()), [], []))

        def wdma(dst, src):
            op("gpsimd", lambda g_, d=dst, s=src: g_.dma_start(d, s), inc="sDw", inc_amt=16)

        for k in range(KC):
            wdma(wqkv[:, 2048 * k:2048 * (k + 1)], d_wqkv[128 * k:128 * (k + 1), :])
            wdma(wout[:, F * k:F * (k + 1)], d_wout[128 * k:128 * (k + 1), :])
            wdma(wvt[:, F * k:F * (k + 1)], d_wvt[128 * k:128 * (k + 1), :])
            wdma(wgt[:, 8 * k:8 * (k + 1)], d_wgt[128 * k:128 * (k + 1), :])
            wdma(wpt[:, 64 * k:64 * (k + 1)], d_wpt[128 * k:128 * (k + 1), :])
            wdma(w1t[:, 256 * k:256 * (k + 1)], d_w1t[128 * k:128 * (k + 1), :])
        for k in range(2):
            wdma(w2t[:, F * k:F * (k + 1)], d_w2t[128 * k:128 * (k + 1), :])
        for dst, src in [(b2sb, d_b2), (smalls, d_smalls), (grbc, d_grbc),
                         (alpsb, d_alpha), (eyeb, d_eyeb), (eyef, d_eyef),
                         (blk8, d_blk8), (ones8, d_ones8), (onesc, d_onesc),
                         (sel32, d_sel32), (selp, d_selp), (reps, d_reps)]:
            wdma(dst[:, :], src)
        w_init = at("sDw")

        op("scalar", lambda s_: s_.activation(sigpp[:, :], alpsb[:, :], AF.Tanh, scale=0.5),
           waits=[("sDw", w_init)], inc="sS")
        op("vector", lambda v_: v_.tensor_scalar(sigpp[:, :], sigpp[:, :], 0.5, 0.5,
                                                 op0=ALU.mult, op1=ALU.add),
           waits=[("sS", at("sS"))], inc="sV")
        op("vector", lambda v_: v_.tensor_scalar(hbias[:, :], smalls[:, 8:10], 0.5, None,
                                                 op0=ALU.mult),
           waits=[("sDw", w_init)], inc="sV")
        v_init = at("sV")

        xav = d_xattn.rearrange("(a u) f -> a u f", u=512)
        xfv = d_xffn.rearrange("g (a s) f -> g s a f", s=128)
        outv = d_out.rearrange("g (a s) f -> g s a f", s=128)
        xa_ready = [None] * NXA
        xf_ready = [None] * 4
        xa_free = [[] for _ in range(NXA)]
        xf_free = [[] for _ in range(4)]

        def dma_xa(tau, j):
            slot = (4 * tau + j) % NXA
            src = xav[:, UT * tau + 8 * j:UT * tau + 8 * j + 8, :]
            src = src.rearrange("a u f -> u a f")
            op("sync", lambda s_, d=xat[:, F * slot:F * (slot + 1)], s2=src: s_.dma_start(d, s2),
               waits=list(xa_free[slot]), inc=f"sXA{slot}", inc_amt=16)
            xa_ready[slot] = (f"sXA{slot}", at(f"sXA{slot}"))

        def dma_xf(tau, g):
            slot = (2 * tau + g) % 4
            src = xfv[g, 8 * tau:8 * tau + 8, :, :]
            op("sync", lambda s_, d=xft[:, F * slot:F * (slot + 1)], s2=src: s_.dma_start(d, s2),
               waits=list(xf_free[slot]), inc=f"sXF{slot}", inc_amt=16)
            xf_ready[slot] = (f"sXF{slot}", at(f"sXF{slot}"))

        router_state = {}
        bg_big = [[("sDw", w_init)], [("sDw", w_init)]]
        bg_S = [[], []]
        bd_guard = [[[], [], [], []], [[], [], [], []]]

        def bdslot(hh, s):
            buf = bdk[hh] if s < 2 else bdv[hh]
            off = 512 * (s % 2)
            return buf, off
        bg_Z = [[], []]
        pfin_guard = [("sDw", w_init)]
        bg_ptr = []
        sT_zrep_last = [0, 0]
        outsb_guard = [[]]
        prev_fin = [[]]
        par_free = [[], []]
        front_state = {}

        attn_stats_ready = [None]

        def attn_stats(tau):
            spar = tau % 2
            a_slots = [(4 * tau + j) % NXA for j in range(4)]
            for j, slot in enumerate(a_slots):
                w_ = [xa_ready[slot], ("sV", v_init)] + prev_fin[0] + list(outsb_guard[0])
                op("scalar", lambda s_, s2=xat[:, F * slot:F * slot + F], c=j: s_.activation(
                    outsb[0][:, :], s2, AF.Square, accum_out=mv[:, 2 + c:3 + c]),
                    waits=w_, inc="sS")
                op("vector", lambda v_, s2=xat[:, F * slot:F * slot + F], c=j: v_.tensor_scalar(
                    deadb2, s2, 1.0, 0.0, op0=ALU.mult, op1=ALU.add,
                    accum_out=bns[:, 2 + c:3 + c]),
                    waits=w_, inc="sV")
            op("scalar", lambda s_: s_.drain(), inc="sS")
            sS_st = at("sS")
            op("vector", lambda v_: v_.drain(), waits=[("sS", sS_st)])
            vtiny(lambda v_: v_.tensor_scalar(bns[:, 2:6], bns[:, 2:6], 1.0 / F, None,
                                              op0=ALU.mult))
            vtiny(lambda v_: v_.tensor_scalar(mv[:, 2:6], mv[:, 2:6], 1.0 / F, None,
                                              op0=ALU.mult))
            vtiny(lambda v_: v_.tensor_tensor(lnv[:, 2:6], bns[:, 2:6], bns[:, 2:6],
                                              op=ALU.mult))
            vtiny(lambda v_: v_.tensor_tensor(lnv[:, 2:6], mv[:, 2:6], lnv[:, 2:6],
                                              op=ALU.subtract))
            vtiny(lambda v_: v_.tensor_scalar(lnv[:, 2:6], lnv[:, 2:6], EPS, None,
                                              op0=ALU.add), mark=True)
            flush_tiny()
            op("scalar", lambda s_: s_.activation(lnv[:, 2:6], lnv[:, 2:6], AF.Ln),
               waits=[("sY", at("sY"))])
            op("scalar", lambda s_, o=rstd[:, 6 * spar:6 * spar + 4]: s_.activation(
                o, lnv[:, 2:6], AF.Exp, scale=-0.5), inc="sS")
            sS_r2 = at("sS")
            vtiny(lambda v_, o=nmr[:, 6 * spar:6 * spar + 4],
                  rs=rstd[:, 6 * spar:6 * spar + 4]: v_.scalar_tensor_tensor(
                o, bns[:, 2:6], -1.0, rs, op0=ALU.mult, op1=ALU.mult),
                waits=[("sS", sS_r2)], mark=True)
            flush_tiny()
            attn_stats_ready[0] = ("sY", at("sY"))
            prev_fin[0] = [("sY", at("sY"))]

        def front(tau):
            par = tau % 2
            a_slots = [(4 * tau + j) % NXA for j in range(4)]
            f_slots = [(2 * tau + g) % 4 for g in (0, 1)]
            for g in (0, 1):
                dma_xf(tau, g)
            ftiles = [("f", s, 4 + i) for i, s in enumerate(f_slots)]
            atiles = [("a", s, i) for i, s in enumerate(a_slots)]
            sT_tr_prev = [at("sT"), at("sT")]

            def apply_tiles(tiles, stats_w):
                for kind, slot, col in tiles:
                    buf = xat if kind == "a" else xft
                    src = buf[:, F * slot:F * (slot + 1)]
                    rp = col % 2
                    dstx = xnat[:, F * rp:F * (rp + 1)]
                    scol = 6 * par + col
                    w_ = list(stats_w) + [("sT", sT_tr_prev[rp])]
                    sS_h = [0, 0]
                    for half in (0, 1):
                        op("scalar", lambda s_, s2=src[:, 512 * half:512 * (half + 1)],
                           d=dstx[:, 512 * half:512 * (half + 1)], c=scol: s_.activation(
                            d, s2, AF.Identity, bias=nmr[:, c:c + 1], scale=rstd[:, c:c + 1]),
                            waits=w_ if half == 0 else [], inc="sS")
                        sS_h[half] = at("sS")
                    for k in range(KC):
                        w_ = []
                        if k == 0:
                            w_ = [("sS", sS_h[0])] + list(bg_ptr)
                        elif k == 4:
                            w_ = [("sS", sS_h[1])]
                        op("tensor", lambda t_, i_=dstx[:, 128 * k:128 * (k + 1)],
                           o_=ptr[:, 128 * k:128 * (k + 1)]: t_.transpose(o_, i_, eyeb[:, :]),
                           waits=w_, inc="sT" if k == KC - 1 else None)
                    sT_tr = at("sT")
                    sT_tr_prev[rp] = sT_tr
                    if kind == "a":
                        dfm = xnfm.ap().rearrange("p (k t) -> p k t", t=TA)[:, :, 128 * col:128 * (col + 1)]
                    else:
                        dfm = xnffm.ap().rearrange("p (k t) -> p k t", t=RF)[:, :, 128 * (col - 4):128 * (col - 3)]
                    sfm = ptr.ap().rearrange("p (k c) -> p k c", c=128)
                    if col % 2 == 0:
                        vop(lambda v_, s2=sfm, d=dfm: v_.tensor_copy(d, s2),
                            waits=[("sT", sT_tr)], inc="sV")
                        bg_ptr[:] = [("sV", at("sV"))]
                    else:
                        op("scalar", lambda s_, s2=sfm, d=dfm: s_.copy(d, s2),
                           waits=[("sT", sT_tr)], inc="sS")
                        bg_ptr[:] = [("sS", at("sS"))]
                return [("sV", at("sV")), ("sS", at("sS"))]

            afm_ready = apply_tiles(atiles, [attn_stats_ready[0]])
            for s in a_slots:
                xa_free[s] = list(afm_ready)
            if tau + 1 < NT:
                for j in range(4):
                    dma_xa(tau + 1, j)
            for kind, slot, col in ftiles:
                w_ = [xf_ready[slot], ("sV", v_init)] + prev_fin[0] + list(outsb_guard[0])
                op("scalar", lambda s_, s2=xft[:, F * slot:F * slot + F], c=col: s_.activation(
                    outsb[0][:, :], s2, AF.Square, accum_out=mv[:, c - 4:c - 3]),
                    waits=w_, inc="sS")
                op("vector", lambda v_, s2=xft[:, F * slot:F * slot + F], c=col: v_.tensor_scalar(
                    deadb2, s2, 1.0, 0.0, op0=ALU.mult, op1=ALU.add,
                    accum_out=bns[:, c - 4:c - 3]),
                    waits=w_, inc="sV")
            op("scalar", lambda s_: s_.drain(), inc="sS")
            sS_stats = at("sS")
            op("vector", lambda v_: v_.drain(), waits=[("sS", sS_stats)])
            vtiny(lambda v_: v_.tensor_scalar(bns[:, 0:2], bns[:, 0:2], 1.0 / F, None,
                                              op0=ALU.mult))
            vtiny(lambda v_: v_.tensor_scalar(mv[:, 0:2], mv[:, 0:2], 1.0 / F, None,
                                              op0=ALU.mult))
            vtiny(lambda v_: v_.tensor_tensor(lnv[:, 0:2], bns[:, 0:2], bns[:, 0:2],
                                              op=ALU.mult))
            vtiny(lambda v_: v_.tensor_tensor(lnv[:, 0:2], mv[:, 0:2], lnv[:, 0:2],
                                              op=ALU.subtract))
            vtiny(lambda v_: v_.tensor_scalar(lnv[:, 0:2], lnv[:, 0:2], EPS, None,
                                              op0=ALU.add), mark=True)
            flush_tiny()
            sY_l = at("sY")
            op("scalar", lambda s_: s_.activation(lnv[:, 0:2], lnv[:, 0:2], AF.Ln),
               waits=[("sY", sY_l)])
            op("scalar", lambda s_, o=rstd[:, 6 * par + 4:6 * par + 6]: s_.activation(
                o, lnv[:, 0:2], AF.Exp, scale=-0.5), inc="sS")
            sS_r = at("sS")
            vtiny(lambda v_, o=nmr[:, 6 * par + 4:6 * par + 6],
                  rs=rstd[:, 6 * par + 4:6 * par + 6]: v_.scalar_tensor_tensor(
                o, bns[:, 0:2], -1.0, rs, op0=ALU.mult, op1=ALU.mult),
                waits=[("sS", sS_r)], mark=True)
            flush_tiny()
            prev_fin[0] = [("sY", at("sY"))]
            if tau >= 1 and (tau - 1) not in router_state and (tau - 1) in front_state:
                router_front(tau - 1)

            cmp_ready = {}
            fi = 0
            for g in (0, 1):
                for j4 in range(4):
                    for hh in (0, 1):
                        bank = fi % 2
                        pb = pbig[bank]
                        for k in range(KC):
                            lhs = wqkv[:, 2048 * k + 1024 * g + 256 * j4 + 128 * hh:
                                       2048 * k + 1024 * g + 256 * j4 + 128 * (hh + 1)]
                            mov = xnfm[:, TA * k:TA * (k + 1)]
                            w_ = (list(afm_ready) + list(bg_big[bank]) + par_free[par]) \
                                if k == 0 else []
                            op("tensor", lambda t_, l=lhs, m=mov, k_=k, o=pb[:, :]: t_.matmul(
                                o, l, m, start=(k_ == 0), stop=(k_ == KC - 1)),
                                waits=w_, inc="sT" if k == KC - 1 else None)
                        sT_fill = at("sT")
                        arr = [qcmp, kcmp, avcmp, wcmp][j4][par][g][hh]
                        src_ap = pb[:, 0:512]
                        dst_ap = arr[:, :]
                        if fi % 2 == 0:
                            op("scalar", lambda s_, s2=src_ap, d=dst_ap: s_.copy(d, s2),
                               waits=[("sT", sT_fill)], inc="sS")
                            mark = ("sS", at("sS"))
                        else:
                            vop(lambda v_, s2=src_ap, d=dst_ap: v_.tensor_copy(d, s2),
                                waits=[("sT", sT_fill)], inc="sV")
                            mark = ("sV", at("sV"))
                        cmp_ready[(j4, g, hh)] = [mark]
                        bg_big[bank] = [mark]
                        fi += 1

            ffm_ready = apply_tiles(ftiles, [("sY", at("sY"))])
            for s in f_slots:
                xf_free[s] = list(ffm_ready)
            fm_ready = afm_ready + ffm_ready

            for dc in range(KC):
                bank = dc % 2
                pb = pbig[bank]
                for k in range(KC):
                    lhs = wvt[:, F * k + 128 * dc:F * k + 128 * (dc + 1)]
                    mov = xnffm[:, RF * k:RF * (k + 1)]
                    w_ = (list(ffm_ready) + list(bg_big[bank]) + par_free[par]) if k == 0 else []
                    op("tensor", lambda t_, l=lhs, m=mov, k_=k, o=pb[:, 0:RF]: t_.matmul(
                        o, l, m, start=(k_ == 0), stop=(k_ == KC - 1)),
                        waits=w_, inc="sT" if k == KC - 1 else None)
                sT_fill = at("sT")
                op("scalar", lambda s_, s2=pb[:, 0:RF], dc_=dc: s_.activation(
                    fvc[par][:, RF * dc_:RF * (dc_ + 1)], s2, AF.Identity,
                    bias=smalls[:, dc_:dc_ + 1]),
                    waits=[("sT", sT_fill)], inc="sS")
                bg_big[bank] = [("sS", at("sS"))]
            fv_ready = [("sS", at("sS"))]

            pb = pbig[0]
            for k in range(KC):
                w_ = (fv_ready + list(bg_big[0]) + par_free[par]) if k == 0 else []
                op("tensor", lambda t_, l=wgt[:, 8 * k:8 * (k + 1)],
                   m=fvc[par][:, RF * k:RF * (k + 1)], k_=k, o=pb[0:8, 0:RF]: t_.matmul(
                    o, l, m, start=(k_ == 0), stop=(k_ == KC - 1)),
                    waits=w_, inc="sT" if k == KC - 1 else None)
            vop(lambda v_, o=pb[0:8, 0:RF]: v_.tensor_copy(rsb[par][:, :], o),
                waits=[("sT", at("sT"))], inc="sV")
            bg_big[0] = [("sV", at("sV"))]
            pb = pbig[1]
            for k in range(KC):
                w_ = (fv_ready + list(bg_big[1]) + par_free[par]) if k == 0 else []
                op("tensor", lambda t_, l=wpt[:, 64 * k:64 * (k + 1)],
                   m=fvc[par][:, RF * k:RF * (k + 1)], k_=k, o=pb[0:64, 0:RF]: t_.matmul(
                    o, l, m, start=(k_ == 0), stop=(k_ == KC - 1)),
                    waits=w_, inc="sT" if k == KC - 1 else None)
            op("scalar", lambda s_, o=pb[0:64, 0:RF]: s_.activation(pv2[par][:, :], o, AF.Square),
               waits=[("sT", at("sT"))], inc="sS")
            sS_pv2 = at("sS")
            bg_big[1] = [("sS", sS_pv2)]
            pb = pbig[0]
            op("tensor", lambda t_, o=pb[0:8, 256:256 + RF]: t_.matmul(
                o, selp[:, :], pv2[par][:, :], start=True, stop=True),
               waits=[("sS", sS_pv2)] + list(bg_big[0]), inc="sT")
            vop(lambda v_, o=pb[0:8, 256:256 + RF]: v_.tensor_copy(c2sb[par][:, :], o),
                waits=[("sT", at("sT"))], inc="sV")
            bg_big[0] = [("sV", at("sV"))]

            front_state[tau] = dict(cmp_ready=cmp_ready, fv_ready=fv_ready,
                                    f_slots=f_slots, fm_ready=fm_ready)
            if tau + 1 < NT:
                attn_stats(tau + 1)

        def outsb_guard_xf(tau, slot):
            return []

        def router_front(rt):
            rpar = rt % 2
            trg = list(bg_S[0])
            for idx, srcb in enumerate((rsb[rpar], c2sb[rpar])):
                for w2 in range(2):
                    w_ = trg if (idx == 0 and w2 == 0) else []
                    op("tensor", lambda t_, i_=srcb[0:8, 128 * w2:128 * (w2 + 1)],
                       o_=psS[0][:, 8 * (2 * idx + w2):8 * (2 * idx + w2 + 1)]:
                       t_.transpose(o_, i_, eyef[0:8, 0:8]),
                       waits=w_, inc="sT" if (idx == 1 and w2 == 1) else None)
            sT_rt = at("sT")

            def re8f(ap_):
                return ap_.rearrange("p (w e) -> p w e", e=8)

            op("scalar", lambda s_: s_.copy(
                re8f(rtm.ap()), psS[0].ap().rearrange("p (w e) -> p w e", e=8)[:, 0:2, :]),
                waits=[("sT", sT_rt)], inc="sS")
            op("scalar", lambda s_: s_.copy(
                re8f(ctm.ap()), psS[0].ap().rearrange("p (w e) -> p w e", e=8)[:, 2:4, :]),
                inc="sS")
            bg_S[0] = [("sS", at("sS"))]
            sS_rc = at("sS")
            vtiny(lambda v_: v_.tensor_tensor(
                re8f(rtm.ap()), re8f(rtm.ap()),
                grbc[:, 32:40].unsqueeze(1).broadcast_to((128, 2, 8)), op=ALU.add),
                waits=[("sS", sS_rc)])
            vtiny(lambda v_: v_.tensor_scalar(ctm[:, :], ctm[:, :], 1e-30, None,
                                              op0=ALU.add), mark=True)
            sY_c = at("sY")
            sY_we_expected = sY_c + 3 + 1
            router_state[rt] = {"sY_c": sY_c, "sY_we_expected": sY_we_expected}

        def back(tau):
            par = tau % 2
            st = front_state.pop(tau)
            cmp_ready = st["cmp_ready"]
            fv_ready = st["fv_ready"]
            f_slots = st["f_slots"]

            rs = router_state.pop(tau)
            sY_we = rs["sY_we_expected"]

            def re8(ap_):
                return ap_.rearrange("p (w e) -> p w e", e=8)

            op("scalar", lambda s_: s_.activation(ctm[:, :], ctm[:, :], AF.Ln),
               waits=[("sY", rs["sY_c"])])
            op("scalar", lambda s_: s_.activation(ctm[:, :], ctm[:, :], AF.Exp, scale=0.5),
               inc="sS")
            sS_c = at("sS")

            def ln8_queue(srcb, mbuf, vbuf, waits=()):
                vtiny(lambda v_: v_.tensor_reduce(mbuf[:, :], re8(srcb.ap()),
                                                  axis=mybir.AxisListType.X, op=ALU.add),
                      waits=waits)
                vtiny(lambda v_: v_.tensor_scalar(mbuf[:, :], mbuf[:, :], 1.0 / 8, None,
                                                  op0=ALU.mult))
                vtiny(lambda v_: v_.tensor_tensor(t1[:, :], srcb[:, :], srcb[:, :],
                                                  op=ALU.mult))
                vtiny(lambda v_: v_.tensor_reduce(vbuf[:, :], re8(t1.ap()),
                                                  axis=mybir.AxisListType.X, op=ALU.add))
                vtiny(lambda v_: v_.tensor_scalar(vbuf[:, :], vbuf[:, :], 1.0 / 8, None,
                                                  op0=ALU.mult))
                vtiny(lambda v_: v_.tensor_tensor(st2d[:, :], mbuf[:, :], mbuf[:, :],
                                                  op=ALU.mult))
                vtiny(lambda v_: v_.tensor_tensor(vbuf[:, :], vbuf[:, :], st2d[:, :],
                                                  op=ALU.subtract))
                vtiny(lambda v_: v_.tensor_scalar(vbuf[:, :], vbuf[:, :], EPS, None,
                                                  op0=ALU.add), mark=True)

            vtiny(lambda v_: v_.tensor_scalar(ctm[:, :], ctm[:, :],
                                              float(1.0 / np.sqrt(8.0)), None, op0=ALU.mult),
                  waits=[("sS", sS_c)])
            ln8_queue(rtm, st2c, st2b)
            ln8_queue(ctm, st2cc, st2bc)
            sY_v = at("sY")

            zw_ready = [[], []]
            for g in (0, 1):
                pair_rdy = {}

                def emit_pair(kbp):
                    kb = 2 * kbp
                    slot = kb % 4
                    for hh in (0, 1):
                        in0 = kcmp[par][g][hh][:, 64 * kb:64 * (kb + 2)]
                        in0 = in0.rearrange("p (u a) -> p u a", a=16).unsqueeze(2)
                        in0 = in0.broadcast_to((128, 8, 8, 16))
                        in1 = blk8.ap().rearrange("p (h a) -> p h a", a=16).unsqueeze(1)
                        in1 = in1.broadcast_to((128, 8, 8, 16))
                        sbuf_, soff = bdslot(hh, slot)
                        dstb = sbuf_[:, 0:1024]
                        dstb = dstb.rearrange("p (u h a) -> p u h a", h=8, a=16)
                        w_ = (cmp_ready[(1, g, hh)] + bd_guard[hh][slot]
                              + bd_guard[hh][slot + 1])
                        if kbp == 0:
                            op("gpsimd", lambda g_, a_=dstb, b_=in0, c_=in1:
                               g_.tensor_tensor(a_, b_, c_, op=ALU.mult),
                               waits=w_, inc="sG")
                            pair_rdy[(kbp, hh)] = ("sG", at("sG"))
                        else:
                            vop(lambda v_, a_=dstb, b_=in0, c_=in1: v_.tensor_tensor(
                                a_, b_, c_, op=ALU.mult), waits=w_, inc="sV")
                            pair_rdy[(kbp, hh)] = ("sV", at("sV"))

                def emit_S(kb):
                    slot = kb % 4
                    for hh in (0, 1):
                        sbuf_, soff = bdslot(hh, slot)
                        for uu in range(4):
                            u = 4 * kb + uu
                            lhs = sbuf_[:, soff + 128 * uu:soff + 128 * (uu + 1)]
                            mov = qcmp[par][g][hh][:, 16 * u:16 * (u + 1)]
                            o = psS[hh][:, 16 * u:16 * (u + 1)]
                            w_ = []
                            if uu == 0:
                                w_ = cmp_ready[(0, g, hh)]
                                if kb % 2 == 0:
                                    w_ = w_ + [pair_rdy[(kb // 2, hh)]]
                                if kb == 0:
                                    w_ += bg_S[hh]
                            op("tensor", lambda t_, l=lhs, m=mov, o_=o: t_.matmul(
                                o_, l, m, start=True, stop=True), waits=w_,
                                inc="sT" if uu == 3 else None)
                        bd_guard[hh][slot] = [("sT", at("sT"))]

                emit_pair(0)
                emit_pair(1)
                emit_S(0)
                emit_S(1)
                emit_pair(2)
                emit_S(2)
                emit_S(3)
                emit_pair(3)
                for kb in range(4, 8):
                    emit_S(kb)
                sT_S = at("sT")
                for hh in (0, 1):
                    op("scalar", lambda s_, hh_=hh: s_.activation(
                        pcmp[hh_][:, :], psS[hh_][:, :], AF.Exp, scale=0.25),
                        waits=[("sT", sT_S)], inc="sS")
                sS_P = at("sS")
                for hh in (0, 1):
                    op("tensor", lambda t_, hh_=hh: t_.matmul(
                        psS[hh_][0:8, :], ones8[:, :], pcmp[hh_][:, :],
                        start=True, stop=True),
                        waits=[("sS", sS_P)] if hh == 0 else [], inc="sT")
                vtm_rdy = [0, 0]
                sT_v_m = [0, 0]
                for hh in (0, 1):
                    pzb = psZ[hh].ap().bitcast(BF16)
                    for ub in range(4):
                        w_ = (cmp_ready[(2, g, hh)] + list(bg_Z[hh])) if ub == 0 else []
                        op("tensor", lambda t_, i_=avcmp[par][g][hh][:, 128 * ub:128 * (ub + 1)],
                           o_=pzb[:, 128 * ub:128 * (ub + 1)]: t_.transpose(o_, i_, eyeb[:, :]),
                           waits=w_, inc="sT" if ub == 3 else None)
                    sT_v_m[hh] = at("sT")
                sT_Z = at("sT")
                for hh in (0, 1):
                    op("scalar", lambda s_, hh_=hh: s_.activation(
                        thsb[32 * hh_:32 * hh_ + 8, 0:512], psS[hh_][0:8, :], AF.Ln),
                        waits=[("sT", sT_Z), ("sT", sT_zrep_last[hh])])
                    op("scalar", lambda s_, hh_=hh: s_.activation(
                        zrs[hh_][:, :], thsb[32 * hh_:32 * hh_ + 8, 0:512], AF.Exp,
                        scale=-1.0), inc="sS")
                    sS_zr = at("sS")
                    op("tensor", lambda t_, hh_=hh: t_.matmul(
                        psS[hh_][:, :], onesc[:, :], zrs[hh_][:, :], start=True, stop=True),
                        waits=[("sS", sS_zr)], inc="sT")
                    sT_zrep_last[hh] = at("sT")
                    vop(lambda v_, hh_=hh, wc=wcmp[par][g][hh][:, :]: v_.scalar_tensor_tensor(
                        wzr[hh_][:, :], psS[hh_][:, :], 1.0, wc,
                        op0=ALU.mult, op1=ALU.mult),
                        waits=[("sT", sT_zrep_last[hh])] + cmp_ready[(3, g, hh)], inc="sV")
                    bg_S[hh] = [("sV", at("sV"))]
                sV_wzr = at("sV")
                for hh in (0, 1):
                    pzb = psZ[hh].ap().bitcast(BF16)
                    op("scalar", lambda s_, hh_=hh, s2=pzb[:, 0:512]: s_.copy(
                        vtm[hh_][:, :], s2), waits=[("sT", sT_v_m[hh])], inc="sS")
                    vtm_rdy[hh] = ("sS", at("sS"))
                    bg_Z[hh] = [("sS", at("sS"))]

                zg = [list(bg_Z[0]), list(bg_Z[1])]
                bank_guards = [list(pfin_guard), [("sV", sV_wzr)], [("sV", sV_wzr)]]
                bdv_mark = {}

                def emit_rep_bdv(hh, qb):
                    bidx = qb % 3
                    slot = qb % 4
                    zbuf_, boff = bdslot(hh, slot)
                    bank = [pfin, psS[0], psS[1]][bidx]
                    for uu in range(4):
                        u = 4 * qb + uu
                        ub, r8 = u // 8, u % 8
                        w_ = []
                        if uu == 0:
                            w_ = list(bank_guards[bidx])
                            if qb <= 1:
                                w_ += [vtm_rdy[hh]]
                        op("tensor", lambda t_, l=reps[:, 128 * r8:128 * (r8 + 1)],
                           m=vtm[hh][:, 128 * ub:128 * (ub + 1)],
                           o=bank[:, 128 * uu:128 * (uu + 1)]: t_.matmul(
                            o, l, m, start=True, stop=True),
                            waits=w_, inc="sT" if uu == 3 else None)
                    sT_rep = at("sT")
                    in1 = blk8.ap().unsqueeze(1).broadcast_to((128, 4, 128))
                    dstb = zbuf_[:, boff:boff + 512].rearrange("p (q c) -> p q c", c=128)
                    vop(lambda v_, d=dstb, m=in1, b=bank: v_.scalar_tensor_tensor(
                        d, b.ap().rearrange("p (q c) -> p q c", c=128), 1.0, m,
                        op0=ALU.mult, op1=ALU.mult),
                        waits=[("sT", sT_rep)] + bd_guard[hh][slot],
                        inc="sV")
                    sV_bdv = at("sV")
                    bdv_mark[(hh, qb)] = sV_bdv
                    bank_guards[bidx] = [("sV", sV_bdv)]
                    if bidx == 0:
                        pfin_guard[:] = [("sV", sV_bdv)]

                def emit_z(hh, qb):
                    slot = qb % 4
                    zbuf_, boff = bdslot(hh, slot)
                    for uu in range(4):
                        u = 4 * qb + uu
                        w_ = []
                        if uu == 0:
                            w_ = [("sV", bdv_mark[(hh, qb)])]
                            if qb == 0:
                                w_ += zg[hh]
                        op("tensor", lambda t_,
                           l=zbuf_[:, boff + 128 * uu:boff + 128 * (uu + 1)],
                           m=pcmp[hh][:, 16 * u:16 * (u + 1)],
                           o=psZ[hh][:, 128 * uu + 16 * qb:128 * uu + 16 * qb + 16]:
                           t_.matmul(o, l, m, start=True, stop=True),
                            waits=w_, inc="sT" if uu == 3 else None)
                    bd_guard[hh][slot] = [("sT", at("sT"))]

                for hh in (0, 1):
                    emit_rep_bdv(hh, 0)
                    emit_rep_bdv(hh, 1)
                    for qb in range(2, 8):
                        emit_rep_bdv(hh, qb)
                        emit_z(hh, qb - 2)
                    emit_z(hh, 6)
                    emit_z(hh, 7)
                    sT_z = at("sT")
                    for r4 in range(4):
                        srcz = psZ[hh][:, 128 * r4:128 * (r4 + 1)].rearrange(
                            "p (u4 a) -> p u4 a", a=16)
                        w1_ = wzr[hh].ap().rearrange(
                            "p (u4 r a) -> p u4 r a", r=4, a=16)[:, :, r4, :]
                        dstz = zw[g][hh][:, 128 * r4:128 * (r4 + 1)]
                        dstz = dstz.rearrange("p (u4 a) -> p u4 a", a=16)
                        w_ = [("sT", sT_z), ("sV", sV_wzr)] if r4 == 0 else []
                        vop(lambda v_, s2=srcz, d=dstz, m=w1_: v_.scalar_tensor_tensor(
                            d, s2, 1.0, m, op0=ALU.mult, op1=ALU.mult),
                            waits=w_, inc="sV" if r4 == 3 else None)
                    bg_Z[hh] = [("sV", at("sV"))]
                    if hh == 1:
                        bg_S[0] = list(bank_guards[1])
                        bg_S[1] = list(bank_guards[2])
                    zw_ready[g] = zw_ready[g] + [("sV", at("sV"))]

                if g == 0:
                    op("scalar", lambda s_: s_.activation(st2b[:, :], st2b[:, :], AF.Ln),
                       waits=[("sY", sY_v)])
                    op("scalar", lambda s_: s_.activation(st2bc[:, :], st2bc[:, :], AF.Ln))
                    op("scalar", lambda s_: s_.drain())
                    op("scalar", lambda s_: s_.activation(st2b[:, :], st2b[:, :], AF.Exp,
                                                          scale=-0.5))
                    op("scalar", lambda s_: s_.activation(st2bc[:, :], st2bc[:, :], AF.Exp,
                                                          scale=-0.5), inc="sS")
                    sS_rstd = at("sS")

                    def ln8_apply(dst_, srcb, mbuf, vbuf, gcol, bcol, waits=()):
                        vtiny(lambda v_: v_.tensor_tensor(
                            re8(dst_.ap()), re8(srcb.ap()),
                            mbuf[:, :].unsqueeze(2).broadcast_to((128, 2, 8)),
                            op=ALU.subtract), waits=waits)
                        vtiny(lambda v_: v_.tensor_tensor(
                            re8(dst_.ap()), re8(dst_.ap()),
                            vbuf[:, :].unsqueeze(2).broadcast_to((128, 2, 8)), op=ALU.mult))
                        vtiny(lambda v_: v_.tensor_tensor(
                            re8(dst_.ap()), re8(dst_.ap()),
                            grbc[:, gcol:gcol + 8].unsqueeze(1).broadcast_to((128, 2, 8)),
                            op=ALU.mult))
                        vtiny(lambda v_: v_.tensor_tensor(
                            re8(dst_.ap()), re8(dst_.ap()),
                            grbc[:, bcol:bcol + 8].unsqueeze(1).broadcast_to((128, 2, 8)),
                            op=ALU.add))

                    ln8_apply(sgt, rtm, st2c, st2b, 0, 8, waits=[("sS", sS_rstd)])
                    ln8_apply(t2, ctm, st2cc, st2bc, 16, 24)
                    vtiny(lambda v_: v_.scalar_tensor_tensor(
                        sgt[:, :], t2[:, :], sigpp[:, 0:1], sgt[:, :],
                        op0=ALU.mult, op1=ALU.add))
                    vtiny(lambda v_: v_.tensor_reduce(st2c[:, :], re8(sgt.ap()),
                                                      axis=mybir.AxisListType.X, op=ALU.max))
                    vtiny(lambda v_: v_.tensor_tensor(
                        re8(t1.ap()), re8(sgt.ap()),
                        st2c[:, :].unsqueeze(2).broadcast_to((128, 2, 8)), op=ALU.is_ge))
                    vtiny(lambda v_: v_.scalar_tensor_tensor(
                        t2[:, :], t1[:, :], -1e30, sgt[:, :], op0=ALU.mult, op1=ALU.add))
                    vtiny(lambda v_: v_.tensor_reduce(st2b[:, :], re8(t2.ap()),
                                                      axis=mybir.AxisListType.X, op=ALU.max))
                    vtiny(lambda v_: v_.tensor_tensor(
                        re8(t3.ap()), re8(t2.ap()),
                        st2b[:, :].unsqueeze(2).broadcast_to((128, 2, 8)), op=ALU.is_ge))
                    vtiny(lambda v_: v_.tensor_tensor(st2d[:, :], st2c[:, :], st2b[:, :],
                                                      op=ALU.subtract), mark=True)
                    sY_d = at("sY")
                else:
                    op("scalar", lambda s_: s_.activation(st2d[:, :], st2d[:, :], AF.Tanh,
                                                          scale=0.5),
                       waits=[("sY", sY_d)], inc="sS")
                    sS_w1 = at("sS")
                    vtiny(lambda v_: v_.tensor_scalar(st2d[:, :], st2d[:, :], 0.5, 0.5,
                                                      op0=ALU.mult, op1=ALU.add),
                          waits=[("sS", sS_w1)])
                    vtiny(lambda v_: v_.tensor_tensor(
                        re8(wetm.ap()), re8(t1.ap()),
                        st2d[:, :].unsqueeze(2).broadcast_to((128, 2, 8)), op=ALU.mult))
                    vtiny(lambda v_: v_.tensor_scalar(st2d[:, :], st2d[:, :], -1.0, 1.0,
                                                      op0=ALU.mult, op1=ALU.add))
                    vtiny(lambda v_: v_.tensor_tensor(
                        re8(t3.ap()), re8(t3.ap()),
                        st2d[:, :].unsqueeze(2).broadcast_to((128, 2, 8)), op=ALU.mult))
                    vtiny(lambda v_: v_.tensor_tensor(wetm[:, :], wetm[:, :], t3[:, :],
                                                      op=ALU.add), mark=True)
                    sY_we = at("sY")

                if debug and tau == 0 and g == 0:
                    def dump(name, src_ap, waits):
                        op("sync", lambda s_, d=dbg[name], s2=src_ap: s_.dma_start(d, s2),
                           waits=waits, inc="sDo", inc_amt=16)
                    dump("dbg_q", qcmp[0][0][0][:, :], cmp_ready[(0, 0, 0)])
                    dump("dbg_k", kcmp[0][0][0][:, :], cmp_ready[(1, 0, 0)])
                    dump("dbg_av", avcmp[0][0][0][:, :], cmp_ready[(2, 0, 0)])
                    dump("dbg_w", wcmp[0][0][0][:, :], cmp_ready[(3, 0, 0)])
                    dump("dbg_p", pcmp[0][:, :], [("sS", sS_P)])
                    dump("dbg_zw", zw[0][0][:, :], [("sV", at("sV"))])
                    dump("dbg_wzr", wzr[0][:, :], [("sV", sV_wzr)])
                    dump("dbg_vtm", vtm[0][:, :], [("sV", at("sV"))])

            flush_tiny()
            if debug and tau == 0:
                op("sync", lambda s_: s_.dma_start(dbg["dbg_we"], wetm[:, :]),
                   waits=[("sY", sY_we)], inc="sDo", inc_amt=16)
                op("sync", lambda s_: s_.dma_start(dbg["dbg_rtm"], rtm[:, :]),
                   waits=[("sY", sY_we)], inc="sDo", inc_amt=16)
                op("sync", lambda s_: s_.dma_start(dbg["dbg_ctm"], ctm[:, :]),
                   waits=[("sY", sY_we)], inc="sDo", inc_amt=16)
                op("sync", lambda s_: s_.dma_start(dbg["dbg_sgt"], sgt[:, :]),
                   waits=[("sY", sY_we)], inc="sDo", inc_amt=16)
            pr = psS[0]
            for w2 in range(2):
                w_ = ([("sY", sY_we)] + list(bg_S[0])) if w2 == 0 else []
                op("tensor", lambda t_, i_=wetm[:, 8 * w2:8 * (w2 + 1)],
                   o_=pr[0:8, 128 * w2:128 * (w2 + 1)]: t_.transpose(o_, i_, eyef[:, :]),
                   waits=w_, inc="sT" if w2 == 1 else None)
            sT_wef = at("sT")
            op("vector", lambda v_: v_.tensor_copy(wef[:, 0:128], pr[0:8, 0:128]),
               waits=[("sT", sT_wef)], inc="sV")
            op("vector", lambda v_: v_.tensor_copy(wef[:, 128:256], pr[0:8, 128:256]),
               inc="sV")
            sV_wef = at("sV")
            bg_S[0] = [("sV", sV_wef)]
            for ch in (0, 1):
                pb2 = psS[ch]
                w_ = [("sV", sV_wef)] + (list(bg_S[1]) if ch == 1 else list(bg_S[0]))
                op("tensor", lambda t_, l=sel32[:, 128 * ch:128 * (ch + 1)], b=pb2:
                   t_.matmul(b[:, 0:RF], l, wef[:, :], start=True, stop=True),
                   waits=w_, inc="sT")
                op("scalar", lambda s_, b=pb2, ch_=ch: s_.copy(
                    werep[:, RF * ch_:RF * (ch_ + 1)], b[:, 0:RF]),
                    waits=[("sT", at("sT"))], inc="sS")
                bg_S[ch] = [("sS", at("sS"))]
            sS_wrep = at("sS")
            for ch in (0, 1):
                pb2 = psS[ch]
                for k in range(KC):
                    lhs = w1t[:, 256 * k + 128 * ch:256 * k + 128 * (ch + 1)]
                    mov = fvc[par][:, RF * k:RF * (k + 1)]
                    w_ = (fv_ready + list(bg_S[ch])) if k == 0 else []
                    op("tensor", lambda t_, l=lhs, m=mov, b=pb2, k_=k: t_.matmul(
                        b[:, 0:RF], l, m, start=(k_ == 0), stop=(k_ == KC - 1)),
                        waits=w_, inc="sT" if k == KC - 1 else None)
                sT_h = at("sT")
                op("scalar", lambda s_, b=pb2, ch_=ch: s_.activation(
                    thsb[:, RF * ch_:RF * (ch_ + 1)], b[:, 0:RF], AF.Tanh, scale=0.5,
                    bias=hbias[:, ch_:ch_ + 1]), waits=[("sT", sT_h)], inc="sS")
                sS_th = at("sS")
                vtiny(lambda v_, ch_=ch: v_.tensor_scalar(
                    thsb[:, RF * ch_:RF * (ch_ + 1)], thsb[:, RF * ch_:RF * (ch_ + 1)],
                    0.5, 0.5, op0=ALU.mult, op1=ALU.add), waits=[("sS", sS_th)])
                vtiny(lambda v_, b=pb2, ch_=ch: v_.scalar_tensor_tensor(
                    thsb[:, RF * ch_:RF * (ch_ + 1)], b[:, 0:RF], smalls[:, 8 + ch_:9 + ch_],
                    thsb[:, RF * ch_:RF * (ch_ + 1)], op0=ALU.add, op1=ALU.mult))
                vtiny(lambda v_, ch_=ch: v_.tensor_tensor(
                    hssb[:, RF * ch_:RF * (ch_ + 1)], thsb[:, RF * ch_:RF * (ch_ + 1)],
                    werep[:, RF * ch_:RF * (ch_ + 1)], op=ALU.mult),
                    waits=[("sS", sS_wrep)], mark=True)
                flush_tiny()
                bg_S[ch] = [("sY", at("sY"))]
            sY_hs = at("sY")
            if debug and tau == 0:
                op("sync", lambda s_: s_.dma_start(dbg["dbg_hs"], hssb[:, :]),
                   waits=[("sY", sY_hs)], inc="sDo", inc_amt=16)
                op("sync", lambda s_: s_.dma_start(dbg["dbg_fv"], fvc[0][:, :]),
                   waits=fv_ready, inc="sDo", inc_amt=16)
                op("sync", lambda s_: s_.dma_start(dbg["dbg_xnfm"], xnfm[:, :]),
                   waits=st["fm_ready"], inc="sDo", inc_amt=16)
                op("sync", lambda s_: s_.dma_start(dbg["dbg_xnffn"], xnffm[:, :]),
                   waits=st["fm_ready"], inc="sDo", inc_amt=16)

            for g in (0, 1):
                for w2 in (0, 1):
                    first = True
                    w0_ = list(pfin_guard) + [("sY", sY_hs), ("sDw", w_init)] + zw_ready[g]
                    for r4 in range(4):
                        for hh in (0, 1):
                            lhs = zw[g][hh][:, 128 * r4:128 * (r4 + 1)]
                            mov = wout[:, F * (2 * r4 + hh) + 512 * w2:
                                       F * (2 * r4 + hh) + 512 * (w2 + 1)]
                            w_ = w0_ if first else []
                            op("tensor", lambda t_, l=lhs, m=mov, st_=first: t_.matmul(
                                pfin[:, :], l, m, start=st_, stop=False), waits=w_)
                            first = False
                    for ch in (0, 1):
                        lhs = hssb[:, RF * ch + 128 * g:RF * ch + 128 * (g + 1)]
                        mov = w2t[:, F * ch + 512 * w2:F * ch + 512 * (w2 + 1)]
                        op("tensor", lambda t_, l=lhs, m=mov: t_.matmul(
                            pfin[:, :], l, m, start=False, stop=False))
                    op("tensor", lambda t_, g_=g, w2_=w2: t_.matmul(
                        pfin[:, :], wef[:, 128 * g_:128 * (g_ + 1)],
                        b2sb[:, 512 * w2_:512 * (w2_ + 1)], start=False, stop=False))
                    for q in range(4):
                        dc = 4 * w2 + q
                        lhs = fvc[par][:, RF * dc + 128 * g:RF * dc + 128 * (g + 1)]
                        stop = (q == 3)
                        op("tensor", lambda t_, l=lhs, o=pfin[:, 128 * q:128 * (q + 1)],
                           sp=stop: t_.matmul(o, l, eyeb[:, :], start=False, stop=sp),
                           inc="sT" if stop else None)
                    sT_fin = at("sT")
                    slot = f_slots[g]
                    xsrc = xft[:, F * slot + 512 * w2:F * slot + 512 * (w2 + 1)]
                    vop(lambda v_, xs=xsrc, o=outsb[g][:, 512 * w2:512 * (w2 + 1)]:
                        v_.scalar_tensor_tensor(o, pfin[:, :], 1.0, xs,
                                                op0=ALU.mult, op1=ALU.add),
                        waits=[("sT", sT_fin)] + list(outsb_guard[0]), inc="sV")
                    outsb_guard[0] = []
                    pfin_guard[:] = [("sV", at("sV"))]
                out_mark = [("sV", at("sV"))]
                dstr = outv[g, 8 * tau:8 * tau + 8, :, :]
                op("sync", lambda s_, d=dstr, g_=g: s_.dma_start(d, outsb[g_][:, :]),
                   waits=out_mark, inc="sOB0", inc_amt=16)
                outsb_guard[0] = [("sOB0", at("sOB0"))]
                xf_free[f_slots[g]] = xf_free[f_slots[g]] + out_mark
            par_free[par] = [("sT", at("sT")), ("sV", at("sV")), ("sS", at("sS"))]

        for j in range(4):
            dma_xa(0, j)
        attn_stats(0)
        front(0)
        front(1)
        for tau in range(NT):
            if tau not in router_state:
                router_front(tau)
            back(tau)
            if tau + 2 < NT:
                front(tau + 2)

        total_out = at("sDo") if debug else 0

        global _captured_ops
        if CAPTURE_OPS:
            _captured_ops = list(ops)

        block = e(nc.Block())

        def emit_for(eng_name):
            def emitter(eng):
                for eng2, fn, waits, incs in ops:
                    if eng2 != eng_name:
                        continue
                    for sem_n, val in waits:
                        eng.wait_ge(sems[sem_n], val)
                    inst = fn(eng)
                    for sem_n, amt in incs:
                        inst.then_inc(sems[sem_n], amt)
                if eng_name == "sync":
                    if debug and total_out > 0:
                        eng.wait_ge(sems["sDo"], total_out)
                    eng.wait_ge(sems["sOB0"], counts.get("sOB0", 0))
            return emitter

        block.sync(emit_for("sync"))
        block.gpsimd(emit_for("gpsimd"))
        block.tensor(emit_for("tensor"))
        block.vector(emit_for("vector"))
        block.scalar(emit_for("scalar"))

    return nc



def _mk_consts():
    eyeb = np.eye(128, dtype=ml_dtypes.bfloat16)
    eyef = np.eye(128, dtype=np.float32)
    blk8 = np.zeros((128, 128), dtype=ml_dtypes.bfloat16)
    for p in range(128):
        blk8[p, (p // 16) * 16:(p // 16) * 16 + 16] = 1
    ones8 = np.zeros((128, 8), dtype=ml_dtypes.bfloat16)
    for p in range(128):
        ones8[p, p // 16] = 1
    onesc = np.zeros((8, 128), dtype=ml_dtypes.bfloat16)
    for hh in range(8):
        onesc[hh, 16 * hh:16 * (hh + 1)] = 1
    sel32 = np.zeros((8, 256), dtype=ml_dtypes.bfloat16)
    for ch in range(2):
        for p in range(128):
            sel32[4 * ch + p // 32, 128 * ch + p] = 1
    selp = np.zeros((64, 8), dtype=ml_dtypes.bfloat16)
    for ep in range(64):
        selp[ep, ep // 8] = 1
    reps = np.zeros((8, 128, 128), dtype=ml_dtypes.bfloat16)
    for r8 in range(8):
        for a2 in range(16):
            for hh in range(8):
                reps[r8, 16 * r8 + a2, 16 * hh + a2] = 1
    reps = reps.transpose(1, 0, 2).reshape(128, 1024)
    return eyeb, eyef, blk8, ones8, onesc, sel32, selp, reps


def _host_prep(inp):
    x = np.ascontiguousarray(np.asarray(inp["L0_attn_x"], dtype=np.float32))
    Wqkv = np.asarray(inp["L0_attn_w_qkvw"], dtype=np.float32)
    Wout = np.ascontiguousarray(np.asarray(inp["L0_attn_w_out"], dtype=np.float32))
    Wv = np.asarray(inp["L0_ffn_Wv"], dtype=np.float32)
    bv = np.asarray(inp["L0_ffn_bv"], dtype=np.float32)
    Wg = np.asarray(inp["L0_ffn_Wg"], dtype=np.float32)
    bg = np.asarray(inp["L0_ffn_bg"], dtype=np.float32)
    Wp = np.asarray(inp["L0_ffn_Wp"], dtype=np.float32)
    alpha = np.ascontiguousarray(np.broadcast_to(
        np.asarray(inp["L0_ffn_alpha"], dtype=np.float32).reshape(1, 1), (128, 1)))
    W1 = np.asarray(inp["L0_ffn_W1"], dtype=np.float32)
    b1 = np.asarray(inp["L0_ffn_b1"], dtype=np.float32)
    W2 = np.asarray(inp["L0_ffn_W2"], dtype=np.float32)
    b2 = np.ascontiguousarray(np.asarray(inp["L0_ffn_b2"], dtype=np.float32))

    Wq6 = Wqkv.reshape(1024, 4, 4, 2, 8, 16)
    wvt = np.ascontiguousarray(Wv.T)
    wgt = np.ascontiguousarray(Wg.T)
    wpt = np.ascontiguousarray(Wp.reshape(64, 1024).T)
    w1t = np.ascontiguousarray(W1.reshape(256, 1024).T)
    w2t = np.ascontiguousarray(W2.transpose(0, 2, 1).reshape(256, 1024))
    smalls = np.zeros((128, 16), dtype=np.float32)
    smalls[:, 0:8] = bv.reshape(8, 128).T
    smalls[:, 8:10] = b1.reshape(2, 128).T
    grbc = np.zeros((128, 40), dtype=np.float32)
    grbc[:, 0:8] = np.asarray(inp["L0_ffn_g_r"], dtype=np.float32)[None, :]
    grbc[:, 8:16] = np.asarray(inp["L0_ffn_b_r"], dtype=np.float32)[None, :]
    grbc[:, 16:24] = np.asarray(inp["L0_ffn_g_c"], dtype=np.float32)[None, :]
    grbc[:, 24:32] = np.asarray(inp["L0_ffn_b_c"], dtype=np.float32)[None, :]
    grbc[:, 32:40] = bg[None, :]
    eyeb, eyef, blk8, ones8, onesc, sel32, selp, reps = _mk_consts()

    in_maps = []
    for c in range(NCORES):
        b = c // 2
        i0, i1 = 2 * (c % 2), 2 * (c % 2) + 1
        x_attn = np.ascontiguousarray(x[b])
        x_ffn = np.ascontiguousarray(np.stack([x[i0, b * 2048:(b + 1) * 2048],
                                               x[i1, b * 2048:(b + 1) * 2048]]))
        wq = np.ascontiguousarray(
            np.stack([Wq6[:, i0], Wq6[:, i1]], axis=1).reshape(1024, 2048))
        in_maps.append({
            "x_attn": x_attn, "x_ffn": x_ffn, "wqkv": wq, "wout": Wout,
            "wvt": wvt, "wgt": wgt, "wpt": wpt, "w1t": w1t, "w2t": w2t,
            "b2": b2, "smalls": smalls, "grbc": grbc, "alpha": alpha,
            "eyeb": eyeb, "eyef": eyef, "blk8": blk8, "ones8": ones8,
            "onesc": onesc, "sel32": sel32, "selp": selp, "reps": reps,
        })
    return in_maps


def kernel(**inputs):
    in_maps = _host_prep(inputs)
    if "prog" not in _prog_cache:
        _prog_cache["prog"] = build_nc(debug=False)
    nc = _prog_cache["prog"]
    res = run_bass_kernel_spmd(nc, in_maps, core_ids=list(range(NCORES)))
    out = np.empty((4, 8192, 1024), dtype=np.float32)
    for c in range(NCORES):
        b = c // 2
        i0, i1 = 2 * (c % 2), 2 * (c % 2) + 1
        oc = res.results[c]["out"]
        out[i0, b * 2048:(b + 1) * 2048] = oc[0]
        out[i1, b * 2048:(b + 1) * 2048] = oc[1]
    return out


# version 34
# speedup vs baseline: 1.0307x; 1.0043x over previous
import numpy as np
import ml_dtypes
import concourse.bass as bass
import concourse.mybir as mybir
from concourse.bass_utils import run_bass_kernel_spmd

F32 = mybir.dt.float32
BF16 = mybir.dt.bfloat16
AF = mybir.ActivationFunctionType
ALU = mybir.AluOpType

NCORES = 8
F = 1024
KC = 8
UT = 32
NT = 512 // UT
TA = 16 * UT
RF = 256
EPS = 1e-5

_prog_cache = {}
CAPTURE_OPS = False
_captured_ops = None


def build_nc(debug=False):
    nc = bass.Bass("TRN2", target_bir_lowering=False)

    d_xattn = nc.dram_tensor("x_attn", [8192, F], F32, kind="ExternalInput").ap()
    d_xffn = nc.dram_tensor("x_ffn", [2, 2048, F], F32, kind="ExternalInput").ap()
    d_wqkv = nc.dram_tensor("wqkv", [F, 2048], F32, kind="ExternalInput").ap()
    d_wout = nc.dram_tensor("wout", [F, F], F32, kind="ExternalInput").ap()
    d_wvt = nc.dram_tensor("wvt", [F, F], F32, kind="ExternalInput").ap()
    d_wgt = nc.dram_tensor("wgt", [F, 8], F32, kind="ExternalInput").ap()
    d_wpt = nc.dram_tensor("wpt", [F, 64], F32, kind="ExternalInput").ap()
    d_w1t = nc.dram_tensor("w1t", [F, 256], F32, kind="ExternalInput").ap()
    d_w2t = nc.dram_tensor("w2t", [256, F], F32, kind="ExternalInput").ap()
    d_b2 = nc.dram_tensor("b2", [8, F], F32, kind="ExternalInput").ap()
    d_smalls = nc.dram_tensor("smalls", [128, 16], F32, kind="ExternalInput").ap()
    d_grbc = nc.dram_tensor("grbc", [128, 40], F32, kind="ExternalInput").ap()
    d_alpha = nc.dram_tensor("alpha", [128, 1], F32, kind="ExternalInput").ap()
    d_eyeb = nc.dram_tensor("eyeb", [128, 128], BF16, kind="ExternalInput").ap()
    d_eyef = nc.dram_tensor("eyef", [128, 128], F32, kind="ExternalInput").ap()
    d_blk8 = nc.dram_tensor("blk8", [128, 128], BF16, kind="ExternalInput").ap()
    d_ones8 = nc.dram_tensor("ones8", [128, 8], BF16, kind="ExternalInput").ap()
    d_onesc = nc.dram_tensor("onesc", [8, 128], BF16, kind="ExternalInput").ap()
    d_sel32 = nc.dram_tensor("sel32", [8, 256], BF16, kind="ExternalInput").ap()
    d_selp = nc.dram_tensor("selp", [64, 8], BF16, kind="ExternalInput").ap()
    d_reps = nc.dram_tensor("reps", [128, 1024], BF16, kind="ExternalInput").ap()
    d_out = nc.dram_tensor("out", [2, 2048, F], F32, kind="ExternalOutput").ap()

    dbg = {}
    if debug:
        for name, shape, dt in [
            ("dbg_xnfm", [128, KC * TA], BF16), ("dbg_q", [128, TA], BF16),
            ("dbg_k", [128, TA], BF16), ("dbg_av", [128, TA], BF16),
            ("dbg_w", [128, TA], BF16), ("dbg_p", [128, TA], BF16),
            ("dbg_zw", [128, TA], BF16), ("dbg_fv", [128, KC * RF], BF16),
            ("dbg_we", [128, 16], F32), ("dbg_hs", [128, 2 * RF], BF16),
            ("dbg_xnffn", [128, KC * RF], BF16), ("dbg_rtm", [128, 16], F32),
            ("dbg_ctm", [128, 16], F32), ("dbg_sgt", [128, 16], F32),
            ("dbg_vtm", [128, 512], BF16), ("dbg_wzr", [128, TA], BF16),
        ]:
            dbg[name] = nc.dram_tensor(name, shape, dt, kind="ExternalOutput").ap()

    from contextlib import ExitStack
    with ExitStack() as stk:
        e = stk.enter_context
        wqkv = e(nc.sbuf_tensor("wqkv_sb", [128, KC * 2048], BF16))
        wout = e(nc.sbuf_tensor("wout_sb", [128, KC * F], BF16))
        wvt = e(nc.sbuf_tensor("wvt_sb", [128, KC * F], BF16))
        wgt = e(nc.sbuf_tensor("wgt_sb", [128, KC * 8], BF16))
        wpt = e(nc.sbuf_tensor("wpt_sb", [128, KC * 64], BF16))
        w1t = e(nc.sbuf_tensor("w1t_sb", [128, KC * 256], BF16))
        w2t = e(nc.sbuf_tensor("w2t_sb", [128, 2 * F], BF16))
        b2sb = e(nc.sbuf_tensor("b2_sb", [8, F], BF16))
        smalls = e(nc.sbuf_tensor("smalls_sb", [128, 16], F32))
        hbias = e(nc.sbuf_tensor("hbias_sb", [128, 2], F32))
        grbc = e(nc.sbuf_tensor("grbc_sb", [128, 40], F32))
        alpsb = e(nc.sbuf_tensor("alpha_sb", [128, 1], F32))
        sigpp = e(nc.sbuf_tensor("sigpp_sb", [128, 1], F32))
        eyeb = e(nc.sbuf_tensor("eyeb_sb", [128, 128], BF16))
        eyef = e(nc.sbuf_tensor("eyef_sb", [128, 128], F32))
        blk8 = e(nc.sbuf_tensor("blk8_sb", [128, 128], BF16))
        ones8 = e(nc.sbuf_tensor("ones8_sb", [128, 8], BF16))
        onesc = e(nc.sbuf_tensor("onesc_sb", [8, 128], BF16))
        sel32 = e(nc.sbuf_tensor("sel32_sb", [8, 256], BF16))
        selp = e(nc.sbuf_tensor("selp_sb", [64, 8], BF16))
        reps = e(nc.sbuf_tensor("reps_sb", [128, 1024], BF16))

        NXA = 5
        xat = e(nc.sbuf_tensor("xat", [128, NXA * F], F32))
        xft = e(nc.sbuf_tensor("xft", [128, 4 * F], F32))
        xnat = e(nc.sbuf_tensor("xnat", [128, 2 * F], BF16))
        deadf = e(nc.sbuf_tensor("deadf", [128, F // 2], F32))
        bns = e(nc.sbuf_tensor("bns", [128, 8], F32))
        mv = e(nc.sbuf_tensor("mv", [128, 8], F32))
        lnv = e(nc.sbuf_tensor("lnv", [128, 6], F32))
        rstd = e(nc.sbuf_tensor("rstd", [128, 12], F32))
        nmr = e(nc.sbuf_tensor("nmr", [128, 12], F32))
        xnfm = e(nc.sbuf_tensor("xnfm", [128, KC * TA], BF16))
        xnffm = e(nc.sbuf_tensor("xnffm", [128, KC * RF], BF16))
        qcmp = [[[e(nc.sbuf_tensor(f"qcmp{p}{g}{h}", [128, TA], BF16))
                  for h in (0, 1)] for g in (0, 1)] for p in (0, 1)]
        kcmp = [[[e(nc.sbuf_tensor(f"kcmp{p}{g}{h}", [128, TA], BF16))
                  for h in (0, 1)] for g in (0, 1)] for p in (0, 1)]
        avcmp = [[[e(nc.sbuf_tensor(f"avcmp{p}{g}{h}", [128, TA], BF16))
                   for h in (0, 1)] for g in (0, 1)] for p in (0, 1)]
        wcmp = [[[e(nc.sbuf_tensor(f"wcmp{p}{g}{h}", [128, TA], BF16))
                  for h in (0, 1)] for g in (0, 1)] for p in (0, 1)]
        fvc = [e(nc.sbuf_tensor(f"fvc{p}", [128, KC * RF], BF16)) for p in (0, 1)]
        rsb = [e(nc.sbuf_tensor(f"rsb{p}", [8, RF], F32)) for p in (0, 1)]
        pv2 = [e(nc.sbuf_tensor(f"pv2{p}", [64, RF], BF16)) for p in (0, 1)]
        c2sb = [e(nc.sbuf_tensor(f"c2sb{p}", [8, RF], F32)) for p in (0, 1)]
        pcmp = [e(nc.sbuf_tensor(f"pcmp{h}", [128, TA], BF16)) for h in (0, 1)]
        bdk = [e(nc.sbuf_tensor(f"bdk{h}", [128, 1024], BF16)) for h in (0, 1)]
        bdv = [e(nc.sbuf_tensor(f"bdv{h}", [128, 1024], BF16)) for h in (0, 1)]
        vtm = [e(nc.sbuf_tensor(f"vtm{h}", [128, 512], BF16)) for h in (0, 1)]
        zrs = [e(nc.sbuf_tensor(f"zrs{h}", [8, 512], BF16)) for h in (0, 1)]
        wzr = [e(nc.sbuf_tensor(f"wzr{h}", [128, TA], BF16)) for h in (0, 1)]
        zw = [[e(nc.sbuf_tensor(f"zw{g}{h}", [128, TA], BF16)) for h in (0, 1)] for g in (0, 1)]
        rtm = e(nc.sbuf_tensor("rtm", [128, 16], F32))
        ctm = e(nc.sbuf_tensor("ctm", [128, 16], F32))
        sgt = e(nc.sbuf_tensor("sgt", [128, 16], F32))
        t1 = e(nc.sbuf_tensor("t1", [128, 16], F32))
        t2 = e(nc.sbuf_tensor("t2", [128, 16], F32))
        t3 = e(nc.sbuf_tensor("t3", [128, 16], F32))
        st2c = e(nc.sbuf_tensor("st2c", [128, 2], F32))
        st2b = e(nc.sbuf_tensor("st2b", [128, 2], F32))
        st2d = e(nc.sbuf_tensor("st2d", [128, 2], F32))
        st2cc = e(nc.sbuf_tensor("st2cc", [128, 2], F32))
        st2bc = e(nc.sbuf_tensor("st2bc", [128, 2], F32))
        wetm = e(nc.sbuf_tensor("wetm", [128, 16], F32))
        wef = e(nc.sbuf_tensor("wef", [8, RF], BF16))
        werep = e(nc.sbuf_tensor("werep", [128, 2 * RF], BF16))
        thsb = e(nc.sbuf_tensor("thsb", [128, 2 * RF], F32))
        hssb = e(nc.sbuf_tensor("hssb", [128, 2 * RF], BF16))
        outsb0 = e(nc.sbuf_tensor("outsb", [128, F], F32))
        outsb = [outsb0, outsb0]

        pbig = [e(nc.psum_tensor(f"pbig{i}", [128, 512], F32)) for i in (0, 1)]
        ptr = e(nc.psum_tensor("ptr", [128, 1024], BF16))
        psS = [e(nc.psum_tensor(f"psS{i}", [128, 512], F32)) for i in (0, 1)]
        psZ = [e(nc.psum_tensor(f"psZ{i}", [128, 512], F32)) for i in (0, 1)]
        pfin = e(nc.psum_tensor("pfin", [128, 512], F32))

        sems = {n: e(nc.semaphore(n)) for n in
                ["sT", "sV", "sS", "sY", "sG", "sDw", "sOB0"]
                + [f"sXA{i}" for i in range(NXA)] + [f"sXF{i}" for i in range(4)]
                + (["sDo"] if debug else [])}

        ops = []
        counts = {}
        tinyq = []

        def op(eng, fn, waits=(), inc=None, inc_amt=1):
            incs = []
            if inc is not None:
                counts[inc] = counts.get(inc, 0) + inc_amt
                incs.append((inc, inc_amt))
            if CAPTURE_OPS:
                import sys as _sys
                f = _sys._getframe(1)
                lines = []
                while f is not None and len(lines) < 3:
                    if f.f_code.co_filename.endswith("kernel.py"):
                        lines.append(f.f_lineno)
                    f = f.f_back
                fn.__op_lines__ = lines
            ops.append((eng, fn, [w for w in waits if w is not None and w[1] > 0], incs))

        def at(sem):
            return counts.get(sem, 0)

        def vtiny(fn, waits=(), mark=False):
            incs = []
            if mark:
                counts["sY"] = counts.get("sY", 0) + 1
                incs.append(("sY", 1))
            tinyq.append((fn, [w for w in waits if w is not None and w[1] > 0], incs))

        def vop(fn, waits=(), inc=None, inc_amt=1):
            op("vector", fn, waits, inc, inc_amt)
            if tinyq:
                fn2, w2, i2 = tinyq.pop(0)
                ops.append(("vector", fn2, w2, i2))

        def flush_tiny():
            while tinyq:
                fn2, w2, i2 = tinyq.pop(0)
                ops.append(("vector", (lambda e_: e_.drain()), [], []))
                ops.append(("vector", fn2, w2, i2))
            ops.append(("vector", (lambda e_: e_.drain()), [], []))

        def wdma(dst, src):
            op("gpsimd", lambda g_, d=dst, s=src: g_.dma_start(d, s), inc="sDw", inc_amt=16)

        for k in range(KC):
            wdma(wqkv[:, 2048 * k:2048 * (k + 1)], d_wqkv[128 * k:128 * (k + 1), :])
            wdma(wout[:, F * k:F * (k + 1)], d_wout[128 * k:128 * (k + 1), :])
            wdma(wvt[:, F * k:F * (k + 1)], d_wvt[128 * k:128 * (k + 1), :])
            wdma(wgt[:, 8 * k:8 * (k + 1)], d_wgt[128 * k:128 * (k + 1), :])
            wdma(wpt[:, 64 * k:64 * (k + 1)], d_wpt[128 * k:128 * (k + 1), :])
            wdma(w1t[:, 256 * k:256 * (k + 1)], d_w1t[128 * k:128 * (k + 1), :])
        for k in range(2):
            wdma(w2t[:, F * k:F * (k + 1)], d_w2t[128 * k:128 * (k + 1), :])
        for dst, src in [(b2sb, d_b2), (smalls, d_smalls), (grbc, d_grbc),
                         (alpsb, d_alpha), (eyeb, d_eyeb), (eyef, d_eyef),
                         (blk8, d_blk8), (ones8, d_ones8), (onesc, d_onesc),
                         (sel32, d_sel32), (selp, d_selp), (reps, d_reps)]:
            wdma(dst[:, :], src)
        w_init = at("sDw")

        op("scalar", lambda s_: s_.activation(sigpp[:, :], alpsb[:, :], AF.Tanh, scale=0.5),
           waits=[("sDw", w_init)], inc="sS")
        op("vector", lambda v_: v_.tensor_scalar(sigpp[:, :], sigpp[:, :], 0.5, 0.5,
                                                 op0=ALU.mult, op1=ALU.add),
           waits=[("sS", at("sS"))], inc="sV")
        op("vector", lambda v_: v_.tensor_scalar(hbias[:, :], smalls[:, 8:10], 0.5, None,
                                                 op0=ALU.mult),
           waits=[("sDw", w_init)], inc="sV")
        v_init = at("sV")

        xav = d_xattn.rearrange("(a u) f -> a u f", u=512)
        xfv = d_xffn.rearrange("g (a s) f -> g s a f", s=128)
        outv = d_out.rearrange("g (a s) f -> g s a f", s=128)
        xa_ready = [None] * NXA
        xf_ready = [None] * 4
        xa_free = [[] for _ in range(NXA)]
        xf_free = [[] for _ in range(4)]

        def dma_xa(tau, j):
            slot = (4 * tau + j) % NXA
            src = xav[:, UT * tau + 8 * j:UT * tau + 8 * j + 8, :]
            src = src.rearrange("a u f -> u a f")
            op("sync", lambda s_, d=xat[:, F * slot:F * (slot + 1)], s2=src: s_.dma_start(d, s2),
               waits=list(xa_free[slot]), inc=f"sXA{slot}", inc_amt=16)
            xa_ready[slot] = (f"sXA{slot}", at(f"sXA{slot}"))

        def dma_xf(tau, g):
            slot = (2 * tau + g) % 4
            src = xfv[g, 8 * tau:8 * tau + 8, :, :]
            op("sync", lambda s_, d=xft[:, F * slot:F * (slot + 1)], s2=src: s_.dma_start(d, s2),
               waits=list(xf_free[slot]), inc=f"sXF{slot}", inc_amt=16)
            xf_ready[slot] = (f"sXF{slot}", at(f"sXF{slot}"))

        router_state = {}
        bg_big = [[("sDw", w_init)], [("sDw", w_init)]]
        bg_S = [[], []]
        bd_guard = [[[], [], [], []], [[], [], [], []]]

        def bdslot(hh, s):
            buf = bdk[hh] if s < 2 else bdv[hh]
            off = 512 * (s % 2)
            return buf, off
        bg_Z = [[], []]
        pfin_guard = [("sDw", w_init)]
        bg_ptr = []
        sT_zrep_last = [0, 0]
        outsb_guard = [[]]
        prev_fin = [[]]
        par_free = [[], []]
        front_state = {}

        attn_stats_ready = [None]

        def attn_stats(tau):
            spar = tau % 2
            a_slots = [(4 * tau + j) % NXA for j in range(4)]
            for j, slot in enumerate(a_slots):
                w_ = [xa_ready[slot], ("sV", v_init)] + prev_fin[0] + list(outsb_guard[0])
                op("scalar", lambda s_, s2=xat[:, F * slot:F * slot + F], c=j: s_.activation(
                    outsb[0][:, :], s2, AF.Square, accum_out=mv[:, 2 + c:3 + c]),
                    waits=w_, inc="sS")
                op("vector", lambda v_, s2=xat[:, F * slot:F * slot + F], c=j: v_.tensor_scalar(
                    outsb[0][:, :], s2, 1.0, 0.0, op0=ALU.mult, op1=ALU.add,
                    accum_out=bns[:, 2 + c:3 + c]),
                    waits=w_, inc="sV")
            op("scalar", lambda s_: s_.drain(), inc="sS")
            sS_st = at("sS")
            op("vector", lambda v_: v_.drain(), waits=[("sS", sS_st)])
            vtiny(lambda v_: v_.tensor_scalar(bns[:, 2:6], bns[:, 2:6], 1.0 / F, None,
                                              op0=ALU.mult))
            vtiny(lambda v_: v_.tensor_scalar(mv[:, 2:6], mv[:, 2:6], 1.0 / F, None,
                                              op0=ALU.mult))
            vtiny(lambda v_: v_.tensor_tensor(lnv[:, 2:6], bns[:, 2:6], bns[:, 2:6],
                                              op=ALU.mult))
            vtiny(lambda v_: v_.tensor_tensor(lnv[:, 2:6], mv[:, 2:6], lnv[:, 2:6],
                                              op=ALU.subtract))
            vtiny(lambda v_: v_.tensor_scalar(lnv[:, 2:6], lnv[:, 2:6], EPS, None,
                                              op0=ALU.add), mark=True)
            flush_tiny()
            op("scalar", lambda s_: s_.activation(lnv[:, 2:6], lnv[:, 2:6], AF.Ln),
               waits=[("sY", at("sY"))])
            op("scalar", lambda s_, o=rstd[:, 6 * spar:6 * spar + 4]: s_.activation(
                o, lnv[:, 2:6], AF.Exp, scale=-0.5), inc="sS")
            sS_r2 = at("sS")
            vtiny(lambda v_, o=nmr[:, 6 * spar:6 * spar + 4],
                  rs=rstd[:, 6 * spar:6 * spar + 4]: v_.scalar_tensor_tensor(
                o, bns[:, 2:6], -1.0, rs, op0=ALU.mult, op1=ALU.mult),
                waits=[("sS", sS_r2)], mark=True)
            flush_tiny()
            attn_stats_ready[0] = ("sY", at("sY"))
            prev_fin[0] = [("sY", at("sY"))]

        def front(tau):
            par = tau % 2
            a_slots = [(4 * tau + j) % NXA for j in range(4)]
            f_slots = [(2 * tau + g) % 4 for g in (0, 1)]
            for g in (0, 1):
                dma_xf(tau, g)
            ftiles = [("f", s, 4 + i) for i, s in enumerate(f_slots)]
            atiles = [("a", s, i) for i, s in enumerate(a_slots)]
            sT_tr_prev = [at("sT"), at("sT")]

            def apply_tiles(tiles, stats_w):
                for kind, slot, col in tiles:
                    buf = xat if kind == "a" else xft
                    src = buf[:, F * slot:F * (slot + 1)]
                    rp = col % 2
                    dstx = xnat[:, F * rp:F * (rp + 1)]
                    scol = 6 * par + col
                    w_ = list(stats_w) + [("sT", sT_tr_prev[rp])]
                    sS_h = [0, 0]
                    for half in (0, 1):
                        op("scalar", lambda s_, s2=src[:, 512 * half:512 * (half + 1)],
                           d=dstx[:, 512 * half:512 * (half + 1)], c=scol: s_.activation(
                            d, s2, AF.Identity, bias=nmr[:, c:c + 1], scale=rstd[:, c:c + 1]),
                            waits=w_ if half == 0 else [], inc="sS")
                        sS_h[half] = at("sS")
                    for k in range(KC):
                        w_ = []
                        if k == 0:
                            w_ = [("sS", sS_h[0])] + list(bg_ptr)
                        elif k == 4:
                            w_ = [("sS", sS_h[1])]
                        op("tensor", lambda t_, i_=dstx[:, 128 * k:128 * (k + 1)],
                           o_=ptr[:, 128 * k:128 * (k + 1)]: t_.transpose(o_, i_, eyeb[:, :]),
                           waits=w_, inc="sT" if k == KC - 1 else None)
                    sT_tr = at("sT")
                    sT_tr_prev[rp] = sT_tr
                    if kind == "a":
                        dfm = xnfm.ap().rearrange("p (k t) -> p k t", t=TA)[:, :, 128 * col:128 * (col + 1)]
                    else:
                        dfm = xnffm.ap().rearrange("p (k t) -> p k t", t=RF)[:, :, 128 * (col - 4):128 * (col - 3)]
                    sfm = ptr.ap().rearrange("p (k c) -> p k c", c=128)
                    if col % 2 == 0:
                        vop(lambda v_, s2=sfm, d=dfm: v_.tensor_copy(d, s2),
                            waits=[("sT", sT_tr)], inc="sV")
                        bg_ptr[:] = [("sV", at("sV"))]
                    else:
                        op("scalar", lambda s_, s2=sfm, d=dfm: s_.copy(d, s2),
                           waits=[("sT", sT_tr)], inc="sS")
                        bg_ptr[:] = [("sS", at("sS"))]
                return [("sV", at("sV")), ("sS", at("sS"))]

            afm_ready = apply_tiles(atiles, [attn_stats_ready[0]])
            for s in a_slots:
                xa_free[s] = list(afm_ready)
            if tau + 1 < NT:
                for j in range(4):
                    dma_xa(tau + 1, j)
            for kind, slot, col in ftiles:
                w_ = [xf_ready[slot], ("sV", v_init)] + prev_fin[0] + list(outsb_guard[0])
                op("scalar", lambda s_, s2=xft[:, F * slot:F * slot + F], c=col: s_.activation(
                    outsb[0][:, :], s2, AF.Square, accum_out=mv[:, c - 4:c - 3]),
                    waits=w_, inc="sS")
                op("vector", lambda v_, s2=xft[:, F * slot:F * slot + F], c=col: v_.tensor_scalar(
                    outsb[0][:, :], s2, 1.0, 0.0, op0=ALU.mult, op1=ALU.add,
                    accum_out=bns[:, c - 4:c - 3]),
                    waits=w_, inc="sV")
            op("scalar", lambda s_: s_.drain(), inc="sS")
            sS_stats = at("sS")
            op("vector", lambda v_: v_.drain(), waits=[("sS", sS_stats)])
            vtiny(lambda v_: v_.tensor_scalar(bns[:, 0:2], bns[:, 0:2], 1.0 / F, None,
                                              op0=ALU.mult))
            vtiny(lambda v_: v_.tensor_scalar(mv[:, 0:2], mv[:, 0:2], 1.0 / F, None,
                                              op0=ALU.mult))
            vtiny(lambda v_: v_.tensor_tensor(lnv[:, 0:2], bns[:, 0:2], bns[:, 0:2],
                                              op=ALU.mult))
            vtiny(lambda v_: v_.tensor_tensor(lnv[:, 0:2], mv[:, 0:2], lnv[:, 0:2],
                                              op=ALU.subtract))
            vtiny(lambda v_: v_.tensor_scalar(lnv[:, 0:2], lnv[:, 0:2], EPS, None,
                                              op0=ALU.add), mark=True)
            flush_tiny()
            sY_l = at("sY")
            op("scalar", lambda s_: s_.activation(lnv[:, 0:2], lnv[:, 0:2], AF.Ln),
               waits=[("sY", sY_l)])
            op("scalar", lambda s_, o=rstd[:, 6 * par + 4:6 * par + 6]: s_.activation(
                o, lnv[:, 0:2], AF.Exp, scale=-0.5), inc="sS")
            sS_r = at("sS")
            vtiny(lambda v_, o=nmr[:, 6 * par + 4:6 * par + 6],
                  rs=rstd[:, 6 * par + 4:6 * par + 6]: v_.scalar_tensor_tensor(
                o, bns[:, 0:2], -1.0, rs, op0=ALU.mult, op1=ALU.mult),
                waits=[("sS", sS_r)], mark=True)
            flush_tiny()
            prev_fin[0] = [("sY", at("sY"))]
            if tau >= 1 and (tau - 1) not in router_state and (tau - 1) in front_state:
                router_front(tau - 1)

            cmp_ready = {}
            fi = 0
            for g in (0, 1):
                for j4 in range(4):
                    for hh in (0, 1):
                        bank = fi % 2
                        pb = pbig[bank]
                        for k in range(KC):
                            lhs = wqkv[:, 2048 * k + 1024 * g + 256 * j4 + 128 * hh:
                                       2048 * k + 1024 * g + 256 * j4 + 128 * (hh + 1)]
                            mov = xnfm[:, TA * k:TA * (k + 1)]
                            w_ = (list(afm_ready) + list(bg_big[bank]) + par_free[par]) \
                                if k == 0 else []
                            op("tensor", lambda t_, l=lhs, m=mov, k_=k, o=pb[:, :]: t_.matmul(
                                o, l, m, start=(k_ == 0), stop=(k_ == KC - 1)),
                                waits=w_, inc="sT" if k == KC - 1 else None)
                        sT_fill = at("sT")
                        arr = [qcmp, kcmp, avcmp, wcmp][j4][par][g][hh]
                        src_ap = pb[:, 0:512]
                        dst_ap = arr[:, :]
                        if fi % 2 == 0:
                            op("scalar", lambda s_, s2=src_ap, d=dst_ap: s_.copy(d, s2),
                               waits=[("sT", sT_fill)], inc="sS")
                            mark = ("sS", at("sS"))
                        else:
                            vop(lambda v_, s2=src_ap, d=dst_ap: v_.tensor_copy(d, s2),
                                waits=[("sT", sT_fill)], inc="sV")
                            mark = ("sV", at("sV"))
                        cmp_ready[(j4, g, hh)] = [mark]
                        bg_big[bank] = [mark]
                        fi += 1

            ffm_ready = apply_tiles(ftiles, [("sY", at("sY"))])
            for s in f_slots:
                xf_free[s] = list(ffm_ready)
            fm_ready = afm_ready + ffm_ready

            for dc in range(KC):
                bank = dc % 2
                pb = pbig[bank]
                for k in range(KC):
                    lhs = wvt[:, F * k + 128 * dc:F * k + 128 * (dc + 1)]
                    mov = xnffm[:, RF * k:RF * (k + 1)]
                    w_ = (list(ffm_ready) + list(bg_big[bank]) + par_free[par]) if k == 0 else []
                    op("tensor", lambda t_, l=lhs, m=mov, k_=k, o=pb[:, 0:RF]: t_.matmul(
                        o, l, m, start=(k_ == 0), stop=(k_ == KC - 1)),
                        waits=w_, inc="sT" if k == KC - 1 else None)
                sT_fill = at("sT")
                op("scalar", lambda s_, s2=pb[:, 0:RF], dc_=dc: s_.activation(
                    fvc[par][:, RF * dc_:RF * (dc_ + 1)], s2, AF.Identity,
                    bias=smalls[:, dc_:dc_ + 1]),
                    waits=[("sT", sT_fill)], inc="sS")
                bg_big[bank] = [("sS", at("sS"))]
            fv_ready = [("sS", at("sS"))]

            pb = pbig[0]
            for k in range(KC):
                w_ = (fv_ready + list(bg_big[0]) + par_free[par]) if k == 0 else []
                op("tensor", lambda t_, l=wgt[:, 8 * k:8 * (k + 1)],
                   m=fvc[par][:, RF * k:RF * (k + 1)], k_=k, o=pb[0:8, 0:RF]: t_.matmul(
                    o, l, m, start=(k_ == 0), stop=(k_ == KC - 1)),
                    waits=w_, inc="sT" if k == KC - 1 else None)
            vop(lambda v_, o=pb[0:8, 0:RF]: v_.tensor_copy(rsb[par][:, :], o),
                waits=[("sT", at("sT"))], inc="sV")
            bg_big[0] = [("sV", at("sV"))]
            pb = pbig[1]
            for k in range(KC):
                w_ = (fv_ready + list(bg_big[1]) + par_free[par]) if k == 0 else []
                op("tensor", lambda t_, l=wpt[:, 64 * k:64 * (k + 1)],
                   m=fvc[par][:, RF * k:RF * (k + 1)], k_=k, o=pb[0:64, 0:RF]: t_.matmul(
                    o, l, m, start=(k_ == 0), stop=(k_ == KC - 1)),
                    waits=w_, inc="sT" if k == KC - 1 else None)
            op("scalar", lambda s_, o=pb[0:64, 0:RF]: s_.activation(pv2[par][:, :], o, AF.Square),
               waits=[("sT", at("sT"))], inc="sS")
            sS_pv2 = at("sS")
            bg_big[1] = [("sS", sS_pv2)]
            pb = pbig[0]
            op("tensor", lambda t_, o=pb[0:8, 256:256 + RF]: t_.matmul(
                o, selp[:, :], pv2[par][:, :], start=True, stop=True),
               waits=[("sS", sS_pv2)] + list(bg_big[0]), inc="sT")
            vop(lambda v_, o=pb[0:8, 256:256 + RF]: v_.tensor_copy(c2sb[par][:, :], o),
                waits=[("sT", at("sT"))], inc="sV")
            bg_big[0] = [("sV", at("sV"))]

            front_state[tau] = dict(cmp_ready=cmp_ready, fv_ready=fv_ready,
                                    f_slots=f_slots, fm_ready=fm_ready)
            if tau + 1 < NT:
                attn_stats(tau + 1)

        def outsb_guard_xf(tau, slot):
            return []

        def router_front(rt):
            rpar = rt % 2
            trg = list(bg_S[0])
            for idx, srcb in enumerate((rsb[rpar], c2sb[rpar])):
                for w2 in range(2):
                    w_ = trg if (idx == 0 and w2 == 0) else []
                    op("tensor", lambda t_, i_=srcb[0:8, 128 * w2:128 * (w2 + 1)],
                       o_=psS[0][:, 8 * (2 * idx + w2):8 * (2 * idx + w2 + 1)]:
                       t_.transpose(o_, i_, eyef[0:8, 0:8]),
                       waits=w_, inc="sT" if (idx == 1 and w2 == 1) else None)
            sT_rt = at("sT")

            def re8f(ap_):
                return ap_.rearrange("p (w e) -> p w e", e=8)

            op("scalar", lambda s_: s_.copy(
                re8f(rtm.ap()), psS[0].ap().rearrange("p (w e) -> p w e", e=8)[:, 0:2, :]),
                waits=[("sT", sT_rt)], inc="sS")
            op("scalar", lambda s_: s_.copy(
                re8f(ctm.ap()), psS[0].ap().rearrange("p (w e) -> p w e", e=8)[:, 2:4, :]),
                inc="sS")
            bg_S[0] = [("sS", at("sS"))]
            sS_rc = at("sS")
            vtiny(lambda v_: v_.tensor_tensor(
                re8f(rtm.ap()), re8f(rtm.ap()),
                grbc[:, 32:40].unsqueeze(1).broadcast_to((128, 2, 8)), op=ALU.add),
                waits=[("sS", sS_rc)])
            vtiny(lambda v_: v_.tensor_scalar(ctm[:, :], ctm[:, :], 1e-30, None,
                                              op0=ALU.add), mark=True)
            sY_c = at("sY")
            sY_we_expected = sY_c + 3 + 1
            router_state[rt] = {"sY_c": sY_c, "sY_we_expected": sY_we_expected}

        def back(tau):
            par = tau % 2
            st = front_state.pop(tau)
            cmp_ready = st["cmp_ready"]
            fv_ready = st["fv_ready"]
            f_slots = st["f_slots"]

            rs = router_state.pop(tau)
            sY_we = rs["sY_we_expected"]

            def re8(ap_):
                return ap_.rearrange("p (w e) -> p w e", e=8)

            op("scalar", lambda s_: s_.activation(ctm[:, :], ctm[:, :], AF.Ln),
               waits=[("sY", rs["sY_c"])])
            op("scalar", lambda s_: s_.activation(ctm[:, :], ctm[:, :], AF.Exp, scale=0.5),
               inc="sS")
            sS_c = at("sS")

            def ln8_queue(srcb, mbuf, vbuf, waits=()):
                vtiny(lambda v_: v_.tensor_reduce(mbuf[:, :], re8(srcb.ap()),
                                                  axis=mybir.AxisListType.X, op=ALU.add),
                      waits=waits)
                vtiny(lambda v_: v_.tensor_scalar(mbuf[:, :], mbuf[:, :], 1.0 / 8, None,
                                                  op0=ALU.mult))
                vtiny(lambda v_: v_.tensor_tensor(t1[:, :], srcb[:, :], srcb[:, :],
                                                  op=ALU.mult))
                vtiny(lambda v_: v_.tensor_reduce(vbuf[:, :], re8(t1.ap()),
                                                  axis=mybir.AxisListType.X, op=ALU.add))
                vtiny(lambda v_: v_.tensor_scalar(vbuf[:, :], vbuf[:, :], 1.0 / 8, None,
                                                  op0=ALU.mult))
                vtiny(lambda v_: v_.tensor_tensor(st2d[:, :], mbuf[:, :], mbuf[:, :],
                                                  op=ALU.mult))
                vtiny(lambda v_: v_.tensor_tensor(vbuf[:, :], vbuf[:, :], st2d[:, :],
                                                  op=ALU.subtract))
                vtiny(lambda v_: v_.tensor_scalar(vbuf[:, :], vbuf[:, :], EPS, None,
                                                  op0=ALU.add), mark=True)

            vtiny(lambda v_: v_.tensor_scalar(ctm[:, :], ctm[:, :],
                                              float(1.0 / np.sqrt(8.0)), None, op0=ALU.mult),
                  waits=[("sS", sS_c)])
            ln8_queue(rtm, st2c, st2b)
            ln8_queue(ctm, st2cc, st2bc)
            sY_v = at("sY")

            zw_ready = [[], []]
            for g in (0, 1):
                pair_rdy = {}

                def emit_pair(kbp):
                    kb = 2 * kbp
                    slot = kb % 4
                    for hh in (0, 1):
                        in0 = kcmp[par][g][hh][:, 64 * kb:64 * (kb + 2)]
                        in0 = in0.rearrange("p (u a) -> p u a", a=16).unsqueeze(2)
                        in0 = in0.broadcast_to((128, 8, 8, 16))
                        in1 = blk8.ap().rearrange("p (h a) -> p h a", a=16).unsqueeze(1)
                        in1 = in1.broadcast_to((128, 8, 8, 16))
                        sbuf_, soff = bdslot(hh, slot)
                        dstb = sbuf_[:, 0:1024]
                        dstb = dstb.rearrange("p (u h a) -> p u h a", h=8, a=16)
                        w_ = (cmp_ready[(1, g, hh)] + bd_guard[hh][slot]
                              + bd_guard[hh][slot + 1])
                        if kbp == 0:
                            op("gpsimd", lambda g_, a_=dstb, b_=in0, c_=in1:
                               g_.tensor_tensor(a_, b_, c_, op=ALU.mult),
                               waits=w_, inc="sG")
                            pair_rdy[(kbp, hh)] = ("sG", at("sG"))
                        else:
                            vop(lambda v_, a_=dstb, b_=in0, c_=in1: v_.tensor_tensor(
                                a_, b_, c_, op=ALU.mult), waits=w_, inc="sV")
                            pair_rdy[(kbp, hh)] = ("sV", at("sV"))

                def emit_S(kb):
                    slot = kb % 4
                    for hh in (0, 1):
                        sbuf_, soff = bdslot(hh, slot)
                        for uu in range(4):
                            u = 4 * kb + uu
                            lhs = sbuf_[:, soff + 128 * uu:soff + 128 * (uu + 1)]
                            mov = qcmp[par][g][hh][:, 16 * u:16 * (u + 1)]
                            o = psS[hh][:, 16 * u:16 * (u + 1)]
                            w_ = []
                            if uu == 0:
                                w_ = cmp_ready[(0, g, hh)]
                                if kb % 2 == 0:
                                    w_ = w_ + [pair_rdy[(kb // 2, hh)]]
                                if kb == 0:
                                    w_ += bg_S[hh]
                            op("tensor", lambda t_, l=lhs, m=mov, o_=o: t_.matmul(
                                o_, l, m, start=True, stop=True), waits=w_,
                                inc="sT" if uu == 3 else None)
                        bd_guard[hh][slot] = [("sT", at("sT"))]

                emit_pair(0)
                emit_pair(1)
                emit_S(0)
                emit_S(1)
                emit_pair(2)
                emit_S(2)
                emit_S(3)
                emit_pair(3)
                for kb in range(4, 8):
                    emit_S(kb)
                sT_S = at("sT")
                for hh in (0, 1):
                    op("scalar", lambda s_, hh_=hh: s_.activation(
                        pcmp[hh_][:, :], psS[hh_][:, :], AF.Exp, scale=0.25),
                        waits=[("sT", sT_S)], inc="sS")
                sS_P = at("sS")
                for hh in (0, 1):
                    op("tensor", lambda t_, hh_=hh: t_.matmul(
                        psS[hh_][0:8, :], ones8[:, :], pcmp[hh_][:, :],
                        start=True, stop=True),
                        waits=[("sS", sS_P)] if hh == 0 else [], inc="sT")
                vtm_rdy = [0, 0]
                sT_v_m = [0, 0]
                for hh in (0, 1):
                    pzb = psZ[hh].ap().bitcast(BF16)
                    for ub in range(4):
                        w_ = (cmp_ready[(2, g, hh)] + list(bg_Z[hh])) if ub == 0 else []
                        op("tensor", lambda t_, i_=avcmp[par][g][hh][:, 128 * ub:128 * (ub + 1)],
                           o_=pzb[:, 128 * ub:128 * (ub + 1)]: t_.transpose(o_, i_, eyeb[:, :]),
                           waits=w_, inc="sT" if ub == 3 else None)
                    sT_v_m[hh] = at("sT")
                sT_Z = at("sT")
                for hh in (0, 1):
                    op("scalar", lambda s_, hh_=hh: s_.activation(
                        thsb[32 * hh_:32 * hh_ + 8, 0:512], psS[hh_][0:8, :], AF.Ln),
                        waits=[("sT", sT_Z), ("sT", sT_zrep_last[hh])])
                    op("scalar", lambda s_, hh_=hh: s_.activation(
                        zrs[hh_][:, :], thsb[32 * hh_:32 * hh_ + 8, 0:512], AF.Exp,
                        scale=-1.0), inc="sS")
                    sS_zr = at("sS")
                    op("tensor", lambda t_, hh_=hh: t_.matmul(
                        psS[hh_][:, :], onesc[:, :], zrs[hh_][:, :], start=True, stop=True),
                        waits=[("sS", sS_zr)], inc="sT")
                    sT_zrep_last[hh] = at("sT")
                    vop(lambda v_, hh_=hh, wc=wcmp[par][g][hh][:, :]: v_.scalar_tensor_tensor(
                        wzr[hh_][:, :], psS[hh_][:, :], 1.0, wc,
                        op0=ALU.mult, op1=ALU.mult),
                        waits=[("sT", sT_zrep_last[hh])] + cmp_ready[(3, g, hh)], inc="sV")
                    bg_S[hh] = [("sV", at("sV"))]
                sV_wzr = at("sV")
                for hh in (0, 1):
                    pzb = psZ[hh].ap().bitcast(BF16)
                    op("scalar", lambda s_, hh_=hh, s2=pzb[:, 0:512]: s_.copy(
                        vtm[hh_][:, :], s2), waits=[("sT", sT_v_m[hh])], inc="sS")
                    vtm_rdy[hh] = ("sS", at("sS"))
                    bg_Z[hh] = [("sS", at("sS"))]

                zg = [list(bg_Z[0]), list(bg_Z[1])]
                bank_guards = [list(pfin_guard), [("sV", sV_wzr)], [("sV", sV_wzr)]]
                bdv_mark = {}

                def emit_rep_bdv(hh, qb):
                    bidx = qb % 3
                    slot = qb % 4
                    zbuf_, boff = bdslot(hh, slot)
                    bank = [pfin, psS[0], psS[1]][bidx]
                    for uu in range(4):
                        u = 4 * qb + uu
                        ub, r8 = u // 8, u % 8
                        w_ = []
                        if uu == 0:
                            w_ = list(bank_guards[bidx])
                            if qb <= 1:
                                w_ += [vtm_rdy[hh]]
                        op("tensor", lambda t_, l=reps[:, 128 * r8:128 * (r8 + 1)],
                           m=vtm[hh][:, 128 * ub:128 * (ub + 1)],
                           o=bank[:, 128 * uu:128 * (uu + 1)]: t_.matmul(
                            o, l, m, start=True, stop=True),
                            waits=w_, inc="sT" if uu == 3 else None)
                    sT_rep = at("sT")
                    in1 = blk8.ap().unsqueeze(1).broadcast_to((128, 4, 128))
                    dstb = zbuf_[:, boff:boff + 512].rearrange("p (q c) -> p q c", c=128)
                    vop(lambda v_, d=dstb, m=in1, b=bank: v_.scalar_tensor_tensor(
                        d, b.ap().rearrange("p (q c) -> p q c", c=128), 1.0, m,
                        op0=ALU.mult, op1=ALU.mult),
                        waits=[("sT", sT_rep)] + bd_guard[hh][slot],
                        inc="sV")
                    sV_bdv = at("sV")
                    bdv_mark[(hh, qb)] = sV_bdv
                    bank_guards[bidx] = [("sV", sV_bdv)]
                    if bidx == 0:
                        pfin_guard[:] = [("sV", sV_bdv)]

                def emit_z(hh, qb):
                    slot = qb % 4
                    zbuf_, boff = bdslot(hh, slot)
                    for uu in range(4):
                        u = 4 * qb + uu
                        w_ = []
                        if uu == 0:
                            w_ = [("sV", bdv_mark[(hh, qb)])]
                            if qb == 0:
                                w_ += zg[hh]
                        op("tensor", lambda t_,
                           l=zbuf_[:, boff + 128 * uu:boff + 128 * (uu + 1)],
                           m=pcmp[hh][:, 16 * u:16 * (u + 1)],
                           o=psZ[hh][:, 128 * uu + 16 * qb:128 * uu + 16 * qb + 16]:
                           t_.matmul(o, l, m, start=True, stop=True),
                            waits=w_, inc="sT" if uu == 3 else None)
                    bd_guard[hh][slot] = [("sT", at("sT"))]

                for hh in (0, 1):
                    emit_rep_bdv(hh, 0)
                    emit_rep_bdv(hh, 1)
                    for qb in range(2, 8):
                        emit_rep_bdv(hh, qb)
                        emit_z(hh, qb - 2)
                    emit_z(hh, 6)
                    emit_z(hh, 7)
                    sT_z = at("sT")
                    for r4 in range(4):
                        srcz = psZ[hh][:, 128 * r4:128 * (r4 + 1)].rearrange(
                            "p (u4 a) -> p u4 a", a=16)
                        w1_ = wzr[hh].ap().rearrange(
                            "p (u4 r a) -> p u4 r a", r=4, a=16)[:, :, r4, :]
                        dstz = zw[g][hh][:, 128 * r4:128 * (r4 + 1)]
                        dstz = dstz.rearrange("p (u4 a) -> p u4 a", a=16)
                        w_ = [("sT", sT_z), ("sV", sV_wzr)] if r4 == 0 else []
                        vop(lambda v_, s2=srcz, d=dstz, m=w1_: v_.scalar_tensor_tensor(
                            d, s2, 1.0, m, op0=ALU.mult, op1=ALU.mult),
                            waits=w_, inc="sV" if r4 == 3 else None)
                    bg_Z[hh] = [("sV", at("sV"))]
                    if hh == 1:
                        bg_S[0] = list(bank_guards[1])
                        bg_S[1] = list(bank_guards[2])
                    zw_ready[g] = zw_ready[g] + [("sV", at("sV"))]

                if g == 0:
                    op("scalar", lambda s_: s_.activation(st2b[:, :], st2b[:, :], AF.Ln),
                       waits=[("sY", sY_v)])
                    op("scalar", lambda s_: s_.activation(st2bc[:, :], st2bc[:, :], AF.Ln))
                    op("scalar", lambda s_: s_.drain())
                    op("scalar", lambda s_: s_.activation(st2b[:, :], st2b[:, :], AF.Exp,
                                                          scale=-0.5))
                    op("scalar", lambda s_: s_.activation(st2bc[:, :], st2bc[:, :], AF.Exp,
                                                          scale=-0.5), inc="sS")
                    sS_rstd = at("sS")

                    def ln8_apply(dst_, srcb, mbuf, vbuf, gcol, bcol, waits=()):
                        vtiny(lambda v_: v_.tensor_tensor(
                            re8(dst_.ap()), re8(srcb.ap()),
                            mbuf[:, :].unsqueeze(2).broadcast_to((128, 2, 8)),
                            op=ALU.subtract), waits=waits)
                        vtiny(lambda v_: v_.tensor_tensor(
                            re8(dst_.ap()), re8(dst_.ap()),
                            vbuf[:, :].unsqueeze(2).broadcast_to((128, 2, 8)), op=ALU.mult))
                        vtiny(lambda v_: v_.tensor_tensor(
                            re8(dst_.ap()), re8(dst_.ap()),
                            grbc[:, gcol:gcol + 8].unsqueeze(1).broadcast_to((128, 2, 8)),
                            op=ALU.mult))
                        vtiny(lambda v_: v_.tensor_tensor(
                            re8(dst_.ap()), re8(dst_.ap()),
                            grbc[:, bcol:bcol + 8].unsqueeze(1).broadcast_to((128, 2, 8)),
                            op=ALU.add))

                    ln8_apply(sgt, rtm, st2c, st2b, 0, 8, waits=[("sS", sS_rstd)])
                    ln8_apply(t2, ctm, st2cc, st2bc, 16, 24)
                    vtiny(lambda v_: v_.scalar_tensor_tensor(
                        sgt[:, :], t2[:, :], sigpp[:, 0:1], sgt[:, :],
                        op0=ALU.mult, op1=ALU.add))
                    vtiny(lambda v_: v_.tensor_reduce(st2c[:, :], re8(sgt.ap()),
                                                      axis=mybir.AxisListType.X, op=ALU.max))
                    vtiny(lambda v_: v_.tensor_tensor(
                        re8(t1.ap()), re8(sgt.ap()),
                        st2c[:, :].unsqueeze(2).broadcast_to((128, 2, 8)), op=ALU.is_ge))
                    vtiny(lambda v_: v_.scalar_tensor_tensor(
                        t2[:, :], t1[:, :], -1e30, sgt[:, :], op0=ALU.mult, op1=ALU.add))
                    vtiny(lambda v_: v_.tensor_reduce(st2b[:, :], re8(t2.ap()),
                                                      axis=mybir.AxisListType.X, op=ALU.max))
                    vtiny(lambda v_: v_.tensor_tensor(
                        re8(t3.ap()), re8(t2.ap()),
                        st2b[:, :].unsqueeze(2).broadcast_to((128, 2, 8)), op=ALU.is_ge))
                    vtiny(lambda v_: v_.tensor_tensor(st2d[:, :], st2c[:, :], st2b[:, :],
                                                      op=ALU.subtract), mark=True)
                    sY_d = at("sY")
                else:
                    op("scalar", lambda s_: s_.activation(st2d[:, :], st2d[:, :], AF.Tanh,
                                                          scale=0.5),
                       waits=[("sY", sY_d)], inc="sS")
                    sS_w1 = at("sS")
                    vtiny(lambda v_: v_.tensor_scalar(st2d[:, :], st2d[:, :], 0.5, 0.5,
                                                      op0=ALU.mult, op1=ALU.add),
                          waits=[("sS", sS_w1)])
                    vtiny(lambda v_: v_.tensor_tensor(
                        re8(wetm.ap()), re8(t1.ap()),
                        st2d[:, :].unsqueeze(2).broadcast_to((128, 2, 8)), op=ALU.mult))
                    vtiny(lambda v_: v_.tensor_scalar(st2d[:, :], st2d[:, :], -1.0, 1.0,
                                                      op0=ALU.mult, op1=ALU.add))
                    vtiny(lambda v_: v_.tensor_tensor(
                        re8(t3.ap()), re8(t3.ap()),
                        st2d[:, :].unsqueeze(2).broadcast_to((128, 2, 8)), op=ALU.mult))
                    vtiny(lambda v_: v_.tensor_tensor(wetm[:, :], wetm[:, :], t3[:, :],
                                                      op=ALU.add), mark=True)
                    sY_we = at("sY")

                if debug and tau == 0 and g == 0:
                    def dump(name, src_ap, waits):
                        op("sync", lambda s_, d=dbg[name], s2=src_ap: s_.dma_start(d, s2),
                           waits=waits, inc="sDo", inc_amt=16)
                    dump("dbg_q", qcmp[0][0][0][:, :], cmp_ready[(0, 0, 0)])
                    dump("dbg_k", kcmp[0][0][0][:, :], cmp_ready[(1, 0, 0)])
                    dump("dbg_av", avcmp[0][0][0][:, :], cmp_ready[(2, 0, 0)])
                    dump("dbg_w", wcmp[0][0][0][:, :], cmp_ready[(3, 0, 0)])
                    dump("dbg_p", pcmp[0][:, :], [("sS", sS_P)])
                    dump("dbg_zw", zw[0][0][:, :], [("sV", at("sV"))])
                    dump("dbg_wzr", wzr[0][:, :], [("sV", sV_wzr)])
                    dump("dbg_vtm", vtm[0][:, :], [("sV", at("sV"))])

            flush_tiny()
            if debug and tau == 0:
                op("sync", lambda s_: s_.dma_start(dbg["dbg_we"], wetm[:, :]),
                   waits=[("sY", sY_we)], inc="sDo", inc_amt=16)
                op("sync", lambda s_: s_.dma_start(dbg["dbg_rtm"], rtm[:, :]),
                   waits=[("sY", sY_we)], inc="sDo", inc_amt=16)
                op("sync", lambda s_: s_.dma_start(dbg["dbg_ctm"], ctm[:, :]),
                   waits=[("sY", sY_we)], inc="sDo", inc_amt=16)
                op("sync", lambda s_: s_.dma_start(dbg["dbg_sgt"], sgt[:, :]),
                   waits=[("sY", sY_we)], inc="sDo", inc_amt=16)
            pr = psS[0]
            for w2 in range(2):
                w_ = ([("sY", sY_we)] + list(bg_S[0])) if w2 == 0 else []
                op("tensor", lambda t_, i_=wetm[:, 8 * w2:8 * (w2 + 1)],
                   o_=pr[0:8, 128 * w2:128 * (w2 + 1)]: t_.transpose(o_, i_, eyef[:, :]),
                   waits=w_, inc="sT" if w2 == 1 else None)
            sT_wef = at("sT")
            op("vector", lambda v_: v_.tensor_copy(wef[:, 0:128], pr[0:8, 0:128]),
               waits=[("sT", sT_wef)], inc="sV")
            op("vector", lambda v_: v_.tensor_copy(wef[:, 128:256], pr[0:8, 128:256]),
               inc="sV")
            sV_wef = at("sV")
            bg_S[0] = [("sV", sV_wef)]
            for ch in (0, 1):
                pb2 = psS[ch]
                w_ = [("sV", sV_wef)] + (list(bg_S[1]) if ch == 1 else list(bg_S[0]))
                op("tensor", lambda t_, l=sel32[:, 128 * ch:128 * (ch + 1)], b=pb2:
                   t_.matmul(b[:, 0:RF], l, wef[:, :], start=True, stop=True),
                   waits=w_, inc="sT")
                op("scalar", lambda s_, b=pb2, ch_=ch: s_.copy(
                    werep[:, RF * ch_:RF * (ch_ + 1)], b[:, 0:RF]),
                    waits=[("sT", at("sT"))], inc="sS")
                bg_S[ch] = [("sS", at("sS"))]
            sS_wrep = at("sS")
            for ch in (0, 1):
                pb2 = psS[ch]
                for k in range(KC):
                    lhs = w1t[:, 256 * k + 128 * ch:256 * k + 128 * (ch + 1)]
                    mov = fvc[par][:, RF * k:RF * (k + 1)]
                    w_ = (fv_ready + list(bg_S[ch])) if k == 0 else []
                    op("tensor", lambda t_, l=lhs, m=mov, b=pb2, k_=k: t_.matmul(
                        b[:, 0:RF], l, m, start=(k_ == 0), stop=(k_ == KC - 1)),
                        waits=w_, inc="sT" if k == KC - 1 else None)
                sT_h = at("sT")
                op("scalar", lambda s_, b=pb2, ch_=ch: s_.activation(
                    thsb[:, RF * ch_:RF * (ch_ + 1)], b[:, 0:RF], AF.Tanh, scale=0.5,
                    bias=hbias[:, ch_:ch_ + 1]), waits=[("sT", sT_h)], inc="sS")
                sS_th = at("sS")
                vtiny(lambda v_, ch_=ch: v_.tensor_scalar(
                    thsb[:, RF * ch_:RF * (ch_ + 1)], thsb[:, RF * ch_:RF * (ch_ + 1)],
                    0.5, 0.5, op0=ALU.mult, op1=ALU.add), waits=[("sS", sS_th)])
                vtiny(lambda v_, b=pb2, ch_=ch: v_.scalar_tensor_tensor(
                    thsb[:, RF * ch_:RF * (ch_ + 1)], b[:, 0:RF], smalls[:, 8 + ch_:9 + ch_],
                    thsb[:, RF * ch_:RF * (ch_ + 1)], op0=ALU.add, op1=ALU.mult))
                vtiny(lambda v_, ch_=ch: v_.tensor_tensor(
                    hssb[:, RF * ch_:RF * (ch_ + 1)], thsb[:, RF * ch_:RF * (ch_ + 1)],
                    werep[:, RF * ch_:RF * (ch_ + 1)], op=ALU.mult),
                    waits=[("sS", sS_wrep)], mark=True)
                flush_tiny()
                bg_S[ch] = [("sY", at("sY"))]
            sY_hs = at("sY")
            if debug and tau == 0:
                op("sync", lambda s_: s_.dma_start(dbg["dbg_hs"], hssb[:, :]),
                   waits=[("sY", sY_hs)], inc="sDo", inc_amt=16)
                op("sync", lambda s_: s_.dma_start(dbg["dbg_fv"], fvc[0][:, :]),
                   waits=fv_ready, inc="sDo", inc_amt=16)
                op("sync", lambda s_: s_.dma_start(dbg["dbg_xnfm"], xnfm[:, :]),
                   waits=st["fm_ready"], inc="sDo", inc_amt=16)
                op("sync", lambda s_: s_.dma_start(dbg["dbg_xnffn"], xnffm[:, :]),
                   waits=st["fm_ready"], inc="sDo", inc_amt=16)

            for g in (0, 1):
                for w2 in (0, 1):
                    first = True
                    w0_ = list(pfin_guard) + [("sY", sY_hs), ("sDw", w_init)] + zw_ready[g]
                    for r4 in range(4):
                        for hh in (0, 1):
                            lhs = zw[g][hh][:, 128 * r4:128 * (r4 + 1)]
                            mov = wout[:, F * (2 * r4 + hh) + 512 * w2:
                                       F * (2 * r4 + hh) + 512 * (w2 + 1)]
                            w_ = w0_ if first else []
                            op("tensor", lambda t_, l=lhs, m=mov, st_=first: t_.matmul(
                                pfin[:, :], l, m, start=st_, stop=False), waits=w_)
                            first = False
                    for ch in (0, 1):
                        lhs = hssb[:, RF * ch + 128 * g:RF * ch + 128 * (g + 1)]
                        mov = w2t[:, F * ch + 512 * w2:F * ch + 512 * (w2 + 1)]
                        op("tensor", lambda t_, l=lhs, m=mov: t_.matmul(
                            pfin[:, :], l, m, start=False, stop=False))
                    op("tensor", lambda t_, g_=g, w2_=w2: t_.matmul(
                        pfin[:, :], wef[:, 128 * g_:128 * (g_ + 1)],
                        b2sb[:, 512 * w2_:512 * (w2_ + 1)], start=False, stop=False))
                    for q in range(4):
                        dc = 4 * w2 + q
                        lhs = fvc[par][:, RF * dc + 128 * g:RF * dc + 128 * (g + 1)]
                        stop = (q == 3)
                        op("tensor", lambda t_, l=lhs, o=pfin[:, 128 * q:128 * (q + 1)],
                           sp=stop: t_.matmul(o, l, eyeb[:, :], start=False, stop=sp),
                           inc="sT" if stop else None)
                    sT_fin = at("sT")
                    slot = f_slots[g]
                    xsrc = xft[:, F * slot + 512 * w2:F * slot + 512 * (w2 + 1)]
                    vop(lambda v_, xs=xsrc, o=outsb[g][:, 512 * w2:512 * (w2 + 1)]:
                        v_.scalar_tensor_tensor(o, pfin[:, :], 1.0, xs,
                                                op0=ALU.mult, op1=ALU.add),
                        waits=[("sT", sT_fin)] + list(outsb_guard[0]), inc="sV")
                    outsb_guard[0] = []
                    pfin_guard[:] = [("sV", at("sV"))]
                out_mark = [("sV", at("sV"))]
                dstr = outv[g, 8 * tau:8 * tau + 8, :, :]
                op("sync", lambda s_, d=dstr, g_=g: s_.dma_start(d, outsb[g_][:, :]),
                   waits=out_mark, inc="sOB0", inc_amt=16)
                outsb_guard[0] = [("sOB0", at("sOB0"))]
                xf_free[f_slots[g]] = xf_free[f_slots[g]] + out_mark
            par_free[par] = [("sT", at("sT")), ("sV", at("sV")), ("sS", at("sS"))]

        for j in range(4):
            dma_xa(0, j)
        attn_stats(0)
        front(0)
        front(1)
        for tau in range(NT):
            if tau not in router_state:
                router_front(tau)
            back(tau)
            if tau + 2 < NT:
                front(tau + 2)

        total_out = at("sDo") if debug else 0

        global _captured_ops
        if CAPTURE_OPS:
            _captured_ops = list(ops)

        block = e(nc.Block())

        def emit_for(eng_name):
            def emitter(eng):
                for eng2, fn, waits, incs in ops:
                    if eng2 != eng_name:
                        continue
                    for sem_n, val in waits:
                        eng.wait_ge(sems[sem_n], val)
                    inst = fn(eng)
                    for sem_n, amt in incs:
                        inst.then_inc(sems[sem_n], amt)
                if eng_name == "sync":
                    if debug and total_out > 0:
                        eng.wait_ge(sems["sDo"], total_out)
                    eng.wait_ge(sems["sOB0"], counts.get("sOB0", 0))
            return emitter

        block.sync(emit_for("sync"))
        block.gpsimd(emit_for("gpsimd"))
        block.tensor(emit_for("tensor"))
        block.vector(emit_for("vector"))
        block.scalar(emit_for("scalar"))

    return nc



def _mk_consts():
    eyeb = np.eye(128, dtype=ml_dtypes.bfloat16)
    eyef = np.eye(128, dtype=np.float32)
    blk8 = np.zeros((128, 128), dtype=ml_dtypes.bfloat16)
    for p in range(128):
        blk8[p, (p // 16) * 16:(p // 16) * 16 + 16] = 1
    ones8 = np.zeros((128, 8), dtype=ml_dtypes.bfloat16)
    for p in range(128):
        ones8[p, p // 16] = 1
    onesc = np.zeros((8, 128), dtype=ml_dtypes.bfloat16)
    for hh in range(8):
        onesc[hh, 16 * hh:16 * (hh + 1)] = 1
    sel32 = np.zeros((8, 256), dtype=ml_dtypes.bfloat16)
    for ch in range(2):
        for p in range(128):
            sel32[4 * ch + p // 32, 128 * ch + p] = 1
    selp = np.zeros((64, 8), dtype=ml_dtypes.bfloat16)
    for ep in range(64):
        selp[ep, ep // 8] = 1
    reps = np.zeros((8, 128, 128), dtype=ml_dtypes.bfloat16)
    for r8 in range(8):
        for a2 in range(16):
            for hh in range(8):
                reps[r8, 16 * r8 + a2, 16 * hh + a2] = 1
    reps = reps.transpose(1, 0, 2).reshape(128, 1024)
    return eyeb, eyef, blk8, ones8, onesc, sel32, selp, reps


def _host_prep(inp):
    x = np.ascontiguousarray(np.asarray(inp["L0_attn_x"], dtype=np.float32))
    Wqkv = np.asarray(inp["L0_attn_w_qkvw"], dtype=np.float32)
    Wout = np.ascontiguousarray(np.asarray(inp["L0_attn_w_out"], dtype=np.float32))
    Wv = np.asarray(inp["L0_ffn_Wv"], dtype=np.float32)
    bv = np.asarray(inp["L0_ffn_bv"], dtype=np.float32)
    Wg = np.asarray(inp["L0_ffn_Wg"], dtype=np.float32)
    bg = np.asarray(inp["L0_ffn_bg"], dtype=np.float32)
    Wp = np.asarray(inp["L0_ffn_Wp"], dtype=np.float32)
    alpha = np.ascontiguousarray(np.broadcast_to(
        np.asarray(inp["L0_ffn_alpha"], dtype=np.float32).reshape(1, 1), (128, 1)))
    W1 = np.asarray(inp["L0_ffn_W1"], dtype=np.float32)
    b1 = np.asarray(inp["L0_ffn_b1"], dtype=np.float32)
    W2 = np.asarray(inp["L0_ffn_W2"], dtype=np.float32)
    b2 = np.ascontiguousarray(np.asarray(inp["L0_ffn_b2"], dtype=np.float32))

    Wq6 = Wqkv.reshape(1024, 4, 4, 2, 8, 16)
    wvt = np.ascontiguousarray(Wv.T)
    wgt = np.ascontiguousarray(Wg.T)
    wpt = np.ascontiguousarray(Wp.reshape(64, 1024).T)
    w1t = np.ascontiguousarray(W1.reshape(256, 1024).T)
    w2t = np.ascontiguousarray(W2.transpose(0, 2, 1).reshape(256, 1024))
    smalls = np.zeros((128, 16), dtype=np.float32)
    smalls[:, 0:8] = bv.reshape(8, 128).T
    smalls[:, 8:10] = b1.reshape(2, 128).T
    grbc = np.zeros((128, 40), dtype=np.float32)
    grbc[:, 0:8] = np.asarray(inp["L0_ffn_g_r"], dtype=np.float32)[None, :]
    grbc[:, 8:16] = np.asarray(inp["L0_ffn_b_r"], dtype=np.float32)[None, :]
    grbc[:, 16:24] = np.asarray(inp["L0_ffn_g_c"], dtype=np.float32)[None, :]
    grbc[:, 24:32] = np.asarray(inp["L0_ffn_b_c"], dtype=np.float32)[None, :]
    grbc[:, 32:40] = bg[None, :]
    eyeb, eyef, blk8, ones8, onesc, sel32, selp, reps = _mk_consts()

    in_maps = []
    for c in range(NCORES):
        b = c // 2
        i0, i1 = 2 * (c % 2), 2 * (c % 2) + 1
        x_attn = np.ascontiguousarray(x[b])
        x_ffn = np.ascontiguousarray(np.stack([x[i0, b * 2048:(b + 1) * 2048],
                                               x[i1, b * 2048:(b + 1) * 2048]]))
        wq = np.ascontiguousarray(
            np.stack([Wq6[:, i0], Wq6[:, i1]], axis=1).reshape(1024, 2048))
        in_maps.append({
            "x_attn": x_attn, "x_ffn": x_ffn, "wqkv": wq, "wout": Wout,
            "wvt": wvt, "wgt": wgt, "wpt": wpt, "w1t": w1t, "w2t": w2t,
            "b2": b2, "smalls": smalls, "grbc": grbc, "alpha": alpha,
            "eyeb": eyeb, "eyef": eyef, "blk8": blk8, "ones8": ones8,
            "onesc": onesc, "sel32": sel32, "selp": selp, "reps": reps,
        })
    return in_maps


def kernel(**inputs):
    in_maps = _host_prep(inputs)
    if "prog" not in _prog_cache:
        _prog_cache["prog"] = build_nc(debug=False)
    nc = _prog_cache["prog"]
    res = run_bass_kernel_spmd(nc, in_maps, core_ids=list(range(NCORES)))
    out = np.empty((4, 8192, 1024), dtype=np.float32)
    for c in range(NCORES):
        b = c // 2
        i0, i1 = 2 * (c % 2), 2 * (c % 2) + 1
        oc = res.results[c]["out"]
        out[i0, b * 2048:(b + 1) * 2048] = oc[0]
        out[i1, b * 2048:(b + 1) * 2048] = oc[1]
    return out
